# version 02
# baseline (speedup 1.0000x reference)
import math
import os
import numpy as np
import ml_dtypes
import concourse.bass as bass
import concourse.mybir as mybir
from concourse.bass_utils import run_bass_kernel_spmd

F32 = mybir.dt.float32
BF16 = mybir.dt.bfloat16
AF = mybir.ActivationFunctionType
ALU = mybir.AluOpType
AX = mybir.AxisListType

B, S, D = 4, 8192, 1024
HALF = S // 2
NCORE = 8
EXP, IMOE, GUF = 8, 2048, 4096
E1, F1 = 4096, 8192
SCALE = 1.0 / math.sqrt(512.0)

ENGINES = ("sync", "scalar", "vector", "tensor", "gpsimd")


class Buf:
    __slots__ = ("name", "ap", "last_write", "readers")

    def __init__(self, name, ap):
        self.name = name
        self.ap = ap
        self.last_write = None
        self.readers = []


class K:

    NLANE = 8

    def __init__(self, nc, stack):
        self.nc = nc
        self.stack = stack
        self.plan = {e: [] for e in ENGINES}
        self.sem_handles = {
            e: stack.enter_context(nc.semaphore(f"sem_{e}")) for e in ENGINES}
        for q in ("sync", "scalar", "gpsimd", "vector"):
            for l in range(self.NLANE):
                nm = f"{q}_d{l}"
                self.sem_handles[nm] = stack.enter_context(nc.semaphore(nm))
        self.cnt = {s: 0 for s in self.sem_handles}
        self.dma_rr = {q: 0 for q in ("sync", "scalar", "gpsimd", "vector")}
        self.waited = {e: {} for e in ENGINES}

    def sbuf(self, name, shape, dtype):
        h = self.stack.enter_context(self.nc.sbuf_tensor(name, shape, dtype))
        return Buf(name, h[:, :])

    def psum(self, name, shape, dtype):
        h = self.stack.enter_context(self.nc.psum_tensor(name, shape, dtype))
        return Buf(name, h[:, :])

    def dram(self, name, shape, dtype, kind):
        return Buf(name, self.nc.dram_tensor(name, shape, dtype,
                                             kind=kind).ap())

    def do(self, eng, fn, reads=(), writes=(), dma=False):
        inc = 16 if dma else 1
        if dma:
            lane = self.dma_rr[eng]
            self.dma_rr[eng] = (lane + 1) % self.NLANE
            sig = f"{eng}_d{lane}"
        else:
            sig = eng
        deps = []
        for r in reads:
            if r.last_write is not None:
                deps.append(r.last_write)
        for w in writes:
            if w.last_write is not None:
                deps.append(w.last_write)
            deps.extend(w.readers)
        waited = self.waited[eng]
        need = {}
        for dsem, dval in deps:
            if dval > need.get(dsem, 0):
                need[dsem] = dval
        todo = []
        for dsem, dval in sorted(need.items()):
            if dsem == eng:
                pass
            if waited.get(dsem, 0) < dval:
                todo.append((dsem, dval))
                waited[dsem] = dval
        self.cnt[sig] += inc
        val = self.cnt[sig]
        handles = self.sem_handles

        def emit(e, todo=tuple(todo), fn=fn, inc=inc, sig=sig):
            for dsem, dval in todo:
                e.wait_ge(handles[dsem], dval)
            fn(e).then_inc(handles[sig], inc)
        self.plan[eng].append(emit)
        dep = (sig, val)
        for w in writes:
            w.last_write = dep
            w.readers = []
        for r in reads:
            r.readers.append(dep)
        return dep

    def flush(self, eng, buf, scratch_ap):
        dep = self.do(eng, lambda e: e.tensor_copy(scratch_ap, scratch_ap),
                      reads=(buf,), writes=())
        buf.last_write = dep
        return dep

    def barrier(self):
        handles = self.sem_handles
        snap = dict(self.cnt)
        for eng in ENGINES:
            todo = []
            for s, v in snap.items():
                if s == eng or v == 0:
                    continue
                if self.waited[eng].get(s, 0) < v:
                    todo.append((s, v))
                    self.waited[eng][s] = v
            if todo:
                def emit(e, todo=tuple(todo)):
                    for ds, dv in todo:
                        e.wait_ge(handles[ds], dv)
                self.plan[eng].append(emit)

    def finish(self, deps):
        handles = self.sem_handles

        def emit(e, deps=tuple(deps)):
            for dsem, dval in deps:
                e.wait_ge(handles[dsem], dval)
        self.plan["gpsimd"].append(emit)

    def emit_block(self):
        with self.nc.Block() as block:
            @block.sync
            def _(eng):
                for f in self.plan["sync"]:
                    f(eng)

            @block.scalar
            def _(eng):
                for f in self.plan["scalar"]:
                    f(eng)

            @block.vector
            def _(eng):
                for f in self.plan["vector"]:
                    f(eng)

            @block.tensor
            def _(eng):
                for f in self.plan["tensor"]:
                    f(eng)

            @block.gpsimd
            def _(eng):
                for f in self.plan["gpsimd"]:
                    f(eng)


class Arena:
    def __init__(self, k, words):
        self.k = k
        self.t = k.stack.enter_context(
            k.nc.sbuf_tensor("arena", [128, words], F32))
        self.words = words
        self.off = 0
        self.watermark = 0
        self.marks = []

    def alloc(self, name, words, dtype=F32, parts=128):
        w = int(words)
        assert self.off + w <= self.words, (
            f"arena overflow: {name} needs {w} at {self.off}/{self.words}")
        ap = self.t[0:parts, self.off:self.off + w]
        if dtype == BF16:
            ap = ap.bitcast(BF16)
        self.off += w
        self.watermark = max(self.watermark, self.off)
        return Buf(name, ap)

    def mark(self):
        self.marks.append(self.off)

    def reset(self):
        self.k.barrier()
        self.off = self.marks.pop()



def _rope_table():
    pos = np.arange(S, dtype=np.float32)
    inv = np.float32(10000.0) ** (
        np.arange(0, 512, 2, dtype=np.float32) / np.float32(512.0))
    radians = (pos[:, None] / inv[None, :]).astype(np.float32)
    sincos = np.stack(
        [np.sin(radians), np.cos(radians)], axis=-1).reshape(S, 512)
    return np.repeat(sincos, 2, axis=-1).astype(np.float32)


def _colchunks(v, nchunk):
    return np.ascontiguousarray(np.asarray(v).reshape(nchunk, 128).T)


def prep_inputs(inp):
    g = {k: np.asarray(v) for k, v in inp.items()}
    x = g["L0_attn_hidden_states"]
    tT = np.ascontiguousarray(_rope_table().T)
    eye_bf = np.eye(128, dtype=np.float32).astype(ml_dtypes.bfloat16)
    sel8 = np.zeros((8, 1024), dtype=np.float32)
    for e in range(8):
        sel8[e, 128 * e:128 * e + 128] = 1.0
    shared = {
        "wqT": np.ascontiguousarray(g["L0_attn_Wq"].T),
        "wkT": np.ascontiguousarray(g["L0_attn_Wk"].T),
        "wvT": np.ascontiguousarray(g["L0_attn_Wv"].T),
        "bq_row": np.ascontiguousarray(g["L0_attn_bq"].reshape(1, D)),
        "bk_row": np.ascontiguousarray(g["L0_attn_bk"].reshape(1, D)),
        "bv_row": np.ascontiguousarray(g["L0_attn_bv"].reshape(1, D)),
        "routerT": np.ascontiguousarray(g["L0_ffn_router_weight"].T),
        "rbias_row": np.ascontiguousarray(
            g["L0_ffn_router_bias"].reshape(1, EXP)),
        "wgu_blk": np.ascontiguousarray(
            g["L0_ffn_gate_up_proj"].transpose(0, 2, 1)
            .reshape(EXP, 8, 128, 32, 128).transpose(0, 3, 2, 1, 4)
            .reshape(EXP, 32, 128, 1024)),
        "gub_col": np.ascontiguousarray(
            g["L0_ffn_gate_up_proj_bias"].reshape(EXP, 32, 128)
            .transpose(0, 2, 1)),
        "dwn_blk": np.ascontiguousarray(
            g["L0_ffn_down_proj"].transpose(0, 2, 1)
            .reshape(EXP, 16, 128, 1024).transpose(0, 2, 1, 3)
            .reshape(EXP, 128, 16384)),
        "db8": np.ascontiguousarray(g["L0_ffn_down_proj_bias"]),
        "w1_blk": np.ascontiguousarray(
            g["L1_attn_W1"].T.reshape(8, 128, 32, 128).transpose(2, 1, 0, 3)
            .reshape(32, 128, 1024)),
        "w2_blk": np.ascontiguousarray(
            g["L1_attn_W2"].T.reshape(8, 128, 32, 128).transpose(2, 1, 0, 3)
            .reshape(32, 128, 1024)),
        "b1c": _colchunks(g["L1_attn_b1"], 32),
        "b2c": _colchunks(g["L1_attn_b2"], 32),
        "wg_blk": np.ascontiguousarray(
            g["L1_attn_Wg"].T.reshape(32, 128, 64, 128).transpose(2, 1, 0, 3)
            .reshape(64, 128, 4096)),
        "bgc": _colchunks(g["L1_attn_bg"], 64),
        "wo_blk": np.ascontiguousarray(
            g["L1_attn_Wo"].T.reshape(32, 128, 8, 128).transpose(2, 1, 0, 3)
            .reshape(8, 128, 4096)).astype(
                ml_dtypes.bfloat16 if os.environ.get("DEVBF16_WO")
                else np.float32),
        "boc": _colchunks(g["L1_attn_bo"], 8),
        "wrT": np.ascontiguousarray(g["L1_ffn_W_r"].T),
        "wkkT": np.ascontiguousarray(g["L1_ffn_W_k"].T),
        "wvvT": np.ascontiguousarray(g["L1_ffn_W_v"].T),
        "murc": _colchunks(g["L1_ffn_mu_r"], 8),
        "mukc": _colchunks(g["L1_ffn_mu_k"], 8),
        "eye_bf": eye_bf,
        "sel8": sel8,
    }
    maps = []
    for c in range(NCORE):
        b, r = c // 2, c % 2
        own = slice(r * HALF, (r + 1) * HALF)
        oth = slice((1 - r) * HALF, (2 - r) * HALF)
        m = dict(shared)
        m["x_own"] = np.ascontiguousarray(x[b, own])
        m["x_oth"] = np.ascontiguousarray(x[b, oth])
        m["tT_own"] = np.ascontiguousarray(tT[:, own])
        m["tT_oth"] = np.ascontiguousarray(tT[:, oth])
        maps.append(m)
    return maps


INPUT_SPECS = [
    ("x_own", [HALF, D], F32), ("x_oth", [HALF, D], F32),
    ("tT_own", [D, HALF], F32), ("tT_oth", [D, HALF], F32),
    ("wqT", [D, D], F32), ("wkT", [D, D], F32), ("wvT", [D, D], F32),
    ("bq_row", [1, D], F32), ("bk_row", [1, D], F32), ("bv_row", [1, D], F32),
    ("routerT", [D, EXP], F32), ("rbias_row", [1, EXP], F32),
    ("wgu_blk", [EXP, 32, 128, 1024], F32), ("gub_col", [EXP, 128, 32], F32),
    ("dwn_blk", [EXP, 128, 16384], F32), ("db8", [EXP, D], F32),
    ("w1_blk", [32, 128, 1024], F32), ("w2_blk", [32, 128, 1024], F32),
    ("b1c", [128, 32], F32), ("b2c", [128, 32], F32),
    ("wg_blk", [64, 128, 4096], F32), ("bgc", [128, 64], F32),
    ("wo_blk", [8, 128, 4096],
     BF16 if os.environ.get("DEVBF16_WO") else F32),
    ("boc", [128, 8], F32),
    ("wrT", [D, D], F32), ("wkkT", [D, D], F32), ("wvvT", [D, D], F32),
    ("murc", [128, 8], F32), ("mukc", [128, 8], F32),
    ("eye_bf", [128, 128], BF16), ("sel8", [8, 1024], F32),
]

L1_INPUTS = {"x_own", "x_oth", "tT_own", "tT_oth", "wqT", "wkT", "wvT",
             "bq_row", "bk_row", "bv_row", "routerT", "rbias_row", "wgu_blk",
             "gub_col", "dwn_blk", "db8", "eye_bf", "sel8"}
L2_INPUTS = {"w1_blk", "w2_blk", "b1c", "b2c", "wg_blk", "bgc", "eye_bf",
             "sel8"}
L3_INPUTS = {"wo_blk", "boc", "wrT", "wkkT", "wvvT", "murc", "mukc", "eye_bf",
             "sel8"}



class Ctx:
    pass


def emit_common(k, A, inp, want_sel8=True):
    c = Ctx()
    c.psA = k.psum("psA", [128, 1024], F32)
    c.psB = k.psum("psB", [128, 512], F32)
    c.psT = k.psum("psT", [128, 512], BF16)
    c.psO = k.psum("psO", [128, 2048], F32)

    c.eye_bf = A.alloc("eye_bf", 64, BF16)
    c.ones_bf = A.alloc("ones_bf", 64, BF16)
    c.ones_f32 = A.alloc("ones_f32", 128, F32)
    c.eps_col = A.alloc("eps_col", 1, F32)

    k.do("sync", lambda e: e.dma_start(out=c.eye_bf.ap, in_=inp["eye_bf"].ap),
         reads=(inp["eye_bf"],), writes=(c.eye_bf,), dma=True)
    k.do("vector", lambda e: e.memset(c.ones_bf.ap, 1.0), writes=(c.ones_bf,))
    k.do("vector", lambda e: e.memset(c.ones_f32.ap, 1.0),
         writes=(c.ones_f32,))
    k.do("vector", lambda e: e.memset(c.eps_col.ap, 1e-5),
         writes=(c.eps_col,))
    if want_sel8:
        c.sel8_bf = A.alloc("sel8_bf", 512, BF16)
        sel_st = A.alloc("sel_st", 1024, F32)
        k.do("sync", lambda e: e.dma_start(out=sel_st.ap[0:8, :],
                                           in_=inp["sel8"].ap),
             reads=(inp["sel8"],), writes=(sel_st,), dma=True)
        k.do("vector", lambda e: e.tensor_copy(c.sel8_bf.ap[0:8, :],
                                               sel_st.ap[0:8, :]),
             reads=(sel_st,), writes=(c.sel8_bf,))
        c.sel8_f32 = sel_st
    return c


def load_bf16_weight(k, A, inp, c, name, rows, cols):
    kc = rows // 128
    wbf = A.alloc(name + "_bf", kc * cols // 2, BF16)
    src = inp[name].ap.rearrange("(kc p) h -> p kc h", p=128)
    dst = wbf.ap.rearrange("p (kc h) -> p kc h", kc=kc)
    k.do("gpsimd", lambda e: e.dma_start(out=dst, in_=src),
         reads=(inp[name],), writes=(wbf,), dma=True)
    return wbf


def build_l0_consts(k, A, inp, c):
    A.mark()
    c.gub = A.alloc("gub", 256, F32)
    c.db8_bf = A.alloc("db8_bf", 512, BF16)
    A.mark()
    c.eye_f32 = A.alloc("eye_f32", 128, F32)
    c.wqT_bf = A.alloc("wqT_bf", 4096, BF16)
    c.wkT_bf = A.alloc("wkT_bf", 4096, BF16)
    c.wvT_bf = A.alloc("wvT_bf", 4096, BF16)
    c.bq_col = A.alloc("bq_col", 8, F32)
    c.bk_col = A.alloc("bk_col", 8, F32)
    c.bv_row_bf = A.alloc("bv_row_bf", 512, BF16)
    c.routerT_bf = A.alloc("routerT_bf", 32, BF16)
    c.rbias_row_bf = A.alloc("rbias_row_bf", 8, BF16)
    A.mark()
    stage = A.alloc("wstage", 8192, F32)
    rstage = A.alloc("rstage", 64, F32)
    rbst = A.alloc("rbst", 8, F32)
    bvraw = A.alloc("bv_raw", 1024, F32)
    dbst = A.alloc("dbst", 1024, F32)
    raws = {n: A.alloc(n + "_raw", 1024, F32) for n in ("bq_row", "bk_row")}
    flds = {n: A.alloc(n + "_fld", 1024, F32) for n in ("bq_row", "bk_row")}

    k.do("vector", lambda e: e.tensor_copy(c.eye_f32.ap, c.eye_bf.ap),
         reads=(c.eye_bf,), writes=(c.eye_f32,))
    k.do("sync", lambda e: e.dma_start(
        out=c.gub.ap.rearrange("p (ex n) -> p ex n", ex=8),
        in_=inp["gub_col"].ap.rearrange("ex p n -> p ex n")), reads=(inp["gub_col"],), writes=(c.gub,),
        dma=True)
    k.do("sync", lambda e: e.dma_start(out=dbst.ap[0:8, :],
                                       in_=inp["db8"].ap),
         reads=(inp["db8"],), writes=(dbst,), dma=True)
    k.do("vector", lambda e: e.tensor_copy(c.db8_bf.ap[0:8, :],
                                           dbst.ap[0:8, :]),
         reads=(dbst,), writes=(c.db8_bf,))
    for wname, fold in (("wqT", True), ("wkT", True), ("wvT", False)):
        wbf = getattr(c, wname + "_bf")
        src_ = inp[wname].ap.rearrange("(kc p) h -> p kc h", p=128)
        dst = stage.ap.rearrange("p (kc h) -> p kc h", kc=8)
        k.do("sync", lambda e, s=src_, d=dst: e.dma_start(out=d, in_=s),
             reads=(inp[wname],), writes=(stage,), dma=True)
        if fold:
            k.do("vector", lambda e, sv=stage.ap, bv=wbf.ap: e.tensor_sub(
                bv[:, 0:8192:2], sv[:, 0:8192:2], sv[:, 1:8192:2]),
                reads=(stage,), writes=(wbf,))
            k.do("vector", lambda e, sv=stage.ap, bv=wbf.ap: e.tensor_add(
                bv[:, 1:8192:2], sv[:, 0:8192:2], sv[:, 1:8192:2]),
                reads=(stage,), writes=(wbf,))
        else:
            k.do("vector", lambda e, sv=stage.ap, bv=wbf.ap: e.tensor_copy(
                bv, sv), reads=(stage,), writes=(wbf,))
    for bname, tgt in (("bq_row", c.bq_col), ("bk_row", c.bk_col)):
        raw, fld = raws[bname], flds[bname]
        k.do("sync", lambda e, raw=raw, s=inp[bname]: e.dma_start(
            out=raw.ap[0:1, :], in_=s.ap), reads=(inp[bname],),
            writes=(raw,), dma=True)
        k.do("vector", lambda e, raw=raw, fld=fld: e.tensor_sub(
            fld.ap[0:1, 0:1024:2], raw.ap[0:1, 0:1024:2],
            raw.ap[0:1, 1:1024:2]), reads=(raw,), writes=(fld,))
        k.do("vector", lambda e, raw=raw, fld=fld: e.tensor_add(
            fld.ap[0:1, 1:1024:2], raw.ap[0:1, 0:1024:2],
            raw.ap[0:1, 1:1024:2]), reads=(raw,), writes=(fld,))
        for m in range(8):
            k.do("tensor", lambda e, fld=fld, m=m: e.matmul(
                c.psB.ap[:, m:m + 1], fld.ap[0:1, m * 128:(m + 1) * 128],
                c.ones_f32.ap[0:1, 0:1], start=True, stop=True,
                skip_group_check=True),
                reads=(fld, c.ones_f32), writes=(c.psB,))
        k.do("vector", lambda e, tgt=tgt: e.tensor_copy(
            tgt.ap, c.psB.ap[:, 0:8]), reads=(c.psB,), writes=(tgt,))
    k.do("sync", lambda e: e.dma_start(out=bvraw.ap[0:1, :],
                                       in_=inp["bv_row"].ap),
         reads=(inp["bv_row"],), writes=(bvraw,), dma=True)
    k.do("vector", lambda e: e.tensor_copy(c.bv_row_bf.ap[0:1, :],
                                           bvraw.ap[0:1, :]),
         reads=(bvraw,), writes=(c.bv_row_bf,))
    k.do("sync", lambda e: e.dma_start(
        out=rstage.ap.rearrange("p (kc h) -> p kc h", kc=8),
        in_=inp["routerT"].ap.rearrange("(kc p) h -> p kc h", p=128)),
        reads=(inp["routerT"],), writes=(rstage,), dma=True)
    k.do("vector", lambda e: e.tensor_copy(c.routerT_bf.ap, rstage.ap),
         reads=(rstage,), writes=(c.routerT_bf,))
    k.do("sync", lambda e: e.dma_start(out=rbst.ap[0:1, 0:8],
                                       in_=inp["rbias_row"].ap),
         reads=(inp["rbias_row"],), writes=(rbst,), dma=True)
    k.do("vector", lambda e: e.tensor_copy(c.rbias_row_bf.ap[0:1, 0:8],
                                           rbst.ap[0:1, 0:8]),
         reads=(rbst,), writes=(c.rbias_row_bf,))
    A.reset()



def emit_p0a(k, A, inp, c, scr):
    A.mark()
    psA, psB, psT, psO = c.psA, c.psB, c.psT, c.psO
    xtok = [A.alloc(f"xtok{i}", 4096) for i in (0, 1)]
    ttb = [A.alloc(f"ttb{i}", 4096) for i in (0, 1)]
    xT = A.alloc("xT", 4096)
    sq = A.alloc("p0sq", 4096)
    sums = A.alloc("p0sums", 4)
    sqs = A.alloc("p0sqs", 4)
    rows_sum = A.alloc("rows_sum", 128)
    rows_sq = A.alloc("rows_sq", 128)
    rows_m = A.alloc("rows_m", 128)
    rows_msq = A.alloc("rows_msq", 128)
    rows_m2 = A.alloc("rows_m2", 128)
    rows_var = A.alloc("rows_var", 128)
    rows_rstd = A.alloc("rows_rstd", 128)
    rows_B = A.alloc("rows_B", 128)
    xn = A.alloc("p0xn", 2048, BF16)
    xnf = A.alloc("p0xnf", 512)
    qtmp = A.alloc("p0qtmp", 512)
    qk_stage = A.alloc("qk_stage", 2048, BF16)
    v_stage = A.alloc("v_stage", 2048, BF16)
    lg = A.alloc("p0lg", 8)
    srt = [A.alloc(f"p0srt{i}", 8) for i in (0, 1)]
    negm = A.alloc("p0negm", 1)
    eexp = A.alloc("p0eexp", 8)
    maskb = A.alloc("p0mask", 8)
    em = A.alloc("p0em", 8)
    ssum = A.alloc("p0ssum", 1)
    rec = A.alloc("p0rec", 1)
    comb = A.alloc("p0comb", 8)
    comb_bf = A.alloc("p0comb_bf", 4, BF16)
    combT = A.alloc("p0combT", 256, BF16)

    for tt in range(16):
        own = tt < 8
        lt = tt % 8
        t0 = lt * 512
        gk = t0 if own else HALF + t0
        xin = inp["x_own" if own else "x_oth"]
        tin = inp["tT_own" if own else "tT_oth"]
        buf = xtok[tt % 2]
        tb = ttb[tt % 2]
        k.do("sync", lambda e, xin=xin, t0=t0, buf=buf: e.dma_start(
            out=buf.ap.rearrange("p (a d) -> p a d", a=4),
            in_=xin.ap[t0:t0 + 512, :].rearrange("(a p) d -> p a d", p=128)),
            reads=(xin,), writes=(buf,), dma=True)
        k.do("sync", lambda e, tin=tin, t0=t0, tb=tb: e.dma_start(
            out=tb.ap.rearrange("p (kc t) -> p kc t", kc=8),
            in_=tin.ap[:, t0:t0 + 512].rearrange("(kc p) t -> p kc t", p=128)),
            reads=(tin,), writes=(tb,), dma=True)
        for a in range(4):
            for kc in range(8):
                k.do("tensor", lambda e, a=a, kc=kc, buf=buf: e.transpose(
                    psA.ap[:, kc * 128:(kc + 1) * 128],
                    buf.ap[:, a * 1024 + kc * 128:a * 1024 + (kc + 1) * 128],
                    c.eye_f32.ap),
                    reads=(buf, c.eye_f32), writes=(psA,))
            k.do("vector", lambda e, a=a: e.tensor_copy(
                xT.ap.rearrange("p (kc t) -> p kc t", kc=8)[
                    :, :, a * 128:(a + 1) * 128],
                psA.ap.rearrange("p (kc t) -> p kc t", kc=8)),
                reads=(psA,), writes=(xT,))
        if own:
            k.do("gpsimd", lambda e, t0=t0: e.dma_start(
                out=scr["sXT"].ap.rearrange("(kc d) t -> d kc t", d=128)[
                    :, :, t0:t0 + 512],
                in_=xT.ap.rearrange("p (kc t) -> p kc t", kc=8)),
                reads=(xT,), writes=(scr["sXT"],), dma=True)
        k.do("scalar", lambda e, buf=buf: e.activation(
            sq.ap, buf.ap, AF.Square), reads=(buf,), writes=(sq,))
        k.do("vector", lambda e, buf=buf: e.reduce_sum(
            sums.ap, buf.ap.rearrange("p (a d) -> p a d", a=4), axis=AX.X),
            reads=(buf,), writes=(sums,))
        k.do("vector", lambda e: e.reduce_sum(
            sqs.ap, sq.ap.rearrange("p (a d) -> p a d", a=4), axis=AX.X),
            reads=(sq,), writes=(sqs,))
        k.do("tensor", lambda e: e.transpose(
            psB.ap[0:4, 0:128], sums.ap, c.eye_f32.ap),
            reads=(sums, c.eye_f32), writes=(psB,))
        k.do("vector", lambda e: e.tensor_copy(
            rows_sum.ap[0:4, :], psB.ap[0:4, 0:128]),
            reads=(psB,), writes=(rows_sum,))
        k.do("tensor", lambda e: e.transpose(
            psB.ap[0:4, 0:128], sqs.ap, c.eye_f32.ap),
            reads=(sqs, c.eye_f32), writes=(psB,))
        k.do("vector", lambda e: e.tensor_copy(
            rows_sq.ap[0:4, :], psB.ap[0:4, 0:128]),
            reads=(psB,), writes=(rows_sq,))
        k.do("scalar", lambda e: e.activation(
            rows_m.ap[0:4, :], rows_sum.ap[0:4, :], AF.Copy,
            scale=1.0 / 1024.0), reads=(rows_sum,), writes=(rows_m,))
        k.do("scalar", lambda e: e.activation(
            rows_msq.ap[0:4, :], rows_sq.ap[0:4, :], AF.Copy,
            scale=1.0 / 1024.0), reads=(rows_sq,), writes=(rows_msq,))
        k.do("scalar", lambda e: e.activation(
            rows_m2.ap[0:4, :], rows_m.ap[0:4, :], AF.Square),
            reads=(rows_m,), writes=(rows_m2,))
        k.do("vector", lambda e: e.tensor_sub(
            rows_var.ap[0:4, :], rows_msq.ap[0:4, :], rows_m2.ap[0:4, :]),
            reads=(rows_msq, rows_m2), writes=(rows_var,))
        k.do("scalar", lambda e: e.activation(
            rows_var.ap[0:4, :], rows_var.ap[0:4, :], AF.Ln,
            bias=c.eps_col.ap[0:4, 0:1]),
            reads=(rows_var, c.eps_col), writes=(rows_var,))
        k.do("scalar", lambda e: e.activation(
            rows_rstd.ap[0:4, :], rows_var.ap[0:4, :], AF.Exp, scale=-0.5),
            reads=(rows_var,), writes=(rows_rstd,))
        k.do("vector", lambda e: e.scalar_tensor_tensor(
            rows_B.ap[0:4, :], rows_m.ap[0:4, :], -1.0, rows_rstd.ap[0:4, :],
            ALU.mult, ALU.mult), reads=(rows_m, rows_rstd), writes=(rows_B,))
        for a in range(4):
            k.do("tensor", lambda e, a=a: e.matmul(
                psA.ap[:, a * 128:(a + 1) * 128],
                c.sel8_f32.ap[0:4, a * 128:(a + 1) * 128],
                rows_rstd.ap[0:4, :], start=True, stop=True),
                reads=(c.sel8_f32, rows_rstd), writes=(psA,))
            k.do("tensor", lambda e, a=a: e.matmul(
                psA.ap[:, 512 + a * 128:512 + (a + 1) * 128],
                c.sel8_f32.ap[0:4, a * 128:(a + 1) * 128],
                rows_B.ap[0:4, :], start=True, stop=True),
                reads=(c.sel8_f32, rows_B), writes=(psA,))
        for kc in range(8):
            k.do("vector", lambda e, kc=kc: e.tensor_mul(
                xnf.ap, xT.ap[:, kc * 512:(kc + 1) * 512], psA.ap[:, 0:512]),
                reads=(xT, psA), writes=(xnf,))
            k.do("vector", lambda e, kc=kc: e.tensor_add(
                xn.ap[:, kc * 512:(kc + 1) * 512], xnf.ap,
                psA.ap[:, 512:1024]), reads=(xnf, psA), writes=(xn,))
        if own:
            k.do("gpsimd", lambda e, t0=t0: e.dma_start(
                out=scr["sXN"].ap.rearrange("(kc d) t -> d kc t", d=128)[
                    :, :, t0:t0 + 512],
                in_=xn.ap.rearrange("p (kc t) -> p kc t", kc=8)),
                reads=(xn,), writes=(scr["sXN"],), dma=True)
        proj = [("sKT", c.wkT_bf, c.bk_col, gk)]
        if own:
            proj.append(("sQT", c.wqT_bf, c.bq_col, t0))
        for sname, wbf, bcol, dcol in proj:
            for m in range(8):
                slot = psA.ap[:, (m % 2) * 512:(m % 2) * 512 + 512]
                pslot = Buf("slot", None)
                for kc in range(8):
                    k.do("tensor", lambda e, m=m, kc=kc, wbf=wbf, slot=slot:
                         e.matmul(
                             slot,
                             wbf.ap[:, kc * 1024 + m * 128:
                                    kc * 1024 + (m + 1) * 128],
                             xn.ap[:, kc * 512:(kc + 1) * 512],
                             start=(kc == 0), stop=(kc == 7)),
                         reads=(wbf, xn), writes=(psA,))
                k.do("vector", lambda e, m=m, bcol=bcol, slot=slot, tb=tb:
                     e.scalar_tensor_tensor(
                         qk_stage.ap[:, m * 512:(m + 1) * 512], slot,
                         bcol.ap[:, m:m + 1],
                         tb.ap[:, m * 512:(m + 1) * 512],
                         ALU.add, ALU.mult),
                     reads=(psA, bcol, tb), writes=(qk_stage,))
            k.do("gpsimd", lambda e, sname=sname, dcol=dcol: e.dma_start(
                out=scr[sname].ap.rearrange("(kc d) t -> d kc t", d=128)[
                    :, :, dcol:dcol + 512],
                in_=qk_stage.ap.rearrange("p (kc t) -> p kc t", kc=8)),
                reads=(qk_stage,), writes=(scr[sname],), dma=True)
        for hv in range(2):
            for a in range(4):
                for kc in range(8):
                    k.do("tensor", lambda e, hv=hv, a=a, kc=kc: e.matmul(
                        psO.ap[:, a * 512:(a + 1) * 512],
                        xn.ap[:, kc * 512 + a * 128:kc * 512 + (a + 1) * 128],
                        c.wvT_bf.ap[:, kc * 1024 + hv * 512:
                                    kc * 1024 + hv * 512 + 512],
                        start=(kc == 0), stop=False),
                        reads=(xn, c.wvT_bf), writes=(psO,))
                k.do("tensor", lambda e, hv=hv, a=a: e.matmul(
                    psO.ap[:, a * 512:(a + 1) * 512],
                    c.ones_bf.ap[0:1, 0:128],
                    c.bv_row_bf.ap[0:1, hv * 512:hv * 512 + 512],
                    start=False, stop=True),
                    reads=(c.ones_bf, c.bv_row_bf), writes=(psO,))
                k.do("vector", lambda e, hv=hv, a=a: e.tensor_copy(
                    v_stage.ap[:, a * 1024 + hv * 512:a * 1024 + hv * 512
                               + 512],
                    psO.ap[:, a * 512:(a + 1) * 512]),
                    reads=(psO,), writes=(v_stage,))
        k.do("gpsimd", lambda e, gk=gk: e.dma_start(
            out=scr["sV"].ap[gk:gk + 512, :].rearrange(
                "(a p) h -> p a h", p=128),
            in_=v_stage.ap.rearrange("p (a h) -> p a h", a=4)),
            reads=(v_stage,), writes=(scr["sV"],), dma=True)
        if not own:
            continue
        for a in range(4):
            for kc in range(8):
                k.do("tensor", lambda e, a=a, kc=kc: e.matmul(
                    psB.ap[:, 0:8],
                    xn.ap[:, kc * 512 + a * 128:kc * 512 + (a + 1) * 128],
                    c.routerT_bf.ap[:, kc * 8:(kc + 1) * 8],
                    start=(kc == 0), stop=False),
                    reads=(xn, c.routerT_bf), writes=(psB,))
            k.do("tensor", lambda e: e.matmul(
                psB.ap[:, 0:8], c.ones_bf.ap[0:1, 0:128],
                c.rbias_row_bf.ap[0:1, 0:8], start=False, stop=True),
                reads=(c.ones_bf, c.rbias_row_bf), writes=(psB,))
            k.do("vector", lambda e: e.tensor_copy(lg.ap, psB.ap[:, 0:8]),
                 reads=(psB,), writes=(lg,))
            cur = lg
            for r in range(8):
                dst = srt[r % 2]
                if r % 2 == 0:
                    k.do("vector", lambda e, cur=cur, dst=dst: e.tensor_max(
                        dst.ap[:, 0:8:2], cur.ap[:, 0:8:2], cur.ap[:, 1:8:2]),
                        reads=(cur,), writes=(dst,))
                    k.do("vector", lambda e, cur=cur, dst=dst: e.tensor_tensor(
                        dst.ap[:, 1:8:2], cur.ap[:, 0:8:2], cur.ap[:, 1:8:2],
                        ALU.min), reads=(cur,), writes=(dst,))
                else:
                    k.do("vector", lambda e, cur=cur, dst=dst: e.tensor_copy(
                        dst.ap[:, 0:1], cur.ap[:, 0:1]),
                        reads=(cur,), writes=(dst,))
                    k.do("vector", lambda e, cur=cur, dst=dst: e.tensor_copy(
                        dst.ap[:, 7:8], cur.ap[:, 7:8]),
                        reads=(cur,), writes=(dst,))
                    k.do("vector", lambda e, cur=cur, dst=dst: e.tensor_max(
                        dst.ap[:, 1:7:2], cur.ap[:, 1:7:2], cur.ap[:, 2:8:2]),
                        reads=(cur,), writes=(dst,))
                    k.do("vector", lambda e, cur=cur, dst=dst: e.tensor_tensor(
                        dst.ap[:, 2:8:2], cur.ap[:, 1:7:2], cur.ap[:, 2:8:2],
                        ALU.min), reads=(cur,), writes=(dst,))
                cur = dst
            k.do("vector", lambda e, cur=cur: e.tensor_scalar_mul(
                negm.ap, cur.ap[:, 0:1], -1.0), reads=(cur,), writes=(negm,))
            k.do("scalar", lambda e: e.activation(
                eexp.ap, lg.ap, AF.Exp, bias=negm.ap),
                reads=(lg, negm), writes=(eexp,))
            k.do("vector", lambda e, cur=cur: e.tensor_single_scalar(
                maskb.ap, lg.ap, cur.ap[:, 3:4], ALU.is_ge),
                reads=(lg, cur), writes=(maskb,))
            k.do("vector", lambda e: e.tensor_mul(em.ap, maskb.ap, eexp.ap),
                 reads=(maskb, eexp), writes=(em,))
            k.do("vector", lambda e: e.reduce_sum(ssum.ap, em.ap, axis=AX.X),
                 reads=(em,), writes=(ssum,))
            k.do("vector", lambda e: e.reciprocal(rec.ap, ssum.ap),
                 reads=(ssum,), writes=(rec,))
            k.do("vector", lambda e: e.tensor_single_scalar(
                comb.ap, em.ap, rec.ap, ALU.mult),
                reads=(em, rec), writes=(comb,))
            k.do("vector", lambda e: e.tensor_copy(comb_bf.ap, comb.ap),
                 reads=(comb,), writes=(comb_bf,))
            k.do("tensor", lambda e, a=a: e.transpose(
                psT.ap[0:8, a * 128:(a + 1) * 128], comb_bf.ap, c.eye_bf.ap),
                reads=(comb_bf, c.eye_bf), writes=(psT,))
            k.do("vector", lambda e, a=a: e.tensor_copy(
                combT.ap[0:8, a * 128:(a + 1) * 128],
                psT.ap[0:8, a * 128:(a + 1) * 128]),
                reads=(psT,), writes=(combT,))
        k.do("gpsimd", lambda e, t0=t0: e.dma_start(
            out=scr["sCT"].ap[:, t0:t0 + 512], in_=combT.ap[0:8, :]),
            reads=(combT,), writes=(scr["sCT"],), dma=True)
    A.reset()



def emit_p0b(k, A, inp, c, scr):
    A.mark()
    psA, psB, psT, psO = c.psA, c.psB, c.psT, c.psO
    xnb = A.alloc("mx_xnb", 4096, BF16)
    ct = A.alloc("mx_ct", 512, BF16)
    moe_acc = A.alloc("mx_acc", 8192)
    act_sb = [A.alloc(f"mx_act{s}", 4096, BF16) for s in range(2)]
    NGU = 8
    gu_roll = [A.alloc(f"mx_gu{i}", 512, BF16) for i in range(NGU)]
    dwn_roll = [A.alloc(f"mx_dwn{i}", 8192, BF16) for i in range(2)]
    comb_sb = [A.alloc(f"mx_comb{s}", 512) for s in range(2)]
    gmin = A.alloc("mx_gmin", 512)
    upc = A.alloc("mx_upc", 512)
    upc2 = A.alloc("mx_upc2", 512)
    gs = A.alloc("mx_gs", 512)
    tmp = A.alloc("mx_tmp", 512)

    def gu_dma(e_i, j, slot):
        src = inp["wgu_blk"].ap[e_i, j]
        k.do("gpsimd", lambda e: e.dma_start(out=gu_roll[slot].ap, in_=src),
             reads=(inp["wgu_blk"],), writes=(gu_roll[slot],), dma=True)

    for tile in range(4):
        t0 = tile * 1024
        k.do("sync", lambda e, t0=t0: e.dma_start(
            out=xnb.ap.rearrange("p (kc t) -> p kc t", kc=8),
            in_=scr["sXN"].ap.rearrange("(kc d) t -> d kc t", d=128)[
                :, :, t0:t0 + 1024]),
            reads=(scr["sXN"],), writes=(xnb,), dma=True)
        k.do("sync", lambda e, t0=t0: e.dma_start(
            out=ct.ap[0:8, :], in_=scr["sCT"].ap[:, t0:t0 + 1024]),
            reads=(scr["sCT"],), writes=(ct,), dma=True)
        for dc in range(8):
            for sub in range(2):
                k.do("tensor", lambda e, dc=dc, sub=sub: e.matmul(
                    psB.ap, c.db8_bf.ap[0:8, dc * 128:(dc + 1) * 128],
                    ct.ap[0:8, sub * 512:(sub + 1) * 512],
                    start=True, stop=True),
                    reads=(c.db8_bf, ct), writes=(psB,))
                k.do("vector", lambda e, dc=dc, sub=sub: e.tensor_copy(
                    moe_acc.ap[:, dc * 1024 + sub * 512:
                               dc * 1024 + sub * 512 + 512], psB.ap),
                    reads=(psB,), writes=(moe_acc,))
        for ei in range(EXP):
            dwn = dwn_roll[ei % 2]
            k.do("gpsimd", lambda e, ei=ei, dwn=dwn: e.dma_start(
                out=dwn.ap, in_=inp["dwn_blk"].ap[ei]),
                reads=(inp["dwn_blk"],), writes=(dwn,), dma=True)
            for sub in range(2):
                k.do("tensor", lambda e, ei=ei, sub=sub: e.matmul(
                    psB.ap, c.sel8_bf.ap[0:8, ei * 128:(ei + 1) * 128],
                    ct.ap[0:8, sub * 512:(sub + 1) * 512],
                    start=True, stop=True),
                    reads=(c.sel8_bf, ct), writes=(psB,))
                k.do("vector", lambda e, sub=sub: e.tensor_scalar_mul(
                    comb_sb[sub].ap, psB.ap, 1.0 / 1.702),
                    reads=(psB,), writes=(comb_sb[sub],))
            if ei == 0 and tile == 0:
                for pre in range(NGU // 2):
                    gu_dma(0, pre, 2 * pre)
                    gu_dma(0, 16 + pre, 2 * pre + 1)
            for ic in range(16):
                gslot, uslot = (ic % (NGU // 2)) * 2, (ic % (NGU // 2)) * 2 + 1
                gbuf, ubuf = gu_roll[gslot], gu_roll[uslot]
                gu_ps = [(psA, 0), (psA, 512)] if ic % 2 == 0 else \
                        [(psB, 0), (psO, 0)]
                for sub in range(2):
                    (gps, goff), (ups, uoff) = gu_ps[0], gu_ps[1]
                    if sub == 1:
                        (gps, goff), (ups, uoff) = (psB, 0), (psO, 0)
                    else:
                        (gps, goff), (ups, uoff) = (psA, 0), (psA, 512)
                    for kc in range(8):
                        k.do("tensor", lambda e, kc=kc, sub=sub, gbuf=gbuf,
                             gps=gps, goff=goff: e.matmul(
                                 gps.ap[:, goff:goff + 512],
                                 gbuf.ap[:, kc * 128:(kc + 1) * 128],
                                 xnb.ap[:, kc * 1024 + sub * 512:
                                        kc * 1024 + sub * 512 + 512],
                                 start=(kc == 0), stop=(kc == 7)),
                             reads=(gbuf, xnb), writes=(gps,))
                    for kc in range(8):
                        k.do("tensor", lambda e, kc=kc, sub=sub, ubuf=ubuf,
                             ups=ups, uoff=uoff: e.matmul(
                                 ups.ap[:, uoff:uoff + 512],
                                 ubuf.ap[:, kc * 128:(kc + 1) * 128],
                                 xnb.ap[:, kc * 1024 + sub * 512:
                                        kc * 1024 + sub * 512 + 512],
                                 start=(kc == 0), stop=(kc == 7)),
                             reads=(ubuf, xnb), writes=(ups,))
                    gb = c.gub.ap[:, ei * 32 + ic:ei * 32 + ic + 1]
                    ub = c.gub.ap[:, ei * 32 + 16 + ic:ei * 32 + 16 + ic + 1]
                    k.do("vector", lambda e, gps=gps, goff=goff, gb=gb:
                         e.tensor_scalar(gmin.ap, gps.ap[:, goff:goff + 512],
                                         gb, 7.0, ALU.add, ALU.min),
                         reads=(gps, c.gub), writes=(gmin,))
                    k.do("vector", lambda e, ups=ups, uoff=uoff, ub=ub:
                         e.tensor_scalar(upc.ap, ups.ap[:, uoff:uoff + 512],
                                         ub, 7.0, ALU.add, ALU.min),
                         reads=(ups, c.gub), writes=(upc,))
                    k.do("vector", lambda e: e.tensor_scalar(
                        upc2.ap, upc.ap, -7.0, 1.0, ALU.max, ALU.add),
                        reads=(upc,), writes=(upc2,))
                    k.do("scalar", lambda e: e.activation(
                        gs.ap, gmin.ap, AF.Silu, scale=1.702),
                        reads=(gmin,), writes=(gs,))
                    k.do("vector", lambda e, sub=sub, ic=ic: e.tensor_mul(
                        act_sb[sub].ap[:, ic * 512:(ic + 1) * 512],
                        upc2.ap, gs.ap),
                        reads=(upc2, gs), writes=(act_sb[sub],))
                nic = ic + NGU // 2
                ne, nj = (ei, nic) if nic < 16 else (ei + 1, nic - 16)
                if ne < EXP:
                    gu_dma(ne, nj, gslot)
                    gu_dma(ne, 16 + nj, uslot)
                elif tile < 3:
                    gu_dma(0, nj, gslot)
                    gu_dma(0, 16 + nj, uslot)
            for sub in range(2):
                for dg in range(2):
                    for dc4 in range(4):
                        dc = dg * 4 + dc4
                        for ic in range(16):
                            k.do("tensor", lambda e, ic=ic, dc=dc, sub=sub,
                                 dc4=dc4, dwn=dwn: e.matmul(
                                     psO.ap[:, dc4 * 512:(dc4 + 1) * 512],
                                     dwn.ap[:, ic * 1024 + dc * 128:
                                            ic * 1024 + (dc + 1) * 128],
                                     act_sb[sub].ap[:, ic * 512:
                                                    (ic + 1) * 512],
                                     start=(ic == 0), stop=(ic == 15)),
                                 reads=(dwn, act_sb[sub]), writes=(psO,))
                        k.do("vector", lambda e, dc=dc, sub=sub, dc4=dc4:
                             e.tensor_mul(
                                 tmp.ap,
                                 psO.ap[:, dc4 * 512:(dc4 + 1) * 512],
                                 comb_sb[sub].ap),
                             reads=(psO, comb_sb[sub]), writes=(tmp,))
                        k.do("vector", lambda e, dc=dc, sub=sub:
                             e.tensor_add(
                                 moe_acc.ap[:, dc * 1024 + sub * 512:
                                            dc * 1024 + sub * 512 + 512],
                                 moe_acc.ap[:, dc * 1024 + sub * 512:
                                            dc * 1024 + sub * 512 + 512],
                                 tmp.ap),
                             reads=(tmp, moe_acc), writes=(moe_acc,))
        k.do("gpsimd", lambda e, t0=t0: e.dma_start(
            out=scr["sMoE"].ap.rearrange("(dc d) t -> d dc t", d=128)[
                :, :, t0:t0 + 1024],
            in_=moe_acc.ap.rearrange("p (dc t) -> p dc t", dc=8)),
            reads=(moe_acc,), writes=(scr["sMoE"],), dma=True)
    A.reset()



def emit_p1(k, A, inp, c, scr):
    A.mark()
    psA, psB, psT, psO = c.psA, c.psB, c.psT, c.psO
    Ksb = A.alloc("at_K", 16384, BF16)
    Vsb = A.alloc("at_V", 16384, BF16)
    Ssb = A.alloc("at_S", 8192)
    Pbf = A.alloc("at_P", 4096, BF16)
    Qsb = [A.alloc(f"at_Q{i}", 1024, BF16) for i in range(2)]
    Mrun = A.alloc("at_M", 256)
    Mbf = A.alloc("at_Mbf", 128, BF16)
    mcolb = A.alloc("at_mcol", 1, BF16)
    mtsb = A.alloc("at_mt", 128, BF16)
    mrow = A.alloc("at_mrow", 128, BF16)
    Mb = A.alloc("at_Mb", 256)
    ost = A.alloc("at_ost", 2048)
    rowst = A.alloc("at_rows", 512)
    stmp = A.alloc("at_stmp", 256)

    for h in range(2):
        r0 = h * 4096
        oname, rname = ("sOA", "sRA") if h == 0 else ("sOB", "sRB")
        k.do("sync", lambda e, r0=r0: e.dma_start(
            out=Ksb.ap.rearrange("p (m t) -> p m t", m=8),
            in_=scr["sKT"].ap.rearrange("(m d) t -> d m t", d=128)[
                :, :, r0:r0 + 4096]),
            reads=(scr["sKT"],), writes=(Ksb,), dma=True)
        k.do("sync", lambda e, r0=r0: e.dma_start(
            out=Vsb.ap.rearrange("p (tc hh) -> p tc hh", tc=32),
            in_=scr["sV"].ap[r0:r0 + 4096, :].rearrange(
                "(tc p) hh -> p tc hh", p=128)),
            reads=(scr["sV"],), writes=(Vsb,), dma=True)
        for qb in range(16):
            q0 = qb * 256
            Q = Qsb[qb % 2]
            k.do("sync", lambda e, q0=q0, Q=Q: e.dma_start(
                out=Q.ap.rearrange("p (m t) -> p m t", m=8),
                in_=scr["sQT"].ap.rearrange("(m d) t -> d m t", d=128)[
                    :, :, q0:q0 + 256]),
                reads=(scr["sQT"],), writes=(Q,), dma=True)
            for kt in range(32):
                off = 0 if kt % 2 == 0 else 512
                for m in range(8):
                    k.do("tensor", lambda e, kt=kt, m=m, off=off, Q=Q:
                         e.matmul(
                             psA.ap[:, off:off + 256],
                             Ksb.ap[:, m * 4096 + kt * 128:
                                    m * 4096 + (kt + 1) * 128],
                             Q.ap[:, m * 256:(m + 1) * 256],
                             start=(m == 0), stop=(m == 7),
                             skip_group_check=True),
                         reads=(Ksb, Q), writes=(psA,))
                k.do("vector", lambda e, kt=kt, off=off: e.tensor_copy(
                    Ssb.ap[:, kt * 256:(kt + 1) * 256],
                    psA.ap[:, off:off + 256]), reads=(psA,), writes=(Ssb,))
                if kt == 0:
                    k.do("vector", lambda e, off=off: e.tensor_copy(
                        Mrun.ap, psA.ap[:, off:off + 256]),
                        reads=(psA,), writes=(Mrun,))
                else:
                    k.do("vector", lambda e, off=off: e.tensor_max(
                        Mrun.ap, Mrun.ap, psA.ap[:, off:off + 256]),
                        reads=(psA, Mrun), writes=(Mrun,))
            k.do("vector", lambda e: e.tensor_copy(Mbf.ap, Mrun.ap),
                 reads=(Mrun,), writes=(Mbf,))
            for j in range(2):
                k.do("tensor", lambda e, j=j: e.transpose(
                    psT.ap[:, j * 128:(j + 1) * 128],
                    Mbf.ap[:, j * 128:(j + 1) * 128], c.eye_bf.ap),
                    reads=(Mbf, c.eye_bf), writes=(psT,))
                k.do("vector", lambda e, j=j: e.tensor_copy(
                    mtsb.ap[:, j * 128:(j + 1) * 128],
                    psT.ap[:, j * 128:(j + 1) * 128]),
                    reads=(psT,), writes=(mtsb,))
                k.do("vector", lambda e, j=j: e.reduce_max(
                    mcolb.ap[:, j:j + 1], mtsb.ap[:, j * 128:(j + 1) * 128],
                    axis=AX.X), reads=(mtsb,), writes=(mcolb,))
            for j in range(2):
                k.do("tensor", lambda e, j=j: e.transpose(
                    psT.ap[0:1, 256 + j * 128:256 + (j + 1) * 128],
                    mcolb.ap[:, j:j + 1], c.eye_bf.ap),
                    reads=(mcolb, c.eye_bf), writes=(psT,))
                k.do("vector", lambda e, j=j: e.tensor_copy(
                    mrow.ap[0:1, j * 128:(j + 1) * 128],
                    psT.ap[0:1, 256 + j * 128:256 + (j + 1) * 128]),
                    reads=(psT,), writes=(mrow,))
            k.do("tensor", lambda e: e.matmul(
                psB.ap[:, 0:256], c.ones_bf.ap[0:1, :],
                mrow.ap[0:1, 0:256], start=True, stop=True,
                skip_group_check=True),
                reads=(c.ones_bf, mrow), writes=(psB,))
            k.do("vector", lambda e: e.tensor_copy(Mb.ap, psB.ap[:, 0:256]),
                 reads=(psB,), writes=(Mb,))
            for kt in range(32):
                k.do("vector", lambda e, kt=kt: e.tensor_sub(
                    stmp.ap, Ssb.ap[:, kt * 256:(kt + 1) * 256], Mb.ap),
                    reads=(Ssb, Mb), writes=(stmp,))
                k.do("scalar", lambda e, kt=kt: e.activation(
                    Pbf.ap[:, kt * 256:(kt + 1) * 256], stmp.ap, AF.Exp,
                    scale=SCALE), reads=(stmp,), writes=(Pbf,))
            for dg in range(2):
                for kt in range(32):
                    for mi in range(4):
                        m = dg * 4 + mi
                        k.do("tensor", lambda e, kt=kt, m=m, mi=mi: e.matmul(
                            psO.ap[:, mi * 512:mi * 512 + 256],
                            Vsb.ap[:, kt * 1024 + m * 128:
                                   kt * 1024 + (m + 1) * 128],
                            Pbf.ap[:, kt * 256:(kt + 1) * 256],
                            start=(kt == 0), stop=(kt == 31),
                            skip_group_check=True),
                            reads=(Vsb, Pbf), writes=(psO,))
                    if dg == 0:
                        k.do("tensor", lambda e, kt=kt: e.matmul(
                            psB.ap[0:1, 256:512], c.ones_bf.ap[:, 0:1],
                            Pbf.ap[:, kt * 256:(kt + 1) * 256],
                            start=(kt == 0), stop=(kt == 31),
                            skip_group_check=True),
                            reads=(c.ones_bf, Pbf), writes=(psB,))
                k.do("vector", lambda e, dg=dg: e.tensor_copy(
                    ost.ap.rearrange("p (m t) -> p m t", m=8)[
                        :, dg * 4:(dg + 1) * 4, :],
                    psO.ap.rearrange("p (b h) -> p b h", b=4)[:, :, 0:256]),
                    reads=(psO,), writes=(ost,))
            k.do("vector", lambda e: e.tensor_copy(
                rowst.ap[0:1, 0:256], mrow.ap[0:1, 0:256]),
                reads=(mrow,), writes=(rowst,))
            k.do("vector", lambda e: e.tensor_copy(
                rowst.ap[0:1, 256:512], psB.ap[0:1, 256:512]),
                reads=(psB,), writes=(rowst,))
            k.do("gpsimd", lambda e, q0=q0, oname=oname: e.dma_start(
                out=scr[oname].ap.rearrange("(m d) t -> d m t", d=128)[
                    :, :, q0:q0 + 256],
                in_=ost.ap.rearrange("p (m t) -> p m t", m=8)),
                reads=(ost,), writes=(scr[oname],), dma=True)
            k.do("gpsimd", lambda e, q0=q0, rname=rname: e.dma_start(
                out=scr[rname].ap[0:1, q0:q0 + 256],
                in_=rowst.ap[0:1, 0:256]),
                reads=(rowst,), writes=(scr[rname],), dma=True)
            k.do("gpsimd", lambda e, q0=q0, rname=rname: e.dma_start(
                out=scr[rname].ap[1:2, q0:q0 + 256],
                in_=rowst.ap[0:1, 256:512]),
                reads=(rowst,), writes=(scr[rname],), dma=True)
    A.reset()



def emit_p1c(k, A, inp, c, scr):
    A.mark()
    psA = c.psA
    xTt = A.alloc("mg_x", 4096)
    moet = A.alloc("mg_moe", 4096)
    oat = A.alloc("mg_oa", 4096)
    obt = A.alloc("mg_ob", 4096)
    x1t = A.alloc("mg_x1", 4096)
    mA = A.alloc("mg_mA", 512)
    sA = A.alloc("mg_sA", 512)
    mB = A.alloc("mg_mB", 512)
    sB = A.alloc("mg_sB", 512)
    mm = A.alloc("mg_m", 512)
    fA = A.alloc("mg_fA", 512)
    fB = A.alloc("mg_fB", 512)
    u1 = A.alloc("mg_u1", 512)
    u2 = A.alloc("mg_u2", 512)
    srow = A.alloc("mg_s", 512)
    rec = A.alloc("mg_rec", 512)
    gA = A.alloc("mg_gA", 512)
    gB = A.alloc("mg_gB", 512)
    ut = A.alloc("mg_ut", 512)
    vt = A.alloc("mg_vt", 512)

    for tile in range(8):
        t0 = tile * 512
        for nm, b in (("sXT", xTt), ("sMoE", moet), ("sOA", oat),
                      ("sOB", obt)):
            k.do("sync", lambda e, nm=nm, b=b, t0=t0: e.dma_start(
                out=b.ap.rearrange("p (m t) -> p m t", m=8),
                in_=scr[nm].ap.rearrange("(m d) t -> d m t", d=128)[
                    :, :, t0:t0 + 512]),
                reads=(scr[nm],), writes=(b,), dma=True)
        for nm, rb, sb_ in (("sRA", mA, sA), ("sRB", mB, sB)):
            k.do("sync", lambda e, nm=nm, rb=rb, t0=t0: e.dma_start(
                out=rb.ap[0:1, :], in_=scr[nm].ap[0:1, t0:t0 + 512]),
                reads=(scr[nm],), writes=(rb,), dma=True)
            k.do("sync", lambda e, nm=nm, sb_=sb_, t0=t0: e.dma_start(
                out=sb_.ap[0:1, :], in_=scr[nm].ap[1:2, t0:t0 + 512]),
                reads=(scr[nm],), writes=(sb_,), dma=True)
        R1 = (slice(0, 1), slice(None))
        k.do("vector", lambda e: e.tensor_max(
            mm.ap[R1], mA.ap[R1], mB.ap[R1]), reads=(mA, mB), writes=(mm,))
        k.do("vector", lambda e: e.tensor_sub(
            fA.ap[R1], mA.ap[R1], mm.ap[R1]), reads=(mA, mm), writes=(fA,))
        k.do("vector", lambda e: e.tensor_sub(
            fB.ap[R1], mB.ap[R1], mm.ap[R1]), reads=(mB, mm), writes=(fB,))
        k.do("scalar", lambda e: e.activation(
            fA.ap[R1], fA.ap[R1], AF.Exp, scale=SCALE),
            reads=(fA,), writes=(fA,))
        k.do("scalar", lambda e: e.activation(
            fB.ap[R1], fB.ap[R1], AF.Exp, scale=SCALE),
            reads=(fB,), writes=(fB,))
        k.do("vector", lambda e: e.tensor_mul(
            u1.ap[R1], sA.ap[R1], fA.ap[R1]), reads=(sA, fA), writes=(u1,))
        k.do("vector", lambda e: e.tensor_mul(
            u2.ap[R1], sB.ap[R1], fB.ap[R1]), reads=(sB, fB), writes=(u2,))
        k.do("vector", lambda e: e.tensor_add(
            srow.ap[R1], u1.ap[R1], u2.ap[R1]), reads=(u1, u2),
            writes=(srow,))
        k.do("vector", lambda e: e.reciprocal(rec.ap[R1], srow.ap[R1]),
             reads=(srow,), writes=(rec,))
        k.do("vector", lambda e: e.tensor_mul(
            gA.ap[R1], fA.ap[R1], rec.ap[R1]), reads=(fA, rec), writes=(gA,))
        k.do("vector", lambda e: e.tensor_mul(
            gB.ap[R1], fB.ap[R1], rec.ap[R1]), reads=(fB, rec), writes=(gB,))
        k.do("tensor", lambda e: e.matmul(
            psA.ap[:, 0:512], c.ones_f32.ap[0:1, :], gA.ap[0:1, :],
            start=True, stop=True, skip_group_check=True),
            reads=(c.ones_f32, gA), writes=(psA,))
        k.do("tensor", lambda e: e.matmul(
            psA.ap[:, 512:1024], c.ones_f32.ap[0:1, :], gB.ap[0:1, :],
            start=True, stop=True, skip_group_check=True),
            reads=(c.ones_f32, gB), writes=(psA,))
        for dc in range(8):
            sl = slice(dc * 512, (dc + 1) * 512)
            k.do("vector", lambda e, sl=sl: e.tensor_mul(
                ut.ap, oat.ap[:, sl], psA.ap[:, 0:512]),
                reads=(oat, psA), writes=(ut,))
            k.do("vector", lambda e, sl=sl: e.tensor_mul(
                vt.ap, obt.ap[:, sl], psA.ap[:, 512:1024]),
                reads=(obt, psA), writes=(vt,))
            k.do("vector", lambda e: e.tensor_add(ut.ap, ut.ap, vt.ap),
                 reads=(ut, vt), writes=(ut,))
            k.do("vector", lambda e, sl=sl: e.tensor_add(
                ut.ap, ut.ap, xTt.ap[:, sl]), reads=(ut, xTt), writes=(ut,))
            k.do("vector", lambda e, sl=sl: e.tensor_add(
                x1t.ap[:, sl], ut.ap, moet.ap[:, sl]),
                reads=(ut, moet), writes=(x1t,))
        k.do("gpsimd", lambda e, t0=t0: e.dma_start(
            out=scr["sX1"].ap.rearrange("(m d) t -> d m t", d=128)[
                :, :, t0:t0 + 512],
            in_=x1t.ap.rearrange("p (m t) -> p m t", m=8)),
            reads=(x1t,), writes=(scr["sX1"],), dma=True)
    A.reset()



def emit_p2a(k, A, inp, c, scr, carry_st):
    A.mark()
    psA, psB, psT, psO = c.psA, c.psB, c.psT, c.psO
    x1b = A.alloc("ga_x1", 8192)
    xn1b = A.alloc("ga_xn1", 4096, BF16)
    xinb = A.alloc("ga_xin", 16384, BF16)
    xstat = [A.alloc(f"ga_xs{i}", 256, BF16) for i in range(2)]
    sqstat = [A.alloc(f"ga_sq{i}", 256, BF16) for i in range(2)]
    srow = A.alloc("ga_srow", 512)
    sqrow = A.alloc("ga_sqrow", 512)
    rstdrow = A.alloc("ga_rstdrow", 512)
    brow = A.alloc("ga_brow", 512)
    lntmp = A.alloc("ga_lntmp", 512)
    NR = 4
    w_roll = [A.alloc(f"ga_w{i}", 512, BF16) for i in range(NR)]
    NWG = 4
    wg_roll = [A.alloc(f"ga_wg{i}", 2048, BF16) for i in range(NWG)]
    xsk_tmp = [A.alloc(f"ga_xsk{i}", 256, BF16) for i in range(2)]
    c_sb = A.alloc("ga_c", 512)
    z_sb = A.alloc("ga_z", 512)
    g_sb = A.alloc("ga_g", 512)
    t1_sb = A.alloc("ga_t1", 512)
    t2_sb = A.alloc("ga_t2", 512)
    gg_sb = A.alloc("ga_gg", 512)
    v_sb = A.alloc("ga_v", 512)
    h_sb = A.alloc("ga_h", 512)
    cp_sb = A.alloc("ga_cp", 512)
    state = A.alloc("ga_state", 32)
    cstate = A.alloc("ga_cstate", 32)
    zt = A.alloc("ga_zt", 512)
    negbg = A.alloc("ga_negbg", 64)
    bgc = A.alloc("ga_bgc", 64)
    b1c = A.alloc("ga_b1c", 32)
    b2c = A.alloc("ga_b2c", 32)

    for nm, b in (("bgc", bgc), ("b1c", b1c), ("b2c", b2c)):
        k.do("sync", lambda e, nm=nm, b=b: e.dma_start(
            out=b.ap, in_=inp[nm].ap), reads=(inp[nm],), writes=(b,),
            dma=True)
    k.do("vector", lambda e: e.tensor_scalar_mul(negbg.ap, bgc.ap, -1.0),
         reads=(bgc,), writes=(negbg,))
    k.do("vector", lambda e: e.memset(state.ap, 0.0), writes=(state,))
    k.do("vector", lambda e: e.memset(cstate.ap, 1.0), writes=(cstate,))
    k.do("vector", lambda e: e.memset(zt.ap, 0.0), writes=(zt,))

    def w_dma(wname, col, slot, nkc, roll):
        src = inp[wname].ap[col]
        k.do("gpsimd", lambda e: e.dma_start(out=roll[slot].ap, in_=src),
             reads=(inp[wname],), writes=(roll[slot],), dma=True)

    for tile in range(4):
        t0 = tile * 1024
        k.do("sync", lambda e, t0=t0: e.dma_start(
            out=x1b.ap.rearrange("p (kc t) -> p kc t", kc=8),
            in_=scr["sX1"].ap.rearrange("(kc d) t -> d kc t", d=128)[
                :, :, t0:t0 + 1024]),
            reads=(scr["sX1"],), writes=(x1b,), dma=True)
        for th in range(2):
            hoff = th * 512
            for kc in range(8):
                xs = xstat[kc % 2]
                sq = sqstat[kc % 2]
                k.do("vector", lambda e, kc=kc, hoff=hoff, xs=xs:
                     e.tensor_copy(
                         xs.ap, x1b.ap[:, kc * 1024 + hoff:
                                       kc * 1024 + hoff + 512]),
                     reads=(x1b,), writes=(xs,))
                k.do("scalar", lambda e, kc=kc, hoff=hoff, sq=sq:
                     e.activation(
                         sq.ap, x1b.ap[:, kc * 1024 + hoff:
                                       kc * 1024 + hoff + 512], AF.Square),
                     reads=(x1b,), writes=(sq,))
                k.do("tensor", lambda e, kc=kc, xs=xs: e.matmul(
                    psB.ap[0:1, 0:512], c.ones_bf.ap[:, 0:1], xs.ap,
                    start=(kc == 0), stop=(kc == 7), skip_group_check=True),
                    reads=(xs, c.ones_bf), writes=(psB,))
                k.do("tensor", lambda e, kc=kc, sq=sq: e.matmul(
                    psA.ap[0:1, 0:512], c.ones_bf.ap[:, 0:1], sq.ap,
                    start=(kc == 0), stop=(kc == 7), skip_group_check=True),
                    reads=(sq, c.ones_bf), writes=(psA,))
            R1 = (slice(0, 1), slice(None))
            k.do("vector", lambda e: e.tensor_copy(
                srow.ap[R1], psB.ap[0:1, 0:512]), reads=(psB,),
                writes=(srow,))
            k.do("vector", lambda e: e.tensor_copy(
                sqrow.ap[R1], psA.ap[0:1, 0:512]), reads=(psA,),
                writes=(sqrow,))
            k.do("scalar", lambda e: e.activation(
                srow.ap[R1], srow.ap[R1], AF.Copy, scale=1.0 / 1024.0),
                reads=(srow,), writes=(srow,))
            k.do("scalar", lambda e: e.activation(
                sqrow.ap[R1], sqrow.ap[R1], AF.Copy, scale=1.0 / 1024.0),
                reads=(sqrow,), writes=(sqrow,))
            k.do("scalar", lambda e: e.activation(
                lntmp.ap[R1], srow.ap[R1], AF.Square),
                reads=(srow,), writes=(lntmp,))
            k.do("vector", lambda e: e.tensor_sub(
                sqrow.ap[R1], sqrow.ap[R1], lntmp.ap[R1]),
                reads=(sqrow, lntmp), writes=(sqrow,))
            k.do("scalar", lambda e: e.activation(
                sqrow.ap[R1], sqrow.ap[R1], AF.Ln,
                bias=c.eps_col.ap[0:1, 0:1]),
                reads=(sqrow, c.eps_col), writes=(sqrow,))
            k.do("scalar", lambda e: e.activation(
                rstdrow.ap[R1], sqrow.ap[R1], AF.Exp, scale=-0.5),
                reads=(sqrow,), writes=(rstdrow,))
            k.do("vector", lambda e: e.scalar_tensor_tensor(
                brow.ap[R1], srow.ap[R1], -1.0, rstdrow.ap[R1],
                ALU.mult, ALU.mult), reads=(srow, rstdrow), writes=(brow,))
            k.do("tensor", lambda e: e.matmul(
                psA.ap[:, 0:512], c.ones_f32.ap[0:1, :], rstdrow.ap[R1],
                start=True, stop=True, skip_group_check=True),
                reads=(c.ones_f32, rstdrow), writes=(psA,))
            k.do("tensor", lambda e: e.matmul(
                psB.ap[:, 0:512], c.ones_f32.ap[0:1, :], brow.ap[R1],
                start=True, stop=True, skip_group_check=True),
                reads=(c.ones_f32, brow), writes=(psB,))
            for kc in range(8):
                k.do("vector", lambda e, kc=kc, hoff=hoff: e.tensor_mul(
                    lntmp.ap, x1b.ap[:, kc * 1024 + hoff:
                                     kc * 1024 + hoff + 512],
                    psA.ap[:, 0:512]), reads=(x1b, psA), writes=(lntmp,))
                k.do("vector", lambda e, kc=kc, hoff=hoff: e.tensor_add(
                    xn1b.ap[:, kc * 1024 + hoff:kc * 1024 + hoff + 512],
                    lntmp.ap, psB.ap[:, 0:512]),
                    reads=(lntmp, psB), writes=(xn1b,))
        k.do("gpsimd", lambda e, t0=t0: e.dma_start(
            out=scr["sXN1"].ap.rearrange("(kc d) t -> d kc t", d=128)[
                :, :, t0:t0 + 1024],
            in_=xn1b.ap.rearrange("p (kc t) -> p kc t", kc=8)),
            reads=(xn1b,), writes=(scr["sXN1"],), dma=True)
        if tile == 0:
            for pre in range(2):
                w_dma("w1_blk", pre, pre, 8, w_roll)
        for ec in range(32):
            wslot = ec % 2
            w2slot = 2 + ec % 2
            w_dma("w2_blk", ec, w2slot, 8, w_roll)
            for th in range(2):
                for kc in range(8):
                    k.do("tensor", lambda e, kc=kc, th=th, ec=ec,
                         w=w_roll[wslot]: e.matmul(
                             psO.ap[:, th * 512:th * 512 + 512],
                             w.ap[:, kc * 128:(kc + 1) * 128],
                             xn1b.ap[:, kc * 1024 + th * 512:
                                     kc * 1024 + th * 512 + 512],
                             start=(kc == 0), stop=(kc == 7),
                             skip_group_check=True),
                         reads=(w_roll[wslot], xn1b), writes=(psO,))
                k.do("scalar", lambda e, ec=ec, th=th: e.activation(
                    xinb.ap[:, ec * 1024 + th * 512:ec * 1024 + th * 512
                            + 512],
                    psO.ap[:, th * 512:th * 512 + 512], AF.Silu,
                    bias=b1c.ap[:, ec:ec + 1]),
                    reads=(psO, b1c), writes=(xinb,))
            if ec < 31:
                w_dma("w1_blk", ec + 1, (ec + 1) % 2, 8, w_roll)
            elif tile < 3:
                w_dma("w1_blk", 0, 0, 8, w_roll)
            for th in range(2):
                xsk = xsk_tmp[th]
                for kc in range(8):
                    k.do("tensor", lambda e, kc=kc, th=th, ec=ec,
                         w=w_roll[w2slot]: e.matmul(
                             psO.ap[:, 1024 + th * 512:1024 + th * 512
                                    + 512],
                             w.ap[:, kc * 128:(kc + 1) * 128],
                             xn1b.ap[:, kc * 1024 + th * 512:
                                     kc * 1024 + th * 512 + 512],
                             start=(kc == 0), stop=(kc == 7),
                             skip_group_check=True),
                         reads=(w_roll[w2slot], xn1b), writes=(psO,))
                k.do("scalar", lambda e, ec=ec, th=th, xsk=xsk: e.activation(
                    xsk.ap, psO.ap[:, 1024 + th * 512:1024 + th * 512 + 512],
                    AF.Silu, bias=b2c.ap[:, ec:ec + 1]),
                    reads=(psO, b2c), writes=(xsk,))
                k.do("gpsimd", lambda e, ec=ec, th=th, t0=t0, xsk=xsk:
                     e.dma_start(
                         out=scr["sXSK"].ap[ec * 128:(ec + 1) * 128,
                                            t0 + th * 512:t0 + th * 512
                                            + 512],
                         in_=xsk.ap), reads=(xsk,), writes=(scr["sXSK"],),
                     dma=True)
        if tile == 0:
            w_dma("wg_blk", 0, 0, 32, wg_roll)
            w_dma("wg_blk", 32, 1, 32, wg_roll)
        for sc in range(32):
            gslot, hslot = (sc % (NWG // 2)) * 2, (sc % (NWG // 2)) * 2 + 1
            if sc < 31:
                nxt = sc + 1
                w_dma("wg_blk", nxt, (nxt % (NWG // 2)) * 2, 32, wg_roll)
                w_dma("wg_blk", 32 + nxt, (nxt % (NWG // 2)) * 2 + 1, 32,
                      wg_roll)
            elif tile < 3:
                w_dma("wg_blk", 0, 0, 32, wg_roll)
                w_dma("wg_blk", 32, 1, 32, wg_roll)
            for th in range(2):
                for ec in range(32):
                    k.do("tensor", lambda e, ec=ec, th=th,
                         w=wg_roll[gslot]: e.matmul(
                             psA.ap[:, 0:512],
                             w.ap[:, ec * 128:(ec + 1) * 128],
                             xinb.ap[:, ec * 1024 + th * 512:
                                     ec * 1024 + th * 512 + 512],
                             start=(ec == 0), stop=(ec == 31),
                             skip_group_check=True),
                         reads=(wg_roll[gslot], xinb), writes=(psA,))
                for ec in range(32):
                    k.do("tensor", lambda e, ec=ec, th=th,
                         w=wg_roll[hslot]: e.matmul(
                             psB.ap[:, 0:512],
                             w.ap[:, ec * 128:(ec + 1) * 128],
                             xinb.ap[:, ec * 1024 + th * 512:
                                     ec * 1024 + th * 512 + 512],
                             start=(ec == 0), stop=(ec == 31),
                             skip_group_check=True),
                         reads=(wg_roll[hslot], xinb), writes=(psB,))
                k.do("scalar", lambda e, sc=sc: e.activation(
                    c_sb.ap, psA.ap[:, 0:512], AF.Sigmoid, scale=-1.0,
                    bias=negbg.ap[:, sc:sc + 1]),
                    reads=(psA, negbg), writes=(c_sb,))
                k.do("scalar", lambda e, sc=sc: e.activation(
                    z_sb.ap, psA.ap[:, 0:512], AF.Sigmoid,
                    bias=bgc.ap[:, sc:sc + 1]),
                    reads=(psA, bgc), writes=(z_sb,))
                k.do("scalar", lambda e, sc=sc: e.activation(
                    g_sb.ap, psB.ap[:, 0:512], AF.Relu,
                    bias=bgc.ap[:, 32 + sc:33 + sc]),
                    reads=(psB, bgc), writes=(g_sb,))
                k.do("scalar", lambda e, sc=sc: e.activation(
                    t1_sb.ap, psB.ap[:, 0:512], AF.Relu, scale=-1.0,
                    bias=negbg.ap[:, 32 + sc:33 + sc]),
                    reads=(psB, negbg), writes=(t1_sb,))
                k.do("scalar", lambda e: e.activation(
                    t2_sb.ap, t1_sb.ap, AF.Sigmoid, scale=-1.0),
                    reads=(t1_sb,), writes=(t2_sb,))
                k.do("vector", lambda e: e.tensor_add(
                    gg_sb.ap, g_sb.ap, t2_sb.ap),
                    reads=(g_sb, t2_sb), writes=(gg_sb,))
                k.do("vector", lambda e: e.tensor_mul(
                    v_sb.ap, z_sb.ap, gg_sb.ap),
                    reads=(z_sb, gg_sb), writes=(v_sb,))
                k.do("vector", lambda e, sc=sc: e.tensor_tensor_scan(
                    h_sb.ap, c_sb.ap, v_sb.ap, state.ap[:, sc:sc + 1],
                    ALU.mult, ALU.add),
                    reads=(c_sb, v_sb, state), writes=(h_sb,))
                k.do("vector", lambda e, sc=sc: e.tensor_copy(
                    state.ap[:, sc:sc + 1], h_sb.ap[:, 511:512]),
                    reads=(h_sb,), writes=(state,))
                k.do("vector", lambda e, sc=sc: e.tensor_tensor_scan(
                    cp_sb.ap, c_sb.ap, zt.ap,
                    cstate.ap[:, sc:sc + 1], ALU.mult, ALU.add),
                    reads=(c_sb, zt, cstate), writes=(cp_sb,))
                k.do("vector", lambda e, sc=sc: e.tensor_copy(
                    cstate.ap[:, sc:sc + 1], cp_sb.ap[:, 511:512]),
                    reads=(cp_sb,), writes=(cstate,))
                k.do("gpsimd", lambda e, sc=sc, th=th, t0=t0: e.dma_start(
                    out=scr["sVloc"].ap[sc * 128:(sc + 1) * 128,
                                        t0 + th * 512:t0 + th * 512 + 512],
                    in_=h_sb.ap), reads=(h_sb,), writes=(scr["sVloc"],),
                    dma=True)
                k.do("gpsimd", lambda e, sc=sc, th=th, t0=t0: e.dma_start(
                    out=scr["sCprod"].ap[sc * 128:(sc + 1) * 128,
                                         t0 + th * 512:t0 + th * 512 + 512],
                    in_=cp_sb.ap), reads=(cp_sb,), writes=(scr["sCprod"],),
                    dma=True)
    k.do("vector", lambda e: e.tensor_copy(carry_st.ap[:, 0:32], state.ap),
         reads=(state,), writes=(carry_st,))
    k.do("vector", lambda e: e.tensor_copy(
        carry_st.ap[:, 32:40], xn1b.ap[:, 1023:8192:1024]),
        reads=(xn1b,), writes=(carry_st,))
    if "carry_out" in scr:
        k.do("gpsimd", lambda e: e.dma_start(
            out=scr["carry_out"].ap, in_=carry_st.ap),
            reads=(carry_st,), writes=(scr["carry_out"],), dma=True)
    A.reset()
    return carry_st



def emit_p2b(k, A, inp, c, scr, carry_sb):
    A.mark()
    psA, psB, psO = c.psA, c.psB, c.psO
    sum_sb = A.alloc("gb_sum", 16384, BF16)
    vl = [A.alloc(f"gb_vl{i}", 512, BF16) for i in range(2)]
    cp = [A.alloc(f"gb_cp{i}", 512, BF16) for i in range(2)]
    xs = [A.alloc(f"gb_xs{i}", 512, BF16) for i in range(2)]
    hf = A.alloc("gb_hf", 1024)
    NW = 2
    wo_roll = [A.alloc(f"gb_wo{i}", 2048, BF16) for i in range(NW)]
    mix_st = A.alloc("gb_mix", 8192)
    mix2_st = A.alloc("gb_mix2", 4096)
    boc = A.alloc("gb_boc", 8)
    k.do("sync", lambda e: e.dma_start(out=boc.ap, in_=inp["boc"].ap),
         reads=(inp["boc"],), writes=(boc,), dma=True)

    def wo_dma(dc, slot):
        src = inp["wo_blk"].ap[dc]
        q = "sync" if os.environ.get("DEVBF16_WO") else "gpsimd"
        k.do(q, lambda e: e.dma_start(out=wo_roll[slot].ap, in_=src),
             reads=(inp["wo_blk"],), writes=(wo_roll[slot],), dma=True)

    for tile in range(4):
        t0 = tile * 1024
        for sc in range(32):
            sl = sc % 2
            k.do("sync", lambda e, sc=sc, t0=t0, b=vl[sl]: e.dma_start(
                out=b.ap, in_=scr["sVloc"].ap[sc * 128:(sc + 1) * 128,
                                              t0:t0 + 1024]),
                reads=(scr["sVloc"],), writes=(vl[sl],), dma=True)
            k.do("sync", lambda e, sc=sc, t0=t0, b=cp[sl]: e.dma_start(
                out=b.ap, in_=scr["sCprod"].ap[sc * 128:(sc + 1) * 128,
                                               t0:t0 + 1024]),
                reads=(scr["sCprod"],), writes=(cp[sl],), dma=True)
            k.do("sync", lambda e, sc=sc, t0=t0, b=xs[sl]: e.dma_start(
                out=b.ap, in_=scr["sXSK"].ap[sc * 128:(sc + 1) * 128,
                                             t0:t0 + 1024]),
                reads=(scr["sXSK"],), writes=(xs[sl],), dma=True)
            k.do("vector", lambda e, sc=sc, sl=sl: e.scalar_tensor_tensor(
                hf.ap, cp[sl].ap, carry_sb.ap[:, sc:sc + 1], vl[sl].ap,
                ALU.mult, ALU.add),
                reads=(cp[sl], vl[sl], carry_sb), writes=(hf,))
            k.do("vector", lambda e, sc=sc, sl=sl: e.tensor_add(
                sum_sb.ap[:, sc * 1024:(sc + 1) * 1024], hf.ap, xs[sl].ap),
                reads=(hf, xs[sl]), writes=(sum_sb,))
        k.flush("vector", sum_sb, hf.ap[:, 0:1])
        if "dbg_sum" in scr:
            k.do("gpsimd", lambda e, t0=t0: e.dma_start(
                out=scr["dbg_sum"].ap[:, t0:t0 + 1024].rearrange(
                    "(ec d) t -> d ec t", d=128),
                in_=sum_sb.ap.rearrange("p (ec t) -> p ec t", ec=32)),
                reads=(sum_sb,), writes=(scr["dbg_sum"],), dma=True)
        if tile == 0:
            wo_dma(0, 0)
        for dc in range(8):
            w = wo_roll[dc % NW]
            if dc < 7:
                wo_dma(dc + 1, (dc + 1) % NW)
            elif tile < 3:
                wo_dma(0, 0)
            for th in range(2):
                pso_off = (th % 2) * 512
                for ec in range(32):
                    k.do("tensor", lambda e, ec=ec, th=th, w=w,
                         pso_off=pso_off: e.matmul(
                             psO.ap[:, pso_off:pso_off + 512],
                             w.ap[:, ec * 128:(ec + 1) * 128],
                             sum_sb.ap[:, ec * 1024 + th * 512:
                                       ec * 1024 + th * 512 + 512],
                             start=(ec == 0), stop=(ec == 31),
                             skip_group_check=True),
                         reads=(w, sum_sb), writes=(psO,))
                k.do("vector", lambda e, dc=dc, th=th, pso_off=pso_off:
                     e.tensor_single_scalar(
                         mix_st.ap[:, dc * 1024 + th * 512:
                                   dc * 1024 + th * 512 + 512],
                         psO.ap[:, pso_off:pso_off + 512],
                         boc.ap[:, dc:dc + 1], ALU.add),
                     reads=(psO, boc), writes=(mix_st,))
                if os.environ.get("DEVSER_P2B"):
                    k.barrier()
                if "dbg_mix2" in scr and th == 0:
                    k.do("vector", lambda e, dc=dc, pso_off=pso_off:
                         e.tensor_single_scalar(
                             mix2_st.ap[:, dc * 512:(dc + 1) * 512],
                             psO.ap[:, pso_off:pso_off + 512],
                             boc.ap[:, dc:dc + 1], ALU.add),
                         reads=(psO, boc), writes=(mix2_st,))
        k.do("gpsimd", lambda e, t0=t0: e.dma_start(
            out=scr["sMix"].ap.rearrange("(dc d) t -> d dc t", d=128)[
                :, :, t0:t0 + 1024],
            in_=mix_st.ap.rearrange("p (dc t) -> p dc t", dc=8)),
            reads=(mix_st,), writes=(scr["sMix"],), dma=True)
        if "dbg_mix2" in scr:
            k.do("gpsimd", lambda e, t0=t0: e.dma_start(
                out=scr["dbg_mix2"].ap.rearrange("(dc d) t -> d dc t",
                                                 d=128)[:, :,
                                                        t0 // 2:t0 // 2 + 512],
                in_=mix2_st.ap.rearrange("p (dc t) -> p dc t", dc=8)),
                reads=(mix2_st,), writes=(scr["dbg_mix2"],), dma=True)
    A.reset()



def emit_p3(k, A, inp, c, scr, carry_sb):
    A.mark()
    psA, psB, psO = c.psA, c.psB, c.psO
    wr_bf = A.alloc("ff_wr", 4096, BF16)
    wk_bf = A.alloc("ff_wk", 4096, BF16)
    wv_bf = A.alloc("ff_wv", 4096, BF16)
    for nm, b in (("wrT", wr_bf), ("wkkT", wk_bf), ("wvvT", wv_bf)):
        src = inp[nm].ap.rearrange("(kc p) h -> p kc h", p=128)
        dst = b.ap.rearrange("p (kc h) -> p kc h", kc=8)
        k.do("gpsimd", lambda e, src=src, dst=dst: e.dma_start(
            out=dst, in_=src), reads=(inp[nm],), writes=(b,), dma=True)
    mur = A.alloc("ff_mur", 8)
    muk = A.alloc("ff_muk", 8)
    omr = A.alloc("ff_omr", 8)
    omk = A.alloc("ff_omk", 8)
    for nm, b in (("murc", mur), ("mukc", muk)):
        k.do("sync", lambda e, nm=nm, b=b: e.dma_start(
            out=b.ap, in_=inp[nm].ap), reads=(inp[nm],), writes=(b,),
            dma=True)
    k.do("vector", lambda e: e.tensor_scalar(
        omr.ap, mur.ap, -1.0, 1.0, ALU.mult, ALU.add),
        reads=(mur,), writes=(omr,))
    k.do("vector", lambda e: e.tensor_scalar(
        omk.ap, muk.ap, -1.0, 1.0, ALU.mult, ALU.add),
        reads=(muk,), writes=(omk,))
    xn1s = A.alloc("ff_xn1", 2056, BF16)
    CW = 514
    x1t = A.alloc("ff_x1", 4096)
    mixt = A.alloc("ff_mix", 4096)
    xr_sb = A.alloc("ff_xr", 2048, BF16)
    xk_sb = A.alloc("ff_xk", 2048, BF16)
    r_sb = A.alloc("ff_r", 4096)
    kact = A.alloc("ff_kact", 2048, BF16)
    ktmp = A.alloc("ff_ktmp", 512)
    t1 = A.alloc("ff_t1", 512)
    ffnc = A.alloc("ff_ffnc", 512)
    ut = A.alloc("ff_ut", 512)
    outt = A.alloc("ff_out", 4096)
    out_deps = []

    for tile in range(8):
        t0 = tile * 512
        if tile == 0:
            k.do("sync", lambda e: e.dma_start(
                out=xn1s.ap.rearrange("p (kc t) -> p kc t", kc=8)[
                    :, :, 1:513],
                in_=scr["sXN1"].ap.rearrange("(kc d) t -> d kc t", d=128)[
                    :, :, 0:512]),
                reads=(scr["sXN1"],), writes=(xn1s,), dma=True)
            k.do("vector", lambda e: e.tensor_copy(
                xn1s.ap[:, 0:8 * CW:CW], carry_sb.ap[:, 32:40]),
                reads=(carry_sb,), writes=(xn1s,))
        else:
            k.do("sync", lambda e, t0=t0: e.dma_start(
                out=xn1s.ap.rearrange("p (kc t) -> p kc t", kc=8)[
                    :, :, 0:513],
                in_=scr["sXN1"].ap.rearrange("(kc d) t -> d kc t", d=128)[
                    :, :, t0 - 1:t0 + 512]),
                reads=(scr["sXN1"],), writes=(xn1s,), dma=True)
        k.do("sync", lambda e, t0=t0: e.dma_start(
            out=x1t.ap.rearrange("p (m t) -> p m t", m=8),
            in_=scr["sX1"].ap.rearrange("(m d) t -> d m t", d=128)[
                :, :, t0:t0 + 512]),
            reads=(scr["sX1"],), writes=(x1t,), dma=True)
        k.do("sync", lambda e, t0=t0: e.dma_start(
            out=mixt.ap.rearrange("p (m t) -> p m t", m=8),
            in_=scr["sMix"].ap.rearrange("(m d) t -> d m t", d=128)[
                :, :, t0:t0 + 512]),
            reads=(scr["sMix"],), writes=(mixt,), dma=True)
        for kc in range(8):
            prevv = xn1s.ap[:, kc * CW:kc * CW + 512]
            curr = xn1s.ap[:, kc * CW + 1:kc * CW + 513]
            for mu, om, dst in ((mur, omr, xr_sb), (muk, omk, xk_sb)):
                k.do("vector", lambda e, kc=kc, om=om, prevv=prevv:
                     e.tensor_single_scalar(
                         t1.ap, prevv, om.ap[:, kc:kc + 1], ALU.mult),
                     reads=(xn1s, om), writes=(t1,))
                k.do("vector", lambda e, kc=kc, mu=mu, curr=curr, dst=dst:
                     e.scalar_tensor_tensor(
                         dst.ap[:, kc * 512:(kc + 1) * 512], curr,
                         mu.ap[:, kc:kc + 1], t1.ap, ALU.mult, ALU.add),
                     reads=(xn1s, mu, t1), writes=(dst,))
        for dc in range(8):
            off = 0 if dc % 2 == 0 else 512
            for kc in range(8):
                k.do("tensor", lambda e, kc=kc, dc=dc, off=off: e.matmul(
                    psA.ap[:, off:off + 512],
                    wr_bf.ap[:, kc * 1024 + dc * 128:
                             kc * 1024 + (dc + 1) * 128],
                    xr_sb.ap[:, kc * 512:(kc + 1) * 512],
                    start=(kc == 0), stop=(kc == 7), skip_group_check=True),
                    reads=(wr_bf, xr_sb), writes=(psA,))
            k.do("scalar", lambda e, dc=dc, off=off: e.activation(
                r_sb.ap[:, dc * 512:(dc + 1) * 512], psA.ap[:, off:off + 512],
                AF.Sigmoid), reads=(psA,), writes=(r_sb,))
            for kc in range(8):
                k.do("tensor", lambda e, kc=kc, dc=dc: e.matmul(
                    psB.ap[:, 0:512],
                    wk_bf.ap[:, kc * 1024 + dc * 128:
                             kc * 1024 + (dc + 1) * 128],
                    xk_sb.ap[:, kc * 512:(kc + 1) * 512],
                    start=(kc == 0), stop=(kc == 7), skip_group_check=True),
                    reads=(wk_bf, xk_sb), writes=(psB,))
            k.do("scalar", lambda e: e.activation(
                ktmp.ap, psB.ap[:, 0:512], AF.Relu),
                reads=(psB,), writes=(ktmp,))
            k.do("scalar", lambda e, dc=dc: e.activation(
                kact.ap[:, dc * 512:(dc + 1) * 512], ktmp.ap, AF.Square),
                reads=(ktmp,), writes=(kact,))
        for dc in range(8):
            slot = dc % 4
            for kc in range(8):
                k.do("tensor", lambda e, kc=kc, dc=dc, slot=slot: e.matmul(
                    psO.ap[:, slot * 512:(slot + 1) * 512],
                    wv_bf.ap[:, kc * 1024 + dc * 128:
                             kc * 1024 + (dc + 1) * 128],
                    kact.ap[:, kc * 512:(kc + 1) * 512],
                    start=(kc == 0), stop=(kc == 7), skip_group_check=True),
                    reads=(wv_bf, kact), writes=(psO,))
            k.do("vector", lambda e, dc=dc, slot=slot: e.tensor_mul(
                ffnc.ap, psO.ap[:, slot * 512:(slot + 1) * 512],
                r_sb.ap[:, dc * 512:(dc + 1) * 512]),
                reads=(psO, r_sb), writes=(ffnc,))
            k.do("vector", lambda e, dc=dc: e.tensor_add(
                ut.ap, x1t.ap[:, dc * 512:(dc + 1) * 512],
                mixt.ap[:, dc * 512:(dc + 1) * 512]),
                reads=(x1t, mixt), writes=(ut,))
            k.do("vector", lambda e, dc=dc: e.tensor_add(
                outt.ap[:, dc * 512:(dc + 1) * 512], ut.ap, ffnc.ap),
                reads=(ut, ffnc), writes=(outt,))
        dep = k.do("gpsimd", lambda e, t0=t0: e.dma_start(
            out=scr["outT"].ap.rearrange("(m d) t -> d m t", d=128)[
                :, :, t0:t0 + 512],
            in_=outt.ap.rearrange("p (m t) -> p m t", m=8)),
            reads=(outt,), writes=(scr["outT"],), dma=True)
        out_deps.append(dep)
    A.reset()
    return out_deps



def emit_exchange(k, A, c, scr, carry_st, carry_sb):
    zt40 = A.alloc("xc_z40", 40, F32)
    k.do("vector", lambda e: e.memset(zt40.ap, 0.0), writes=(zt40,))
    for blk_i in range(3):
        k.do("gpsimd", lambda e, blk_i=blk_i: e.dma_start(
            out=scr["cc_in"].ap[blk_i * 128:(blk_i + 1) * 128, :],
            in_=zt40.ap), reads=(zt40,), writes=(scr["cc_in"],), dma=True)

    def own_rows(e):
        par = e.compute_val(e.partition_id() % 2)
        return scr["cc_in"].ap[bass.ds(par * 128 + 128, 128), :]
    k.do("gpsimd", lambda e: e.dma_start(out=own_rows(e), in_=carry_st.ap),
         reads=(carry_st,), writes=(scr["cc_in"],), dma=True)
    k.do("gpsimd", lambda e: e.collective_compute(
        "AllReduce", ALU.add,
        replica_groups=[[0, 1], [2, 3], [4, 5], [6, 7]],
        ins=[scr["cc_in"].ap], outs=[scr["cc_out"].ap]),
        reads=(scr["cc_in"],), writes=(scr["cc_out"],), dma=True)

    def my_rows(e):
        par = e.compute_val(e.partition_id() % 2)
        return scr["cc_out"].ap[bass.ds(par * 128, 128), :]
    k.do("sync", lambda e: e.dma_start(out=carry_sb.ap, in_=my_rows(e)),
         reads=(scr["cc_out"],), writes=(carry_sb,), dma=True)



ARENA_W = 52900

SCRATCH_SPECS = {
    "sQT": ([D, HALF], BF16), "sKT": ([D, S], BF16), "sV": ([S, D], BF16),
    "sXT": ([D, HALF], F32), "sXN": ([D, HALF], BF16),
    "sCT": ([8, HALF], BF16), "sMoE": ([D, HALF], F32),
    "sOA": ([D, HALF], F32), "sOB": ([D, HALF], F32),
    "sRA": ([2, HALF], F32), "sRB": ([2, HALF], F32),
    "sX1": ([D, HALF], F32), "sXN1": ([D, HALF], BF16),
    "sXSK": ([E1, HALF], BF16), "sVloc": ([E1, HALF], BF16),
    "sCprod": ([E1, HALF], BF16), "carry_out": ([128, 40], F32),
    "carry_in": ([128, 40], F32), "sMix": ([D, HALF], F32),
    "cc_in": ([384, 40], F32), "cc_out": ([384, 40], F32),
    "dbg_sum": ([E1, HALF], BF16), "dbg_mix2": ([D, HALF // 2], F32),
    "outT": ([D, HALF], F32),
}

LAUNCH_DEFS = {
    4: dict(inputs=L1_INPUTS | L2_INPUTS,
            scr_in=set(),
            scr_out={"sXN1", "sXSK", "sVloc", "sCprod", "carry_out", "sX1"},
            scr_int={"sQT", "sKT", "sV", "sXT", "sXN", "sCT", "sMoE",
                     "sOA", "sOB", "sRA", "sRB"}),
    0: dict(inputs=L1_INPUTS | L2_INPUTS | L3_INPUTS,
            scr_in=set(),
            scr_out={"outT"},
            scr_int={"sQT", "sKT", "sV", "sXT", "sXN", "sCT", "sMoE",
                     "sOA", "sOB", "sRA", "sRB", "sX1", "sXN1", "sXSK",
                     "sVloc", "sCprod", "sMix", "cc_in", "cc_out"}),
    1: dict(inputs=L1_INPUTS,
            scr_in=set(),
            scr_out={"sX1"},
            scr_int={"sQT", "sKT", "sV", "sXT", "sXN", "sCT", "sMoE",
                     "sOA", "sOB", "sRA", "sRB"}),
    2: dict(inputs=L2_INPUTS,
            scr_in={"sX1"},
            scr_out={"sXN1", "sXSK", "sVloc", "sCprod", "carry_out"},
            scr_int=set()),
    3: dict(inputs=L3_INPUTS,
            scr_in={"sX1", "sXN1", "sXSK", "sVloc", "sCprod", "carry_in"},
            scr_out={"outT"},
            scr_int={"sMix"}),
}


def build_launch(which, dbg=(), phases=None):
    import contextlib
    nc = bass.Bass("TRN2", target_bir_lowering=False, debug=False)
    stack = contextlib.ExitStack()
    k = K(nc, stack)
    A = Arena(k, ARENA_W)
    ldef = LAUNCH_DEFS[which]
    inp = {}
    for name, shape, dt in INPUT_SPECS:
        if name in ldef["inputs"]:
            inp[name] = k.dram(name, shape, dt, "ExternalInput")
    scr = {}
    for name in (ldef["scr_in"] | ldef["scr_out"] | ldef["scr_int"]
                 | set(dbg)):
        shape, dt = SCRATCH_SPECS[name]
        if name in ldef["scr_in"]:
            kind = "ExternalInput"
        elif name in ldef["scr_out"] or name in dbg:
            kind = "ExternalOutput"
        else:
            kind = "Internal"
        scr[name] = k.dram(name, shape, dt, kind)
    c = emit_common(k, A, inp, want_sel8=(which in (0, 1, 4)))
    if which == 0:
        build_l0_consts(k, A, inp, c)
        emit_p0a(k, A, inp, c, scr)
        A.reset()
        emit_p0b(k, A, inp, c, scr)
        A.reset()
        emit_p1(k, A, inp, c, scr)
        emit_p1c(k, A, inp, c, scr)
        carry_st = A.alloc("carry_st", 40, F32)
        carry_sb = A.alloc("carry_sb", 40, F32)
        emit_p2a(k, A, inp, c, scr, carry_st)
        emit_exchange(k, A, c, scr, carry_st, carry_sb)
        emit_p2b(k, A, inp, c, scr, carry_sb)
        emit_p3(k, A, inp, c, scr, carry_sb)
    elif which == 4:
        build_l0_consts(k, A, inp, c)
        emit_p0a(k, A, inp, c, scr)
        A.reset()
        emit_p0b(k, A, inp, c, scr)
        A.reset()
        emit_p1(k, A, inp, c, scr)
        emit_p1c(k, A, inp, c, scr)
        carry_st = A.alloc("carry_st", 40, F32)
        emit_p2a(k, A, inp, c, scr, carry_st)
    elif which == 1:
        ph = phases or ("p0a", "p0b", "p1", "p1c")
        build_l0_consts(k, A, inp, c)
        if "p0a" in ph:
            emit_p0a(k, A, inp, c, scr)
        A.reset()
        if "p0b" in ph:
            emit_p0b(k, A, inp, c, scr)
        A.reset()
        if "p1" in ph:
            emit_p1(k, A, inp, c, scr)
        if "p1c" in ph:
            emit_p1c(k, A, inp, c, scr)
    elif which == 2:
        carry_st = A.alloc("carry_st", 40, F32)
        emit_p2a(k, A, inp, c, scr, carry_st)
    else:
        carry_sb = A.alloc("carry_sb", 40, F32)
        k.do("sync", lambda e: e.dma_start(
            out=carry_sb.ap, in_=scr["carry_in"].ap),
            reads=(scr["carry_in"],), writes=(carry_sb,), dma=True)
        ph = phases or ("p2b", "p3")
        if "p2b" in ph:
            emit_p2b(k, A, inp, c, scr, carry_sb)
        if "p3" in ph:
            emit_p3(k, A, inp, c, scr, carry_sb)
    k.barrier()
    k.finish([(s, v) for s, v in k.cnt.items() if v > 0])
    k.emit_block()
    stack.close()
    return nc


MODE = os.environ.get("KERNEL_MODE", "2L")


def kernel(**inputs):
    maps = prep_inputs(inputs)
    cores = list(range(NCORE))
    if MODE == "2L":
        nc1 = build_launch(4)
        names = sorted(LAUNCH_DEFS[4]["inputs"])
        m1 = [{n: maps[c][n] for n in names} for c in cores]
        r1 = run_bass_kernel_spmd(nc1, m1, core_ids=cores)
        nc2 = build_launch(3)
        zc = np.zeros((128, 40), np.float32)
        m3 = []
        for c in cores:
            carry = (np.asarray(r1.results[c - 1]["carry_out"])
                     if c % 2 == 1 else zc)
            m3.append(dict({n: maps[c][n] for n in sorted(L3_INPUTS)},
                           sX1=r1.results[c]["sX1"],
                           sXN1=r1.results[c]["sXN1"],
                           sXSK=r1.results[c]["sXSK"],
                           sVloc=r1.results[c]["sVloc"],
                           sCprod=r1.results[c]["sCprod"],
                           carry_in=carry))
        r3 = run_bass_kernel_spmd(nc2, m3, core_ids=cores)
        out = np.empty((B, S, D), np.float32)
        for c in cores:
            b, r = c // 2, c % 2
            out[b, r * HALF:(r + 1) * HALF] = \
                np.asarray(r3.results[c]["outT"]).T
        return out
    if MODE == "fused":
        nc = build_launch(0)
        names = sorted(LAUNCH_DEFS[0]["inputs"])
        m = [{n: maps[c][n] for n in names} for c in cores]
        res = run_bass_kernel_spmd(nc, m, core_ids=cores)
        out = np.empty((B, S, D), np.float32)
        for c in cores:
            b, r = c // 2, c % 2
            out[b, r * HALF:(r + 1) * HALF] = \
                np.asarray(res.results[c]["outT"]).T
        return out
    nc1 = build_launch(1)
    m1 = [{n: maps[c][n] for n in sorted(L1_INPUTS)} for c in cores]
    r1 = run_bass_kernel_spmd(nc1, m1, core_ids=cores)
    nc2 = build_launch(2)
    m2 = [dict({n: maps[c][n] for n in sorted(L2_INPUTS)},
               sX1=r1.results[c]["sX1"]) for c in cores]
    r2 = run_bass_kernel_spmd(nc2, m2, core_ids=cores)
    nc3 = build_launch(3)
    zc = np.zeros((128, 40), np.float32)
    m3 = []
    for c in cores:
        carry = r2.results[c - 1]["carry_out"] if c % 2 == 1 else zc
        m3.append(dict({n: maps[c][n] for n in sorted(L3_INPUTS)},
                       sX1=r1.results[c]["sX1"],
                       sXN1=r2.results[c]["sXN1"],
                       sXSK=r2.results[c]["sXSK"],
                       sVloc=r2.results[c]["sVloc"],
                       sCprod=r2.results[c]["sCprod"],
                       carry_in=carry))
    r3 = run_bass_kernel_spmd(nc3, m3, core_ids=cores)
    out = np.empty((B, S, D), np.float32)
    for c in cores:
        b, r = c // 2, c % 2
        out[b, r * HALF:(r + 1) * HALF] = \
            np.asarray(r3.results[c]["outT"]).T
    return out


# version 03
# speedup vs baseline: 1.0233x; 1.0233x over previous
import math
import os
import numpy as np
import ml_dtypes
import concourse.bass as bass
import concourse.mybir as mybir
from concourse.bass_utils import run_bass_kernel_spmd

F32 = mybir.dt.float32
BF16 = mybir.dt.bfloat16
AF = mybir.ActivationFunctionType
ALU = mybir.AluOpType
AX = mybir.AxisListType

B, S, D = 4, 8192, 1024
HALF = S // 2
NCORE = 8
EXP, IMOE, GUF = 8, 2048, 4096
E1, F1 = 4096, 8192
SCALE = 1.0 / math.sqrt(512.0)

ENGINES = ("sync", "scalar", "vector", "tensor", "gpsimd")


class Buf:
    __slots__ = ("name", "ap", "last_write", "readers")

    def __init__(self, name, ap):
        self.name = name
        self.ap = ap
        self.last_write = None
        self.readers = []


class K:

    NLANE = 8

    def __init__(self, nc, stack):
        self.nc = nc
        self.stack = stack
        self.plan = {e: [] for e in ENGINES}
        self.sem_handles = {
            e: stack.enter_context(nc.semaphore(f"sem_{e}")) for e in ENGINES}
        for q in ("sync", "scalar", "gpsimd", "vector"):
            for l in range(self.NLANE):
                nm = f"{q}_d{l}"
                self.sem_handles[nm] = stack.enter_context(nc.semaphore(nm))
        self.cnt = {s: 0 for s in self.sem_handles}
        self.dma_rr = {q: 0 for q in ("sync", "scalar", "gpsimd", "vector")}
        self.waited = {e: {} for e in ENGINES}

    def sbuf(self, name, shape, dtype):
        h = self.stack.enter_context(self.nc.sbuf_tensor(name, shape, dtype))
        return Buf(name, h[:, :])

    def psum(self, name, shape, dtype):
        h = self.stack.enter_context(self.nc.psum_tensor(name, shape, dtype))
        return Buf(name, h[:, :])

    def dram(self, name, shape, dtype, kind):
        return Buf(name, self.nc.dram_tensor(name, shape, dtype,
                                             kind=kind).ap())

    def do(self, eng, fn, reads=(), writes=(), dma=False):
        inc = 16 if dma else 1
        if dma:
            lane = self.dma_rr[eng]
            self.dma_rr[eng] = (lane + 1) % self.NLANE
            sig = f"{eng}_d{lane}"
        else:
            sig = eng
        deps = []
        for r in reads:
            if r.last_write is not None:
                deps.append(r.last_write)
        for w in writes:
            if w.last_write is not None:
                deps.append(w.last_write)
            deps.extend(w.readers)
        waited = self.waited[eng]
        need = {}
        for dsem, dval in deps:
            if dval > need.get(dsem, 0):
                need[dsem] = dval
        todo = []
        for dsem, dval in sorted(need.items()):
            if dsem == eng:
                pass
            if waited.get(dsem, 0) < dval:
                todo.append((dsem, dval))
                waited[dsem] = dval
        self.cnt[sig] += inc
        val = self.cnt[sig]
        handles = self.sem_handles

        def emit(e, todo=tuple(todo), fn=fn, inc=inc, sig=sig):
            for dsem, dval in todo:
                e.wait_ge(handles[dsem], dval)
            fn(e).then_inc(handles[sig], inc)
        self.plan[eng].append(emit)
        dep = (sig, val)
        for w in writes:
            w.last_write = dep
            w.readers = []
        for r in reads:
            r.readers.append(dep)
        return dep

    def flush(self, eng, buf, scratch_ap):
        dep = self.do(eng, lambda e: e.tensor_copy(scratch_ap, scratch_ap),
                      reads=(buf,), writes=())
        buf.last_write = dep
        return dep

    def barrier(self):
        handles = self.sem_handles
        snap = dict(self.cnt)
        for eng in ENGINES:
            todo = []
            for s, v in snap.items():
                if s == eng or v == 0:
                    continue
                if self.waited[eng].get(s, 0) < v:
                    todo.append((s, v))
                    self.waited[eng][s] = v
            if todo:
                def emit(e, todo=tuple(todo)):
                    for ds, dv in todo:
                        e.wait_ge(handles[ds], dv)
                self.plan[eng].append(emit)

    def finish(self, deps):
        handles = self.sem_handles

        def emit(e, deps=tuple(deps)):
            for dsem, dval in deps:
                e.wait_ge(handles[dsem], dval)
        self.plan["gpsimd"].append(emit)

    def emit_block(self):
        with self.nc.Block() as block:
            @block.sync
            def _(eng):
                for f in self.plan["sync"]:
                    f(eng)

            @block.scalar
            def _(eng):
                for f in self.plan["scalar"]:
                    f(eng)

            @block.vector
            def _(eng):
                for f in self.plan["vector"]:
                    f(eng)

            @block.tensor
            def _(eng):
                for f in self.plan["tensor"]:
                    f(eng)

            @block.gpsimd
            def _(eng):
                for f in self.plan["gpsimd"]:
                    f(eng)


class Arena:
    def __init__(self, k, words):
        self.k = k
        self.t = k.stack.enter_context(
            k.nc.sbuf_tensor("arena", [128, words], F32))
        self.words = words
        self.off = 0
        self.watermark = 0
        self.marks = []

    def alloc(self, name, words, dtype=F32, parts=128):
        w = int(words)
        assert self.off + w <= self.words, (
            f"arena overflow: {name} needs {w} at {self.off}/{self.words}")
        ap = self.t[0:parts, self.off:self.off + w]
        if dtype == BF16:
            ap = ap.bitcast(BF16)
        self.off += w
        self.watermark = max(self.watermark, self.off)
        return Buf(name, ap)

    def mark(self):
        self.marks.append(self.off)

    def reset(self):
        self.k.barrier()
        self.off = self.marks.pop()



def _rope_table():
    pos = np.arange(S, dtype=np.float32)
    inv = np.float32(10000.0) ** (
        np.arange(0, 512, 2, dtype=np.float32) / np.float32(512.0))
    radians = (pos[:, None] / inv[None, :]).astype(np.float32)
    sincos = np.stack(
        [np.sin(radians), np.cos(radians)], axis=-1).reshape(S, 512)
    return np.repeat(sincos, 2, axis=-1).astype(np.float32)


def _colchunks(v, nchunk):
    return np.ascontiguousarray(np.asarray(v).reshape(nchunk, 128).T)


def prep_inputs(inp):
    g = {k: np.asarray(v) for k, v in inp.items()}
    x = g["L0_attn_hidden_states"]
    tT = np.ascontiguousarray(_rope_table().T)
    eye_bf = np.eye(128, dtype=np.float32).astype(ml_dtypes.bfloat16)
    sel8 = np.zeros((8, 1024), dtype=np.float32)
    for e in range(8):
        sel8[e, 128 * e:128 * e + 128] = 1.0
    shared = {
        "wqT": np.ascontiguousarray(g["L0_attn_Wq"].T),
        "wkT": np.ascontiguousarray(g["L0_attn_Wk"].T),
        "wvT": np.ascontiguousarray(g["L0_attn_Wv"].T),
        "bq_row": np.ascontiguousarray(g["L0_attn_bq"].reshape(1, D)),
        "bk_row": np.ascontiguousarray(g["L0_attn_bk"].reshape(1, D)),
        "bv_row": np.ascontiguousarray(g["L0_attn_bv"].reshape(1, D)),
        "routerT": np.ascontiguousarray(g["L0_ffn_router_weight"].T),
        "rbias_row": np.ascontiguousarray(
            g["L0_ffn_router_bias"].reshape(1, EXP)),
        "wgu_blk": np.ascontiguousarray(
            g["L0_ffn_gate_up_proj"].transpose(0, 2, 1)
            .reshape(EXP, 8, 128, 32, 128).transpose(0, 3, 2, 1, 4)
            .reshape(EXP, 32, 128, 1024)),
        "gub_col": np.ascontiguousarray(
            g["L0_ffn_gate_up_proj_bias"].reshape(EXP, 32, 128)
            .transpose(0, 2, 1)),
        "dwn_blk": np.ascontiguousarray(
            g["L0_ffn_down_proj"].transpose(0, 2, 1)
            .reshape(EXP, 16, 128, 1024).transpose(0, 2, 1, 3)
            .reshape(EXP, 128, 16384)),
        "db8": np.ascontiguousarray(g["L0_ffn_down_proj_bias"]),
        "w1_blk": np.ascontiguousarray(
            g["L1_attn_W1"].T.reshape(8, 128, 32, 128).transpose(2, 1, 0, 3)
            .reshape(32, 128, 1024)),
        "w2_blk": np.ascontiguousarray(
            g["L1_attn_W2"].T.reshape(8, 128, 32, 128).transpose(2, 1, 0, 3)
            .reshape(32, 128, 1024)),
        "b1c": _colchunks(g["L1_attn_b1"], 32),
        "b2c": _colchunks(g["L1_attn_b2"], 32),
        "wg_blk": np.ascontiguousarray(
            g["L1_attn_Wg"].T.reshape(32, 128, 64, 128).transpose(2, 1, 0, 3)
            .reshape(64, 128, 4096)),
        "bgc": _colchunks(g["L1_attn_bg"], 64),
        "wo_blk": np.ascontiguousarray(
            g["L1_attn_Wo"].T.reshape(32, 128, 8, 128).transpose(2, 1, 0, 3)
            .reshape(8, 128, 4096)).astype(
                ml_dtypes.bfloat16 if os.environ.get("DEVBF16_WO")
                else np.float32),
        "boc": _colchunks(g["L1_attn_bo"], 8),
        "wrT": np.ascontiguousarray(g["L1_ffn_W_r"].T),
        "wkkT": np.ascontiguousarray(g["L1_ffn_W_k"].T),
        "wvvT": np.ascontiguousarray(g["L1_ffn_W_v"].T),
        "murc": _colchunks(g["L1_ffn_mu_r"], 8),
        "mukc": _colchunks(g["L1_ffn_mu_k"], 8),
        "eye_bf": eye_bf,
        "sel8": sel8,
    }
    maps = []
    for c in range(NCORE):
        b, r = c // 2, c % 2
        own = slice(r * HALF, (r + 1) * HALF)
        oth = slice((1 - r) * HALF, (2 - r) * HALF)
        m = dict(shared)
        m["x_own"] = np.ascontiguousarray(x[b, own])
        m["x_oth"] = np.ascontiguousarray(x[b, oth])
        m["tT_own"] = np.ascontiguousarray(tT[:, own])
        m["tT_oth"] = np.ascontiguousarray(tT[:, oth])
        maps.append(m)
    return maps


INPUT_SPECS = [
    ("x_own", [HALF, D], F32), ("x_oth", [HALF, D], F32),
    ("tT_own", [D, HALF], F32), ("tT_oth", [D, HALF], F32),
    ("wqT", [D, D], F32), ("wkT", [D, D], F32), ("wvT", [D, D], F32),
    ("bq_row", [1, D], F32), ("bk_row", [1, D], F32), ("bv_row", [1, D], F32),
    ("routerT", [D, EXP], F32), ("rbias_row", [1, EXP], F32),
    ("wgu_blk", [EXP, 32, 128, 1024], F32), ("gub_col", [EXP, 128, 32], F32),
    ("dwn_blk", [EXP, 128, 16384], F32), ("db8", [EXP, D], F32),
    ("w1_blk", [32, 128, 1024], F32), ("w2_blk", [32, 128, 1024], F32),
    ("b1c", [128, 32], F32), ("b2c", [128, 32], F32),
    ("wg_blk", [64, 128, 4096], F32), ("bgc", [128, 64], F32),
    ("wo_blk", [8, 128, 4096],
     BF16 if os.environ.get("DEVBF16_WO") else F32),
    ("boc", [128, 8], F32),
    ("wrT", [D, D], F32), ("wkkT", [D, D], F32), ("wvvT", [D, D], F32),
    ("murc", [128, 8], F32), ("mukc", [128, 8], F32),
    ("eye_bf", [128, 128], BF16), ("sel8", [8, 1024], F32),
]

L1_INPUTS = {"x_own", "x_oth", "tT_own", "tT_oth", "wqT", "wkT", "wvT",
             "bq_row", "bk_row", "bv_row", "routerT", "rbias_row", "wgu_blk",
             "gub_col", "dwn_blk", "db8", "eye_bf", "sel8"}
L2_INPUTS = {"w1_blk", "w2_blk", "b1c", "b2c", "wg_blk", "bgc", "eye_bf",
             "sel8"}
L3_INPUTS = {"wo_blk", "boc", "wrT", "wkkT", "wvvT", "murc", "mukc", "eye_bf",
             "sel8"}



class Ctx:
    pass


def emit_common(k, A, inp, want_sel8=True):
    c = Ctx()
    c.psA = k.psum("psA", [128, 1024], F32)
    c.psB = k.psum("psB", [128, 512], F32)
    c.psT = k.psum("psT", [128, 512], BF16)
    c.psO = k.psum("psO", [128, 2048], F32)

    c.eye_bf = A.alloc("eye_bf", 64, BF16)
    c.ones_bf = A.alloc("ones_bf", 64, BF16)
    c.ones_f32 = A.alloc("ones_f32", 128, F32)
    c.eps_col = A.alloc("eps_col", 1, F32)

    k.do("sync", lambda e: e.dma_start(out=c.eye_bf.ap, in_=inp["eye_bf"].ap),
         reads=(inp["eye_bf"],), writes=(c.eye_bf,), dma=True)
    k.do("vector", lambda e: e.memset(c.ones_bf.ap, 1.0), writes=(c.ones_bf,))
    k.do("vector", lambda e: e.memset(c.ones_f32.ap, 1.0),
         writes=(c.ones_f32,))
    k.do("vector", lambda e: e.memset(c.eps_col.ap, 1e-5),
         writes=(c.eps_col,))
    if want_sel8:
        c.sel8_bf = A.alloc("sel8_bf", 512, BF16)
        sel_st = A.alloc("sel_st", 1024, F32)
        k.do("sync", lambda e: e.dma_start(out=sel_st.ap[0:8, :],
                                           in_=inp["sel8"].ap),
             reads=(inp["sel8"],), writes=(sel_st,), dma=True)
        k.do("vector", lambda e: e.tensor_copy(c.sel8_bf.ap[0:8, :],
                                               sel_st.ap[0:8, :]),
             reads=(sel_st,), writes=(c.sel8_bf,))
        c.sel8_f32 = sel_st
    return c


def load_bf16_weight(k, A, inp, c, name, rows, cols):
    kc = rows // 128
    wbf = A.alloc(name + "_bf", kc * cols // 2, BF16)
    src = inp[name].ap.rearrange("(kc p) h -> p kc h", p=128)
    dst = wbf.ap.rearrange("p (kc h) -> p kc h", kc=kc)
    k.do("gpsimd", lambda e: e.dma_start(out=dst, in_=src),
         reads=(inp[name],), writes=(wbf,), dma=True)
    return wbf


def build_l0_consts(k, A, inp, c):
    A.mark()
    c.gub = A.alloc("gub", 256, F32)
    c.db8_bf = A.alloc("db8_bf", 512, BF16)
    A.mark()
    c.eye_f32 = A.alloc("eye_f32", 128, F32)
    c.wqT_bf = A.alloc("wqT_bf", 4096, BF16)
    c.wkT_bf = A.alloc("wkT_bf", 4096, BF16)
    c.wvT_bf = A.alloc("wvT_bf", 4096, BF16)
    c.bq_col = A.alloc("bq_col", 8, F32)
    c.bk_col = A.alloc("bk_col", 8, F32)
    c.bv_row_bf = A.alloc("bv_row_bf", 512, BF16)
    c.routerT_bf = A.alloc("routerT_bf", 32, BF16)
    c.rbias_row_bf = A.alloc("rbias_row_bf", 8, BF16)
    A.mark()
    stage = A.alloc("wstage", 8192, F32)
    rstage = A.alloc("rstage", 64, F32)
    rbst = A.alloc("rbst", 8, F32)
    bvraw = A.alloc("bv_raw", 1024, F32)
    dbst = A.alloc("dbst", 1024, F32)
    raws = {n: A.alloc(n + "_raw", 1024, F32) for n in ("bq_row", "bk_row")}
    flds = {n: A.alloc(n + "_fld", 1024, F32) for n in ("bq_row", "bk_row")}

    k.do("vector", lambda e: e.tensor_copy(c.eye_f32.ap, c.eye_bf.ap),
         reads=(c.eye_bf,), writes=(c.eye_f32,))
    k.do("sync", lambda e: e.dma_start(
        out=c.gub.ap.rearrange("p (ex n) -> p ex n", ex=8),
        in_=inp["gub_col"].ap.rearrange("ex p n -> p ex n")), reads=(inp["gub_col"],), writes=(c.gub,),
        dma=True)
    k.do("sync", lambda e: e.dma_start(out=dbst.ap[0:8, :],
                                       in_=inp["db8"].ap),
         reads=(inp["db8"],), writes=(dbst,), dma=True)
    k.do("vector", lambda e: e.tensor_copy(c.db8_bf.ap[0:8, :],
                                           dbst.ap[0:8, :]),
         reads=(dbst,), writes=(c.db8_bf,))
    for wname, fold in (("wqT", True), ("wkT", True), ("wvT", False)):
        wbf = getattr(c, wname + "_bf")
        src_ = inp[wname].ap.rearrange("(kc p) h -> p kc h", p=128)
        dst = stage.ap.rearrange("p (kc h) -> p kc h", kc=8)
        k.do("sync", lambda e, s=src_, d=dst: e.dma_start(out=d, in_=s),
             reads=(inp[wname],), writes=(stage,), dma=True)
        if fold:
            k.do("vector", lambda e, sv=stage.ap, bv=wbf.ap: e.tensor_sub(
                bv[:, 0:8192:2], sv[:, 0:8192:2], sv[:, 1:8192:2]),
                reads=(stage,), writes=(wbf,))
            k.do("vector", lambda e, sv=stage.ap, bv=wbf.ap: e.tensor_add(
                bv[:, 1:8192:2], sv[:, 0:8192:2], sv[:, 1:8192:2]),
                reads=(stage,), writes=(wbf,))
        else:
            k.do("vector", lambda e, sv=stage.ap, bv=wbf.ap: e.tensor_copy(
                bv, sv), reads=(stage,), writes=(wbf,))
    for bname, tgt in (("bq_row", c.bq_col), ("bk_row", c.bk_col)):
        raw, fld = raws[bname], flds[bname]
        k.do("sync", lambda e, raw=raw, s=inp[bname]: e.dma_start(
            out=raw.ap[0:1, :], in_=s.ap), reads=(inp[bname],),
            writes=(raw,), dma=True)
        k.do("vector", lambda e, raw=raw, fld=fld: e.tensor_sub(
            fld.ap[0:1, 0:1024:2], raw.ap[0:1, 0:1024:2],
            raw.ap[0:1, 1:1024:2]), reads=(raw,), writes=(fld,))
        k.do("vector", lambda e, raw=raw, fld=fld: e.tensor_add(
            fld.ap[0:1, 1:1024:2], raw.ap[0:1, 0:1024:2],
            raw.ap[0:1, 1:1024:2]), reads=(raw,), writes=(fld,))
        for m in range(8):
            k.do("tensor", lambda e, fld=fld, m=m: e.matmul(
                c.psB.ap[:, m:m + 1], fld.ap[0:1, m * 128:(m + 1) * 128],
                c.ones_f32.ap[0:1, 0:1], start=True, stop=True,
                skip_group_check=True),
                reads=(fld, c.ones_f32), writes=(c.psB,))
        k.do("vector", lambda e, tgt=tgt: e.tensor_copy(
            tgt.ap, c.psB.ap[:, 0:8]), reads=(c.psB,), writes=(tgt,))
    k.do("sync", lambda e: e.dma_start(out=bvraw.ap[0:1, :],
                                       in_=inp["bv_row"].ap),
         reads=(inp["bv_row"],), writes=(bvraw,), dma=True)
    k.do("vector", lambda e: e.tensor_copy(c.bv_row_bf.ap[0:1, :],
                                           bvraw.ap[0:1, :]),
         reads=(bvraw,), writes=(c.bv_row_bf,))
    k.do("sync", lambda e: e.dma_start(
        out=rstage.ap.rearrange("p (kc h) -> p kc h", kc=8),
        in_=inp["routerT"].ap.rearrange("(kc p) h -> p kc h", p=128)),
        reads=(inp["routerT"],), writes=(rstage,), dma=True)
    k.do("vector", lambda e: e.tensor_copy(c.routerT_bf.ap, rstage.ap),
         reads=(rstage,), writes=(c.routerT_bf,))
    k.do("sync", lambda e: e.dma_start(out=rbst.ap[0:1, 0:8],
                                       in_=inp["rbias_row"].ap),
         reads=(inp["rbias_row"],), writes=(rbst,), dma=True)
    k.do("vector", lambda e: e.tensor_copy(c.rbias_row_bf.ap[0:1, 0:8],
                                           rbst.ap[0:1, 0:8]),
         reads=(rbst,), writes=(c.rbias_row_bf,))
    A.reset()



def emit_p0a(k, A, inp, c, scr):
    A.mark()
    psA, psB, psT, psO = c.psA, c.psB, c.psT, c.psO
    xtok = [A.alloc(f"xtok{i}", 4096) for i in (0, 1)]
    ttb = [A.alloc(f"ttb{i}", 4096) for i in (0, 1)]
    xT = A.alloc("xT", 4096)
    sq = A.alloc("p0sq", 4096)
    sums = A.alloc("p0sums", 4)
    sqs = A.alloc("p0sqs", 4)
    rows_sum = A.alloc("rows_sum", 128)
    rows_sq = A.alloc("rows_sq", 128)
    rows_m = A.alloc("rows_m", 128)
    rows_msq = A.alloc("rows_msq", 128)
    rows_m2 = A.alloc("rows_m2", 128)
    rows_var = A.alloc("rows_var", 128)
    rows_rstd = A.alloc("rows_rstd", 128)
    rows_B = A.alloc("rows_B", 128)
    xn = A.alloc("p0xn", 2048, BF16)
    xnf = A.alloc("p0xnf", 512)
    qtmp = A.alloc("p0qtmp", 512)
    qk_stage = A.alloc("qk_stage", 2048, BF16)
    v_stage = A.alloc("v_stage", 2048, BF16)
    lg = A.alloc("p0lg", 8)
    srt = [A.alloc(f"p0srt{i}", 8) for i in (0, 1)]
    negm = A.alloc("p0negm", 1)
    eexp = A.alloc("p0eexp", 8)
    maskb = A.alloc("p0mask", 8)
    em = A.alloc("p0em", 8)
    ssum = A.alloc("p0ssum", 1)
    rec = A.alloc("p0rec", 1)
    comb = A.alloc("p0comb", 8)
    comb_bf = A.alloc("p0comb_bf", 4, BF16)
    combT = A.alloc("p0combT", 256, BF16)

    for tt in range(16):
        own = tt < 8
        lt = tt % 8
        t0 = lt * 512
        gk = t0 if own else HALF + t0
        xin = inp["x_own" if own else "x_oth"]
        tin = inp["tT_own" if own else "tT_oth"]
        buf = xtok[tt % 2]
        tb = ttb[tt % 2]
        k.do("sync", lambda e, xin=xin, t0=t0, buf=buf: e.dma_start(
            out=buf.ap.rearrange("p (a d) -> p a d", a=4),
            in_=xin.ap[t0:t0 + 512, :].rearrange("(a p) d -> p a d", p=128)),
            reads=(xin,), writes=(buf,), dma=True)
        k.do("sync", lambda e, tin=tin, t0=t0, tb=tb: e.dma_start(
            out=tb.ap.rearrange("p (kc t) -> p kc t", kc=8),
            in_=tin.ap[:, t0:t0 + 512].rearrange("(kc p) t -> p kc t", p=128)),
            reads=(tin,), writes=(tb,), dma=True)
        for a in range(4):
            for kc in range(8):
                k.do("tensor", lambda e, a=a, kc=kc, buf=buf: e.transpose(
                    psA.ap[:, kc * 128:(kc + 1) * 128],
                    buf.ap[:, a * 1024 + kc * 128:a * 1024 + (kc + 1) * 128],
                    c.eye_f32.ap),
                    reads=(buf, c.eye_f32), writes=(psA,))
            k.do("vector", lambda e, a=a: e.tensor_copy(
                xT.ap.rearrange("p (kc t) -> p kc t", kc=8)[
                    :, :, a * 128:(a + 1) * 128],
                psA.ap.rearrange("p (kc t) -> p kc t", kc=8)),
                reads=(psA,), writes=(xT,))
        if own:
            k.do("gpsimd", lambda e, t0=t0: e.dma_start(
                out=scr["sXT"].ap.rearrange("(kc d) t -> d kc t", d=128)[
                    :, :, t0:t0 + 512],
                in_=xT.ap.rearrange("p (kc t) -> p kc t", kc=8)),
                reads=(xT,), writes=(scr["sXT"],), dma=True)
        k.do("scalar", lambda e, buf=buf: e.activation(
            sq.ap, buf.ap, AF.Square), reads=(buf,), writes=(sq,))
        k.do("vector", lambda e, buf=buf: e.reduce_sum(
            sums.ap, buf.ap.rearrange("p (a d) -> p a d", a=4), axis=AX.X),
            reads=(buf,), writes=(sums,))
        k.do("vector", lambda e: e.reduce_sum(
            sqs.ap, sq.ap.rearrange("p (a d) -> p a d", a=4), axis=AX.X),
            reads=(sq,), writes=(sqs,))
        k.do("tensor", lambda e: e.transpose(
            psB.ap[0:4, 0:128], sums.ap, c.eye_f32.ap),
            reads=(sums, c.eye_f32), writes=(psB,))
        k.do("vector", lambda e: e.tensor_copy(
            rows_sum.ap[0:4, :], psB.ap[0:4, 0:128]),
            reads=(psB,), writes=(rows_sum,))
        k.do("tensor", lambda e: e.transpose(
            psB.ap[0:4, 0:128], sqs.ap, c.eye_f32.ap),
            reads=(sqs, c.eye_f32), writes=(psB,))
        k.do("vector", lambda e: e.tensor_copy(
            rows_sq.ap[0:4, :], psB.ap[0:4, 0:128]),
            reads=(psB,), writes=(rows_sq,))
        k.do("scalar", lambda e: e.activation(
            rows_m.ap[0:4, :], rows_sum.ap[0:4, :], AF.Copy,
            scale=1.0 / 1024.0), reads=(rows_sum,), writes=(rows_m,))
        k.do("scalar", lambda e: e.activation(
            rows_msq.ap[0:4, :], rows_sq.ap[0:4, :], AF.Copy,
            scale=1.0 / 1024.0), reads=(rows_sq,), writes=(rows_msq,))
        k.do("scalar", lambda e: e.activation(
            rows_m2.ap[0:4, :], rows_m.ap[0:4, :], AF.Square),
            reads=(rows_m,), writes=(rows_m2,))
        k.do("vector", lambda e: e.tensor_sub(
            rows_var.ap[0:4, :], rows_msq.ap[0:4, :], rows_m2.ap[0:4, :]),
            reads=(rows_msq, rows_m2), writes=(rows_var,))
        k.do("scalar", lambda e: e.activation(
            rows_var.ap[0:4, :], rows_var.ap[0:4, :], AF.Ln,
            bias=c.eps_col.ap[0:4, 0:1]),
            reads=(rows_var, c.eps_col), writes=(rows_var,))
        k.do("scalar", lambda e: e.activation(
            rows_rstd.ap[0:4, :], rows_var.ap[0:4, :], AF.Exp, scale=-0.5),
            reads=(rows_var,), writes=(rows_rstd,))
        k.do("vector", lambda e: e.scalar_tensor_tensor(
            rows_B.ap[0:4, :], rows_m.ap[0:4, :], -1.0, rows_rstd.ap[0:4, :],
            ALU.mult, ALU.mult), reads=(rows_m, rows_rstd), writes=(rows_B,))
        for a in range(4):
            k.do("tensor", lambda e, a=a: e.matmul(
                psA.ap[:, a * 128:(a + 1) * 128],
                c.sel8_f32.ap[0:4, a * 128:(a + 1) * 128],
                rows_rstd.ap[0:4, :], start=True, stop=True),
                reads=(c.sel8_f32, rows_rstd), writes=(psA,))
            k.do("tensor", lambda e, a=a: e.matmul(
                psA.ap[:, 512 + a * 128:512 + (a + 1) * 128],
                c.sel8_f32.ap[0:4, a * 128:(a + 1) * 128],
                rows_B.ap[0:4, :], start=True, stop=True),
                reads=(c.sel8_f32, rows_B), writes=(psA,))
        for kc in range(8):
            k.do("vector", lambda e, kc=kc: e.tensor_mul(
                xnf.ap, xT.ap[:, kc * 512:(kc + 1) * 512], psA.ap[:, 0:512]),
                reads=(xT, psA), writes=(xnf,))
            k.do("vector", lambda e, kc=kc: e.tensor_add(
                xn.ap[:, kc * 512:(kc + 1) * 512], xnf.ap,
                psA.ap[:, 512:1024]), reads=(xnf, psA), writes=(xn,))
        if own:
            k.do("gpsimd", lambda e, t0=t0: e.dma_start(
                out=scr["sXN"].ap.rearrange("(kc d) t -> d kc t", d=128)[
                    :, :, t0:t0 + 512],
                in_=xn.ap.rearrange("p (kc t) -> p kc t", kc=8)),
                reads=(xn,), writes=(scr["sXN"],), dma=True)
        proj = [("sKT", c.wkT_bf, c.bk_col, gk)]
        if own:
            proj.append(("sQT", c.wqT_bf, c.bq_col, t0))
        for sname, wbf, bcol, dcol in proj:
            for m in range(8):
                slot = psA.ap[:, (m % 2) * 512:(m % 2) * 512 + 512]
                pslot = Buf("slot", None)
                for kc in range(8):
                    k.do("tensor", lambda e, m=m, kc=kc, wbf=wbf, slot=slot:
                         e.matmul(
                             slot,
                             wbf.ap[:, kc * 1024 + m * 128:
                                    kc * 1024 + (m + 1) * 128],
                             xn.ap[:, kc * 512:(kc + 1) * 512],
                             start=(kc == 0), stop=(kc == 7)),
                         reads=(wbf, xn), writes=(psA,))
                k.do("vector", lambda e, m=m, bcol=bcol, slot=slot, tb=tb:
                     e.scalar_tensor_tensor(
                         qk_stage.ap[:, m * 512:(m + 1) * 512], slot,
                         bcol.ap[:, m:m + 1],
                         tb.ap[:, m * 512:(m + 1) * 512],
                         ALU.add, ALU.mult),
                     reads=(psA, bcol, tb), writes=(qk_stage,))
            k.do("gpsimd", lambda e, sname=sname, dcol=dcol: e.dma_start(
                out=scr[sname].ap.rearrange("(kc d) t -> d kc t", d=128)[
                    :, :, dcol:dcol + 512],
                in_=qk_stage.ap.rearrange("p (kc t) -> p kc t", kc=8)),
                reads=(qk_stage,), writes=(scr[sname],), dma=True)
        for hv in range(2):
            for a in range(4):
                for kc in range(8):
                    k.do("tensor", lambda e, hv=hv, a=a, kc=kc: e.matmul(
                        psO.ap[:, a * 512:(a + 1) * 512],
                        xn.ap[:, kc * 512 + a * 128:kc * 512 + (a + 1) * 128],
                        c.wvT_bf.ap[:, kc * 1024 + hv * 512:
                                    kc * 1024 + hv * 512 + 512],
                        start=(kc == 0), stop=False),
                        reads=(xn, c.wvT_bf), writes=(psO,))
                k.do("tensor", lambda e, hv=hv, a=a: e.matmul(
                    psO.ap[:, a * 512:(a + 1) * 512],
                    c.ones_bf.ap[0:1, 0:128],
                    c.bv_row_bf.ap[0:1, hv * 512:hv * 512 + 512],
                    start=False, stop=True),
                    reads=(c.ones_bf, c.bv_row_bf), writes=(psO,))
                k.do("vector", lambda e, hv=hv, a=a: e.tensor_copy(
                    v_stage.ap[:, a * 1024 + hv * 512:a * 1024 + hv * 512
                               + 512],
                    psO.ap[:, a * 512:(a + 1) * 512]),
                    reads=(psO,), writes=(v_stage,))
        k.do("gpsimd", lambda e, gk=gk: e.dma_start(
            out=scr["sV"].ap[gk:gk + 512, :].rearrange(
                "(a p) h -> p a h", p=128),
            in_=v_stage.ap.rearrange("p (a h) -> p a h", a=4)),
            reads=(v_stage,), writes=(scr["sV"],), dma=True)
        if not own:
            continue
        for a in range(4):
            for kc in range(8):
                k.do("tensor", lambda e, a=a, kc=kc: e.matmul(
                    psB.ap[:, 0:8],
                    xn.ap[:, kc * 512 + a * 128:kc * 512 + (a + 1) * 128],
                    c.routerT_bf.ap[:, kc * 8:(kc + 1) * 8],
                    start=(kc == 0), stop=False),
                    reads=(xn, c.routerT_bf), writes=(psB,))
            k.do("tensor", lambda e: e.matmul(
                psB.ap[:, 0:8], c.ones_bf.ap[0:1, 0:128],
                c.rbias_row_bf.ap[0:1, 0:8], start=False, stop=True),
                reads=(c.ones_bf, c.rbias_row_bf), writes=(psB,))
            k.do("vector", lambda e: e.tensor_copy(lg.ap, psB.ap[:, 0:8]),
                 reads=(psB,), writes=(lg,))
            cur = lg
            for r in range(8):
                dst = srt[r % 2]
                if r % 2 == 0:
                    k.do("vector", lambda e, cur=cur, dst=dst: e.tensor_max(
                        dst.ap[:, 0:8:2], cur.ap[:, 0:8:2], cur.ap[:, 1:8:2]),
                        reads=(cur,), writes=(dst,))
                    k.do("vector", lambda e, cur=cur, dst=dst: e.tensor_tensor(
                        dst.ap[:, 1:8:2], cur.ap[:, 0:8:2], cur.ap[:, 1:8:2],
                        ALU.min), reads=(cur,), writes=(dst,))
                else:
                    k.do("vector", lambda e, cur=cur, dst=dst: e.tensor_copy(
                        dst.ap[:, 0:1], cur.ap[:, 0:1]),
                        reads=(cur,), writes=(dst,))
                    k.do("vector", lambda e, cur=cur, dst=dst: e.tensor_copy(
                        dst.ap[:, 7:8], cur.ap[:, 7:8]),
                        reads=(cur,), writes=(dst,))
                    k.do("vector", lambda e, cur=cur, dst=dst: e.tensor_max(
                        dst.ap[:, 1:7:2], cur.ap[:, 1:7:2], cur.ap[:, 2:8:2]),
                        reads=(cur,), writes=(dst,))
                    k.do("vector", lambda e, cur=cur, dst=dst: e.tensor_tensor(
                        dst.ap[:, 2:8:2], cur.ap[:, 1:7:2], cur.ap[:, 2:8:2],
                        ALU.min), reads=(cur,), writes=(dst,))
                cur = dst
            k.do("vector", lambda e, cur=cur: e.tensor_scalar_mul(
                negm.ap, cur.ap[:, 0:1], -1.0), reads=(cur,), writes=(negm,))
            k.do("scalar", lambda e: e.activation(
                eexp.ap, lg.ap, AF.Exp, bias=negm.ap),
                reads=(lg, negm), writes=(eexp,))
            k.do("vector", lambda e, cur=cur: e.tensor_single_scalar(
                maskb.ap, lg.ap, cur.ap[:, 3:4], ALU.is_ge),
                reads=(lg, cur), writes=(maskb,))
            k.do("vector", lambda e: e.tensor_mul(em.ap, maskb.ap, eexp.ap),
                 reads=(maskb, eexp), writes=(em,))
            k.do("vector", lambda e: e.reduce_sum(ssum.ap, em.ap, axis=AX.X),
                 reads=(em,), writes=(ssum,))
            k.do("vector", lambda e: e.reciprocal(rec.ap, ssum.ap),
                 reads=(ssum,), writes=(rec,))
            k.do("vector", lambda e: e.tensor_single_scalar(
                comb.ap, em.ap, rec.ap, ALU.mult),
                reads=(em, rec), writes=(comb,))
            k.do("vector", lambda e: e.tensor_copy(comb_bf.ap, comb.ap),
                 reads=(comb,), writes=(comb_bf,))
            k.do("tensor", lambda e, a=a: e.transpose(
                psT.ap[0:8, a * 128:(a + 1) * 128], comb_bf.ap, c.eye_bf.ap),
                reads=(comb_bf, c.eye_bf), writes=(psT,))
            k.do("vector", lambda e, a=a: e.tensor_copy(
                combT.ap[0:8, a * 128:(a + 1) * 128],
                psT.ap[0:8, a * 128:(a + 1) * 128]),
                reads=(psT,), writes=(combT,))
        k.do("gpsimd", lambda e, t0=t0: e.dma_start(
            out=scr["sCT"].ap[:, t0:t0 + 512], in_=combT.ap[0:8, :]),
            reads=(combT,), writes=(scr["sCT"],), dma=True)
    A.reset()



def emit_p0b(k, A, inp, c, scr):
    A.mark()
    psA, psB, psT, psO = c.psA, c.psB, c.psT, c.psO
    xnb = A.alloc("mx_xnb", 4096, BF16)
    ct = A.alloc("mx_ct", 512, BF16)
    moe_acc = A.alloc("mx_acc", 8192)
    act_sb = [A.alloc(f"mx_act{s}", 4096, BF16) for s in range(2)]
    NGU = 8
    gu_roll = [A.alloc(f"mx_gu{i}", 512, BF16) for i in range(NGU)]
    dwn_roll = [A.alloc(f"mx_dwn{i}", 8192, BF16) for i in range(2)]
    comb_sb = [A.alloc(f"mx_comb{s}", 512) for s in range(2)]
    gmin2 = [A.alloc(f"mx_gmin{i}", 512) for i in range(2)]
    upc_2 = [A.alloc(f"mx_upc{i}", 512) for i in range(2)]
    upc22 = [A.alloc(f"mx_upc2{i}", 512) for i in range(2)]
    gs2 = [A.alloc(f"mx_gs{i}", 512) for i in range(2)]
    tmp2 = [A.alloc(f"mx_tmp{i}", 512) for i in range(2)]

    def gu_dma(e_i, j, slot):
        src = inp["wgu_blk"].ap[e_i, j]
        k.do("gpsimd", lambda e: e.dma_start(out=gu_roll[slot].ap, in_=src),
             reads=(inp["wgu_blk"],), writes=(gu_roll[slot],), dma=True)

    for tile in range(4):
        t0 = tile * 1024
        k.do("sync", lambda e, t0=t0: e.dma_start(
            out=xnb.ap.rearrange("p (kc t) -> p kc t", kc=8),
            in_=scr["sXN"].ap.rearrange("(kc d) t -> d kc t", d=128)[
                :, :, t0:t0 + 1024]),
            reads=(scr["sXN"],), writes=(xnb,), dma=True)
        k.do("sync", lambda e, t0=t0: e.dma_start(
            out=ct.ap[0:8, :], in_=scr["sCT"].ap[:, t0:t0 + 1024]),
            reads=(scr["sCT"],), writes=(ct,), dma=True)
        for dc in range(8):
            for sub in range(2):
                k.do("tensor", lambda e, dc=dc, sub=sub: e.matmul(
                    psB.ap, c.db8_bf.ap[0:8, dc * 128:(dc + 1) * 128],
                    ct.ap[0:8, sub * 512:(sub + 1) * 512],
                    start=True, stop=True),
                    reads=(c.db8_bf, ct), writes=(psB,))
                k.do("vector", lambda e, dc=dc, sub=sub: e.tensor_copy(
                    moe_acc.ap[:, dc * 1024 + sub * 512:
                               dc * 1024 + sub * 512 + 512], psB.ap),
                    reads=(psB,), writes=(moe_acc,))
        for ei in range(EXP):
            dwn = dwn_roll[ei % 2]
            k.do("gpsimd", lambda e, ei=ei, dwn=dwn: e.dma_start(
                out=dwn.ap, in_=inp["dwn_blk"].ap[ei]),
                reads=(inp["dwn_blk"],), writes=(dwn,), dma=True)
            for sub in range(2):
                k.do("tensor", lambda e, ei=ei, sub=sub: e.matmul(
                    psB.ap, c.sel8_bf.ap[0:8, ei * 128:(ei + 1) * 128],
                    ct.ap[0:8, sub * 512:(sub + 1) * 512],
                    start=True, stop=True),
                    reads=(c.sel8_bf, ct), writes=(psB,))
                k.do("vector", lambda e, sub=sub: e.tensor_scalar_mul(
                    comb_sb[sub].ap, psB.ap, 1.0 / 1.702),
                    reads=(psB,), writes=(comb_sb[sub],))
            if ei == 0 and tile == 0:
                for pre in range(NGU // 2):
                    gu_dma(0, pre, 2 * pre)
                    gu_dma(0, 16 + pre, 2 * pre + 1)
            for ic in range(16):
                gslot, uslot = (ic % (NGU // 2)) * 2, (ic % (NGU // 2)) * 2 + 1
                gbuf, ubuf = gu_roll[gslot], gu_roll[uslot]
                gu_ps = [(psA, 0), (psA, 512)] if ic % 2 == 0 else \
                        [(psB, 0), (psO, 0)]
                for sub in range(2):
                    (gps, goff), (ups, uoff) = gu_ps[0], gu_ps[1]
                    if sub == 1:
                        (gps, goff), (ups, uoff) = (psB, 0), (psO, 0)
                    else:
                        (gps, goff), (ups, uoff) = (psA, 0), (psA, 512)
                    for kc in range(8):
                        k.do("tensor", lambda e, kc=kc, sub=sub, gbuf=gbuf,
                             gps=gps, goff=goff: e.matmul(
                                 gps.ap[:, goff:goff + 512],
                                 gbuf.ap[:, kc * 128:(kc + 1) * 128],
                                 xnb.ap[:, kc * 1024 + sub * 512:
                                        kc * 1024 + sub * 512 + 512],
                                 start=(kc == 0), stop=(kc == 7)),
                             reads=(gbuf, xnb), writes=(gps,))
                    for kc in range(8):
                        k.do("tensor", lambda e, kc=kc, sub=sub, ubuf=ubuf,
                             ups=ups, uoff=uoff: e.matmul(
                                 ups.ap[:, uoff:uoff + 512],
                                 ubuf.ap[:, kc * 128:(kc + 1) * 128],
                                 xnb.ap[:, kc * 1024 + sub * 512:
                                        kc * 1024 + sub * 512 + 512],
                                 start=(kc == 0), stop=(kc == 7)),
                             reads=(ubuf, xnb), writes=(ups,))
                    gb = c.gub.ap[:, ei * 32 + ic:ei * 32 + ic + 1]
                    ub = c.gub.ap[:, ei * 32 + 16 + ic:ei * 32 + 16 + ic + 1]
                    gmin, upc = gmin2[ic % 2], upc_2[ic % 2]
                    upc2, gs = upc22[ic % 2], gs2[ic % 2]
                    k.do("vector", lambda e, gps=gps, goff=goff, gb=gb,
                         gmin=gmin: e.tensor_scalar(
                             gmin.ap, gps.ap[:, goff:goff + 512],
                             gb, 7.0, ALU.add, ALU.min),
                         reads=(gps, c.gub), writes=(gmin,))
                    k.do("vector", lambda e, ups=ups, uoff=uoff, ub=ub,
                         upc=upc: e.tensor_scalar(
                             upc.ap, ups.ap[:, uoff:uoff + 512],
                             ub, 7.0, ALU.add, ALU.min),
                         reads=(ups, c.gub), writes=(upc,))
                    k.do("vector", lambda e, upc=upc, upc2=upc2:
                         e.tensor_scalar(
                             upc2.ap, upc.ap, -7.0, 1.0, ALU.max, ALU.add),
                         reads=(upc,), writes=(upc2,))
                    k.do("scalar", lambda e, gmin=gmin, gs=gs: e.activation(
                        gs.ap, gmin.ap, AF.Silu, scale=1.702),
                        reads=(gmin,), writes=(gs,))
                    k.do("vector", lambda e, sub=sub, ic=ic, upc2=upc2,
                         gs=gs: e.tensor_mul(
                             act_sb[sub].ap[:, ic * 512:(ic + 1) * 512],
                             upc2.ap, gs.ap),
                         reads=(upc2, gs), writes=(act_sb[sub],))
                nic = ic + NGU // 2
                ne, nj = (ei, nic) if nic < 16 else (ei + 1, nic - 16)
                if ne < EXP:
                    gu_dma(ne, nj, gslot)
                    gu_dma(ne, 16 + nj, uslot)
                elif tile < 3:
                    gu_dma(0, nj, gslot)
                    gu_dma(0, 16 + nj, uslot)
            for sub in range(2):
                for dg in range(2):
                    for dc4 in range(4):
                        dc = dg * 4 + dc4
                        for ic in range(16):
                            k.do("tensor", lambda e, ic=ic, dc=dc, sub=sub,
                                 dc4=dc4, dwn=dwn: e.matmul(
                                     psO.ap[:, dc4 * 512:(dc4 + 1) * 512],
                                     dwn.ap[:, ic * 1024 + dc * 128:
                                            ic * 1024 + (dc + 1) * 128],
                                     act_sb[sub].ap[:, ic * 512:
                                                    (ic + 1) * 512],
                                     start=(ic == 0), stop=(ic == 15)),
                                 reads=(dwn, act_sb[sub]), writes=(psO,))
                        tmp = tmp2[dc4 % 2]
                        k.do("vector", lambda e, dc=dc, sub=sub, dc4=dc4,
                             tmp=tmp: e.tensor_mul(
                                 tmp.ap,
                                 psO.ap[:, dc4 * 512:(dc4 + 1) * 512],
                                 comb_sb[sub].ap),
                             reads=(psO, comb_sb[sub]), writes=(tmp,))
                        k.do("vector", lambda e, dc=dc, sub=sub, tmp=tmp:
                             e.tensor_add(
                                 moe_acc.ap[:, dc * 1024 + sub * 512:
                                            dc * 1024 + sub * 512 + 512],
                                 moe_acc.ap[:, dc * 1024 + sub * 512:
                                            dc * 1024 + sub * 512 + 512],
                                 tmp.ap),
                             reads=(tmp, moe_acc), writes=(moe_acc,))
        k.do("gpsimd", lambda e, t0=t0: e.dma_start(
            out=scr["sMoE"].ap.rearrange("(dc d) t -> d dc t", d=128)[
                :, :, t0:t0 + 1024],
            in_=moe_acc.ap.rearrange("p (dc t) -> p dc t", dc=8)),
            reads=(moe_acc,), writes=(scr["sMoE"],), dma=True)
    A.reset()



def emit_p1(k, A, inp, c, scr):
    A.mark()
    psA, psB, psT, psO = c.psA, c.psB, c.psT, c.psO
    Ksb = A.alloc("at_K", 16384, BF16)
    Vsb = A.alloc("at_V", 16384, BF16)
    Ssb_r = A.alloc("at_S", 8192)
    Ssb = [Buf(f"at_S{i}", Ssb_r.ap[:, i * 256:(i + 1) * 256])
           for i in range(32)]
    Pbf_r = A.alloc("at_P", 4096, BF16)
    Pbf = [Buf(f"at_P{i}", Pbf_r.ap[:, i * 256:(i + 1) * 256])
           for i in range(32)]
    Qsb = [A.alloc(f"at_Q{i}", 1024, BF16) for i in range(2)]
    Mrun = A.alloc("at_M", 256)
    Mbf = A.alloc("at_Mbf", 128, BF16)
    mcolb = A.alloc("at_mcol", 1, BF16)
    mtsb = A.alloc("at_mt", 128, BF16)
    mrow = A.alloc("at_mrow", 128, BF16)
    Mb = A.alloc("at_Mb", 256)
    ost = A.alloc("at_ost", 2048)
    rowst = A.alloc("at_rows", 512)
    stmp2 = [A.alloc(f"at_stmp{i}", 256) for i in range(2)]

    for h in range(2):
        r0 = h * 4096
        oname, rname = ("sOA", "sRA") if h == 0 else ("sOB", "sRB")
        k.do("sync", lambda e, r0=r0: e.dma_start(
            out=Ksb.ap.rearrange("p (m t) -> p m t", m=8),
            in_=scr["sKT"].ap.rearrange("(m d) t -> d m t", d=128)[
                :, :, r0:r0 + 4096]),
            reads=(scr["sKT"],), writes=(Ksb,), dma=True)
        k.do("sync", lambda e, r0=r0: e.dma_start(
            out=Vsb.ap.rearrange("p (tc hh) -> p tc hh", tc=32),
            in_=scr["sV"].ap[r0:r0 + 4096, :].rearrange(
                "(tc p) hh -> p tc hh", p=128)),
            reads=(scr["sV"],), writes=(Vsb,), dma=True)
        for qb in range(16):
            q0 = qb * 256
            Q = Qsb[qb % 2]
            k.do("sync", lambda e, q0=q0, Q=Q: e.dma_start(
                out=Q.ap.rearrange("p (m t) -> p m t", m=8),
                in_=scr["sQT"].ap.rearrange("(m d) t -> d m t", d=128)[
                    :, :, q0:q0 + 256]),
                reads=(scr["sQT"],), writes=(Q,), dma=True)
            for kt in range(32):
                off = 0 if kt % 2 == 0 else 512
                for m in range(8):
                    k.do("tensor", lambda e, kt=kt, m=m, off=off, Q=Q:
                         e.matmul(
                             psA.ap[:, off:off + 256],
                             Ksb.ap[:, m * 4096 + kt * 128:
                                    m * 4096 + (kt + 1) * 128],
                             Q.ap[:, m * 256:(m + 1) * 256],
                             start=(m == 0), stop=(m == 7),
                             skip_group_check=True),
                         reads=(Ksb, Q), writes=(psA,))
                k.do("vector", lambda e, kt=kt, off=off: e.tensor_copy(
                    Ssb[kt].ap, psA.ap[:, off:off + 256]),
                    reads=(psA,), writes=(Ssb[kt],))
                if kt == 0:
                    k.do("vector", lambda e, off=off: e.tensor_copy(
                        Mrun.ap, psA.ap[:, off:off + 256]),
                        reads=(psA,), writes=(Mrun,))
                else:
                    k.do("vector", lambda e, off=off: e.tensor_max(
                        Mrun.ap, Mrun.ap, psA.ap[:, off:off + 256]),
                        reads=(psA, Mrun), writes=(Mrun,))
            k.do("vector", lambda e: e.tensor_copy(Mbf.ap, Mrun.ap),
                 reads=(Mrun,), writes=(Mbf,))
            for j in range(2):
                k.do("tensor", lambda e, j=j: e.transpose(
                    psT.ap[:, j * 128:(j + 1) * 128],
                    Mbf.ap[:, j * 128:(j + 1) * 128], c.eye_bf.ap),
                    reads=(Mbf, c.eye_bf), writes=(psT,))
                k.do("vector", lambda e, j=j: e.tensor_copy(
                    mtsb.ap[:, j * 128:(j + 1) * 128],
                    psT.ap[:, j * 128:(j + 1) * 128]),
                    reads=(psT,), writes=(mtsb,))
                k.do("vector", lambda e, j=j: e.reduce_max(
                    mcolb.ap[:, j:j + 1], mtsb.ap[:, j * 128:(j + 1) * 128],
                    axis=AX.X), reads=(mtsb,), writes=(mcolb,))
            for j in range(2):
                k.do("tensor", lambda e, j=j: e.transpose(
                    psT.ap[0:1, 256 + j * 128:256 + (j + 1) * 128],
                    mcolb.ap[:, j:j + 1], c.eye_bf.ap),
                    reads=(mcolb, c.eye_bf), writes=(psT,))
                k.do("vector", lambda e, j=j: e.tensor_copy(
                    mrow.ap[0:1, j * 128:(j + 1) * 128],
                    psT.ap[0:1, 256 + j * 128:256 + (j + 1) * 128]),
                    reads=(psT,), writes=(mrow,))
            k.do("tensor", lambda e: e.matmul(
                psB.ap[:, 0:256], c.ones_bf.ap[0:1, :],
                mrow.ap[0:1, 0:256], start=True, stop=True,
                skip_group_check=True),
                reads=(c.ones_bf, mrow), writes=(psB,))
            k.do("vector", lambda e: e.tensor_copy(Mb.ap, psB.ap[:, 0:256]),
                 reads=(psB,), writes=(Mb,))
            for kt in range(32):
                st_ = stmp2[kt % 2]
                k.do("vector", lambda e, kt=kt, st_=st_: e.tensor_sub(
                    st_.ap, Ssb[kt].ap, Mb.ap),
                    reads=(Ssb[kt], Mb), writes=(st_,))
                k.do("scalar", lambda e, kt=kt, st_=st_: e.activation(
                    Pbf[kt].ap, st_.ap, AF.Exp,
                    scale=SCALE), reads=(st_,), writes=(Pbf[kt],))
            for dg in range(2):
                for kt in range(32):
                    for mi in range(4):
                        m = dg * 4 + mi
                        k.do("tensor", lambda e, kt=kt, m=m, mi=mi: e.matmul(
                            psO.ap[:, mi * 512:mi * 512 + 256],
                            Vsb.ap[:, kt * 1024 + m * 128:
                                   kt * 1024 + (m + 1) * 128],
                            Pbf[kt].ap,
                            start=(kt == 0), stop=(kt == 31),
                            skip_group_check=True),
                            reads=(Vsb, Pbf[kt]), writes=(psO,))
                    if dg == 0:
                        k.do("tensor", lambda e, kt=kt: e.matmul(
                            psB.ap[0:1, 256:512], c.ones_bf.ap[:, 0:1],
                            Pbf[kt].ap,
                            start=(kt == 0), stop=(kt == 31),
                            skip_group_check=True),
                            reads=(c.ones_bf, Pbf[kt]), writes=(psB,))
                k.do("vector", lambda e, dg=dg: e.tensor_copy(
                    ost.ap.rearrange("p (m t) -> p m t", m=8)[
                        :, dg * 4:(dg + 1) * 4, :],
                    psO.ap.rearrange("p (b h) -> p b h", b=4)[:, :, 0:256]),
                    reads=(psO,), writes=(ost,))
            k.do("vector", lambda e: e.tensor_copy(
                rowst.ap[0:1, 0:256], mrow.ap[0:1, 0:256]),
                reads=(mrow,), writes=(rowst,))
            k.do("vector", lambda e: e.tensor_copy(
                rowst.ap[0:1, 256:512], psB.ap[0:1, 256:512]),
                reads=(psB,), writes=(rowst,))
            k.do("gpsimd", lambda e, q0=q0, oname=oname: e.dma_start(
                out=scr[oname].ap.rearrange("(m d) t -> d m t", d=128)[
                    :, :, q0:q0 + 256],
                in_=ost.ap.rearrange("p (m t) -> p m t", m=8)),
                reads=(ost,), writes=(scr[oname],), dma=True)
            k.do("gpsimd", lambda e, q0=q0, rname=rname: e.dma_start(
                out=scr[rname].ap[0:1, q0:q0 + 256],
                in_=rowst.ap[0:1, 0:256]),
                reads=(rowst,), writes=(scr[rname],), dma=True)
            k.do("gpsimd", lambda e, q0=q0, rname=rname: e.dma_start(
                out=scr[rname].ap[1:2, q0:q0 + 256],
                in_=rowst.ap[0:1, 256:512]),
                reads=(rowst,), writes=(scr[rname],), dma=True)
    A.reset()



def emit_p1c(k, A, inp, c, scr):
    A.mark()
    psA = c.psA
    xTt = A.alloc("mg_x", 4096)
    moet = A.alloc("mg_moe", 4096)
    oat = A.alloc("mg_oa", 4096)
    obt = A.alloc("mg_ob", 4096)
    x1t = A.alloc("mg_x1", 4096)
    mA = A.alloc("mg_mA", 512)
    sA = A.alloc("mg_sA", 512)
    mB = A.alloc("mg_mB", 512)
    sB = A.alloc("mg_sB", 512)
    mm = A.alloc("mg_m", 512)
    fA = A.alloc("mg_fA", 512)
    fB = A.alloc("mg_fB", 512)
    u1 = A.alloc("mg_u1", 512)
    u2 = A.alloc("mg_u2", 512)
    srow = A.alloc("mg_s", 512)
    rec = A.alloc("mg_rec", 512)
    gA = A.alloc("mg_gA", 512)
    gB = A.alloc("mg_gB", 512)
    ut = A.alloc("mg_ut", 512)
    vt = A.alloc("mg_vt", 512)

    for tile in range(8):
        t0 = tile * 512
        for nm, b in (("sXT", xTt), ("sMoE", moet), ("sOA", oat),
                      ("sOB", obt)):
            k.do("sync", lambda e, nm=nm, b=b, t0=t0: e.dma_start(
                out=b.ap.rearrange("p (m t) -> p m t", m=8),
                in_=scr[nm].ap.rearrange("(m d) t -> d m t", d=128)[
                    :, :, t0:t0 + 512]),
                reads=(scr[nm],), writes=(b,), dma=True)
        for nm, rb, sb_ in (("sRA", mA, sA), ("sRB", mB, sB)):
            k.do("sync", lambda e, nm=nm, rb=rb, t0=t0: e.dma_start(
                out=rb.ap[0:1, :], in_=scr[nm].ap[0:1, t0:t0 + 512]),
                reads=(scr[nm],), writes=(rb,), dma=True)
            k.do("sync", lambda e, nm=nm, sb_=sb_, t0=t0: e.dma_start(
                out=sb_.ap[0:1, :], in_=scr[nm].ap[1:2, t0:t0 + 512]),
                reads=(scr[nm],), writes=(sb_,), dma=True)
        R1 = (slice(0, 1), slice(None))
        k.do("vector", lambda e: e.tensor_max(
            mm.ap[R1], mA.ap[R1], mB.ap[R1]), reads=(mA, mB), writes=(mm,))
        k.do("vector", lambda e: e.tensor_sub(
            fA.ap[R1], mA.ap[R1], mm.ap[R1]), reads=(mA, mm), writes=(fA,))
        k.do("vector", lambda e: e.tensor_sub(
            fB.ap[R1], mB.ap[R1], mm.ap[R1]), reads=(mB, mm), writes=(fB,))
        k.do("scalar", lambda e: e.activation(
            fA.ap[R1], fA.ap[R1], AF.Exp, scale=SCALE),
            reads=(fA,), writes=(fA,))
        k.do("scalar", lambda e: e.activation(
            fB.ap[R1], fB.ap[R1], AF.Exp, scale=SCALE),
            reads=(fB,), writes=(fB,))
        k.do("vector", lambda e: e.tensor_mul(
            u1.ap[R1], sA.ap[R1], fA.ap[R1]), reads=(sA, fA), writes=(u1,))
        k.do("vector", lambda e: e.tensor_mul(
            u2.ap[R1], sB.ap[R1], fB.ap[R1]), reads=(sB, fB), writes=(u2,))
        k.do("vector", lambda e: e.tensor_add(
            srow.ap[R1], u1.ap[R1], u2.ap[R1]), reads=(u1, u2),
            writes=(srow,))
        k.do("vector", lambda e: e.reciprocal(rec.ap[R1], srow.ap[R1]),
             reads=(srow,), writes=(rec,))
        k.do("vector", lambda e: e.tensor_mul(
            gA.ap[R1], fA.ap[R1], rec.ap[R1]), reads=(fA, rec), writes=(gA,))
        k.do("vector", lambda e: e.tensor_mul(
            gB.ap[R1], fB.ap[R1], rec.ap[R1]), reads=(fB, rec), writes=(gB,))
        k.do("tensor", lambda e: e.matmul(
            psA.ap[:, 0:512], c.ones_f32.ap[0:1, :], gA.ap[0:1, :],
            start=True, stop=True, skip_group_check=True),
            reads=(c.ones_f32, gA), writes=(psA,))
        k.do("tensor", lambda e: e.matmul(
            psA.ap[:, 512:1024], c.ones_f32.ap[0:1, :], gB.ap[0:1, :],
            start=True, stop=True, skip_group_check=True),
            reads=(c.ones_f32, gB), writes=(psA,))
        for dc in range(8):
            sl = slice(dc * 512, (dc + 1) * 512)
            k.do("vector", lambda e, sl=sl: e.tensor_mul(
                ut.ap, oat.ap[:, sl], psA.ap[:, 0:512]),
                reads=(oat, psA), writes=(ut,))
            k.do("vector", lambda e, sl=sl: e.tensor_mul(
                vt.ap, obt.ap[:, sl], psA.ap[:, 512:1024]),
                reads=(obt, psA), writes=(vt,))
            k.do("vector", lambda e: e.tensor_add(ut.ap, ut.ap, vt.ap),
                 reads=(ut, vt), writes=(ut,))
            k.do("vector", lambda e, sl=sl: e.tensor_add(
                ut.ap, ut.ap, xTt.ap[:, sl]), reads=(ut, xTt), writes=(ut,))
            k.do("vector", lambda e, sl=sl: e.tensor_add(
                x1t.ap[:, sl], ut.ap, moet.ap[:, sl]),
                reads=(ut, moet), writes=(x1t,))
        k.do("gpsimd", lambda e, t0=t0: e.dma_start(
            out=scr["sX1"].ap.rearrange("(m d) t -> d m t", d=128)[
                :, :, t0:t0 + 512],
            in_=x1t.ap.rearrange("p (m t) -> p m t", m=8)),
            reads=(x1t,), writes=(scr["sX1"],), dma=True)
    A.reset()



def emit_p2a(k, A, inp, c, scr, carry_st):
    A.mark()
    psA, psB, psT, psO = c.psA, c.psB, c.psT, c.psO
    x1b = A.alloc("ga_x1", 8192)
    xn1b = A.alloc("ga_xn1", 4096, BF16)
    xinb = A.alloc("ga_xin", 16384, BF16)
    xstat = [A.alloc(f"ga_xs{i}", 256, BF16) for i in range(2)]
    sqstat = [A.alloc(f"ga_sq{i}", 256, BF16) for i in range(2)]
    srow = A.alloc("ga_srow", 512)
    sqrow = A.alloc("ga_sqrow", 512)
    rstdrow = A.alloc("ga_rstdrow", 512)
    brow = A.alloc("ga_brow", 512)
    lntmp = A.alloc("ga_lntmp", 512)
    NR = 4
    w_roll = [A.alloc(f"ga_w{i}", 512, BF16) for i in range(3)]
    w_roll.append(w_roll[2])
    NWG = 4
    wg_roll = [A.alloc(f"ga_wg{i}", 2048, BF16) for i in range(NWG)]
    xsk_tmp = [A.alloc(f"ga_xsk{i}", 256, BF16) for i in range(2)]
    c_sb2 = [A.alloc(f"ga_c{i}", 512) for i in range(2)]
    z_sb2 = [A.alloc(f"ga_z{i}", 512) for i in range(2)]
    g_sb2 = [A.alloc(f"ga_g{i}", 512) for i in range(2)]
    t1_only = A.alloc("ga_t1", 512)
    t2_only = A.alloc("ga_t2", 512)
    gg_only = A.alloc("ga_gg", 512)
    t1_sb2 = [t1_only, t1_only]
    t2_sb2 = [t2_only, t2_only]
    gg_sb2 = [gg_only, gg_only]
    v_sb2 = [A.alloc(f"ga_v{i}", 512) for i in range(2)]
    h_sb2 = [A.alloc(f"ga_h{i}", 512) for i in range(2)]
    cp_sb2 = [A.alloc(f"ga_cp{i}", 512) for i in range(2)]
    state = A.alloc("ga_state", 32)
    cstate = A.alloc("ga_cstate", 32)
    zt = A.alloc("ga_zt", 512)
    negbg = A.alloc("ga_negbg", 64)
    bgc = A.alloc("ga_bgc", 64)
    b1c = A.alloc("ga_b1c", 32)
    b2c = A.alloc("ga_b2c", 32)

    for nm, b in (("bgc", bgc), ("b1c", b1c), ("b2c", b2c)):
        k.do("sync", lambda e, nm=nm, b=b: e.dma_start(
            out=b.ap, in_=inp[nm].ap), reads=(inp[nm],), writes=(b,),
            dma=True)
    k.do("vector", lambda e: e.tensor_scalar_mul(negbg.ap, bgc.ap, -1.0),
         reads=(bgc,), writes=(negbg,))
    k.do("vector", lambda e: e.memset(state.ap, 0.0), writes=(state,))
    k.do("vector", lambda e: e.memset(cstate.ap, 1.0), writes=(cstate,))
    k.do("vector", lambda e: e.memset(zt.ap, 0.0), writes=(zt,))

    def w_dma(wname, col, slot, nkc, roll):
        src = inp[wname].ap[col]
        k.do("gpsimd", lambda e: e.dma_start(out=roll[slot].ap, in_=src),
             reads=(inp[wname],), writes=(roll[slot],), dma=True)

    for tile in range(4):
        t0 = tile * 1024
        k.do("sync", lambda e, t0=t0: e.dma_start(
            out=x1b.ap.rearrange("p (kc t) -> p kc t", kc=8),
            in_=scr["sX1"].ap.rearrange("(kc d) t -> d kc t", d=128)[
                :, :, t0:t0 + 1024]),
            reads=(scr["sX1"],), writes=(x1b,), dma=True)
        for th in range(2):
            hoff = th * 512
            for kc in range(8):
                xs = xstat[kc % 2]
                sq = sqstat[kc % 2]
                k.do("vector", lambda e, kc=kc, hoff=hoff, xs=xs:
                     e.tensor_copy(
                         xs.ap, x1b.ap[:, kc * 1024 + hoff:
                                       kc * 1024 + hoff + 512]),
                     reads=(x1b,), writes=(xs,))
                k.do("scalar", lambda e, kc=kc, hoff=hoff, sq=sq:
                     e.activation(
                         sq.ap, x1b.ap[:, kc * 1024 + hoff:
                                       kc * 1024 + hoff + 512], AF.Square),
                     reads=(x1b,), writes=(sq,))
                k.do("tensor", lambda e, kc=kc, xs=xs: e.matmul(
                    psB.ap[0:1, 0:512], c.ones_bf.ap[:, 0:1], xs.ap,
                    start=(kc == 0), stop=(kc == 7), skip_group_check=True),
                    reads=(xs, c.ones_bf), writes=(psB,))
                k.do("tensor", lambda e, kc=kc, sq=sq: e.matmul(
                    psA.ap[0:1, 0:512], c.ones_bf.ap[:, 0:1], sq.ap,
                    start=(kc == 0), stop=(kc == 7), skip_group_check=True),
                    reads=(sq, c.ones_bf), writes=(psA,))
            R1 = (slice(0, 1), slice(None))
            k.do("vector", lambda e: e.tensor_copy(
                srow.ap[R1], psB.ap[0:1, 0:512]), reads=(psB,),
                writes=(srow,))
            k.do("vector", lambda e: e.tensor_copy(
                sqrow.ap[R1], psA.ap[0:1, 0:512]), reads=(psA,),
                writes=(sqrow,))
            k.do("scalar", lambda e: e.activation(
                srow.ap[R1], srow.ap[R1], AF.Copy, scale=1.0 / 1024.0),
                reads=(srow,), writes=(srow,))
            k.do("scalar", lambda e: e.activation(
                sqrow.ap[R1], sqrow.ap[R1], AF.Copy, scale=1.0 / 1024.0),
                reads=(sqrow,), writes=(sqrow,))
            k.do("scalar", lambda e: e.activation(
                lntmp.ap[R1], srow.ap[R1], AF.Square),
                reads=(srow,), writes=(lntmp,))
            k.do("vector", lambda e: e.tensor_sub(
                sqrow.ap[R1], sqrow.ap[R1], lntmp.ap[R1]),
                reads=(sqrow, lntmp), writes=(sqrow,))
            k.do("scalar", lambda e: e.activation(
                sqrow.ap[R1], sqrow.ap[R1], AF.Ln,
                bias=c.eps_col.ap[0:1, 0:1]),
                reads=(sqrow, c.eps_col), writes=(sqrow,))
            k.do("scalar", lambda e: e.activation(
                rstdrow.ap[R1], sqrow.ap[R1], AF.Exp, scale=-0.5),
                reads=(sqrow,), writes=(rstdrow,))
            k.do("vector", lambda e: e.scalar_tensor_tensor(
                brow.ap[R1], srow.ap[R1], -1.0, rstdrow.ap[R1],
                ALU.mult, ALU.mult), reads=(srow, rstdrow), writes=(brow,))
            k.do("tensor", lambda e: e.matmul(
                psA.ap[:, 0:512], c.ones_f32.ap[0:1, :], rstdrow.ap[R1],
                start=True, stop=True, skip_group_check=True),
                reads=(c.ones_f32, rstdrow), writes=(psA,))
            k.do("tensor", lambda e: e.matmul(
                psB.ap[:, 0:512], c.ones_f32.ap[0:1, :], brow.ap[R1],
                start=True, stop=True, skip_group_check=True),
                reads=(c.ones_f32, brow), writes=(psB,))
            for kc in range(8):
                k.do("vector", lambda e, kc=kc, hoff=hoff: e.tensor_mul(
                    lntmp.ap, x1b.ap[:, kc * 1024 + hoff:
                                     kc * 1024 + hoff + 512],
                    psA.ap[:, 0:512]), reads=(x1b, psA), writes=(lntmp,))
                k.do("vector", lambda e, kc=kc, hoff=hoff: e.tensor_add(
                    xn1b.ap[:, kc * 1024 + hoff:kc * 1024 + hoff + 512],
                    lntmp.ap, psB.ap[:, 0:512]),
                    reads=(lntmp, psB), writes=(xn1b,))
        k.do("gpsimd", lambda e, t0=t0: e.dma_start(
            out=scr["sXN1"].ap.rearrange("(kc d) t -> d kc t", d=128)[
                :, :, t0:t0 + 1024],
            in_=xn1b.ap.rearrange("p (kc t) -> p kc t", kc=8)),
            reads=(xn1b,), writes=(scr["sXN1"],), dma=True)
        if tile == 0:
            for pre in range(2):
                w_dma("w1_blk", pre, pre, 8, w_roll)
        for ec in range(32):
            wslot = ec % 2
            w2slot = 2
            w_dma("w2_blk", ec, w2slot, 8, w_roll)
            for th in range(2):
                for kc in range(8):
                    k.do("tensor", lambda e, kc=kc, th=th, ec=ec,
                         w=w_roll[wslot]: e.matmul(
                             psO.ap[:, th * 512:th * 512 + 512],
                             w.ap[:, kc * 128:(kc + 1) * 128],
                             xn1b.ap[:, kc * 1024 + th * 512:
                                     kc * 1024 + th * 512 + 512],
                             start=(kc == 0), stop=(kc == 7),
                             skip_group_check=True),
                         reads=(w_roll[wslot], xn1b), writes=(psO,))
                k.do("scalar", lambda e, ec=ec, th=th: e.activation(
                    xinb.ap[:, ec * 1024 + th * 512:ec * 1024 + th * 512
                            + 512],
                    psO.ap[:, th * 512:th * 512 + 512], AF.Silu,
                    bias=b1c.ap[:, ec:ec + 1]),
                    reads=(psO, b1c), writes=(xinb,))
            if ec < 31:
                w_dma("w1_blk", ec + 1, (ec + 1) % 2, 8, w_roll)
            elif tile < 3:
                w_dma("w1_blk", 0, 0, 8, w_roll)
            for th in range(2):
                xsk = xsk_tmp[th]
                for kc in range(8):
                    k.do("tensor", lambda e, kc=kc, th=th, ec=ec,
                         w=w_roll[w2slot]: e.matmul(
                             psO.ap[:, 1024 + th * 512:1024 + th * 512
                                    + 512],
                             w.ap[:, kc * 128:(kc + 1) * 128],
                             xn1b.ap[:, kc * 1024 + th * 512:
                                     kc * 1024 + th * 512 + 512],
                             start=(kc == 0), stop=(kc == 7),
                             skip_group_check=True),
                         reads=(w_roll[w2slot], xn1b), writes=(psO,))
                k.do("scalar", lambda e, ec=ec, th=th, xsk=xsk: e.activation(
                    xsk.ap, psO.ap[:, 1024 + th * 512:1024 + th * 512 + 512],
                    AF.Silu, bias=b2c.ap[:, ec:ec + 1]),
                    reads=(psO, b2c), writes=(xsk,))
                k.do("gpsimd", lambda e, ec=ec, th=th, t0=t0, xsk=xsk:
                     e.dma_start(
                         out=scr["sXSK"].ap[ec * 128:(ec + 1) * 128,
                                            t0 + th * 512:t0 + th * 512
                                            + 512],
                         in_=xsk.ap), reads=(xsk,), writes=(scr["sXSK"],),
                     dma=True)
        if tile == 0:
            w_dma("wg_blk", 0, 0, 32, wg_roll)
            w_dma("wg_blk", 32, 1, 32, wg_roll)
        for sc in range(32):
            gslot, hslot = (sc % (NWG // 2)) * 2, (sc % (NWG // 2)) * 2 + 1
            if sc < 31:
                nxt = sc + 1
                w_dma("wg_blk", nxt, (nxt % (NWG // 2)) * 2, 32, wg_roll)
                w_dma("wg_blk", 32 + nxt, (nxt % (NWG // 2)) * 2 + 1, 32,
                      wg_roll)
            elif tile < 3:
                w_dma("wg_blk", 0, 0, 32, wg_roll)
                w_dma("wg_blk", 32, 1, 32, wg_roll)
            for th in range(2):
                par = (sc * 2 + th) % 2
                gps = psA.ap[:, 0:512] if par == 0 else psO.ap[:, 0:512]
                gbuf = psA if par == 0 else psO
                hps = (psA.ap[:, 512:1024] if par == 0
                       else psO.ap[:, 512:1024])
                c_sb, z_sb, g_sb = c_sb2[par], z_sb2[par], g_sb2[par]
                t1_sb, t2_sb, gg_sb = t1_sb2[par], t2_sb2[par], gg_sb2[par]
                v_sb, h_sb, cp_sb = v_sb2[par], h_sb2[par], cp_sb2[par]
                for ec in range(32):
                    k.do("tensor", lambda e, ec=ec, th=th, gps=gps,
                         w=wg_roll[gslot]: e.matmul(
                             gps,
                             w.ap[:, ec * 128:(ec + 1) * 128],
                             xinb.ap[:, ec * 1024 + th * 512:
                                     ec * 1024 + th * 512 + 512],
                             start=(ec == 0), stop=(ec == 31),
                             skip_group_check=True),
                         reads=(wg_roll[gslot], xinb), writes=(gbuf,))
                for ec in range(32):
                    k.do("tensor", lambda e, ec=ec, th=th, hps=hps,
                         w=wg_roll[hslot]: e.matmul(
                             hps,
                             w.ap[:, ec * 128:(ec + 1) * 128],
                             xinb.ap[:, ec * 1024 + th * 512:
                                     ec * 1024 + th * 512 + 512],
                             start=(ec == 0), stop=(ec == 31),
                             skip_group_check=True),
                         reads=(wg_roll[hslot], xinb), writes=(gbuf,))
                k.do("scalar", lambda e, sc=sc, gps=gps, c_sb=c_sb:
                     e.activation(
                         c_sb.ap, gps, AF.Sigmoid, scale=-1.0,
                         bias=negbg.ap[:, sc:sc + 1]),
                     reads=(gbuf, negbg), writes=(c_sb,))
                k.do("scalar", lambda e, sc=sc, gps=gps, z_sb=z_sb:
                     e.activation(
                         z_sb.ap, gps, AF.Sigmoid,
                         bias=bgc.ap[:, sc:sc + 1]),
                     reads=(gbuf, bgc), writes=(z_sb,))
                k.do("scalar", lambda e, sc=sc, hps=hps, g_sb=g_sb:
                     e.activation(
                         g_sb.ap, hps, AF.Relu,
                         bias=bgc.ap[:, 32 + sc:33 + sc]),
                     reads=(gbuf, bgc), writes=(g_sb,))
                k.do("scalar", lambda e, sc=sc, hps=hps, t1_sb=t1_sb:
                     e.activation(
                         t1_sb.ap, hps, AF.Relu, scale=-1.0,
                         bias=negbg.ap[:, 32 + sc:33 + sc]),
                     reads=(gbuf, negbg), writes=(t1_sb,))
                k.do("scalar", lambda e, t1_sb=t1_sb, t2_sb=t2_sb:
                     e.activation(
                         t2_sb.ap, t1_sb.ap, AF.Sigmoid, scale=-1.0),
                     reads=(t1_sb,), writes=(t2_sb,))
                k.do("vector", lambda e, g_sb=g_sb, t2_sb=t2_sb,
                     gg_sb=gg_sb: e.tensor_add(
                         gg_sb.ap, g_sb.ap, t2_sb.ap),
                     reads=(g_sb, t2_sb), writes=(gg_sb,))
                k.do("vector", lambda e, z_sb=z_sb, gg_sb=gg_sb, v_sb=v_sb:
                     e.tensor_mul(
                         v_sb.ap, z_sb.ap, gg_sb.ap),
                     reads=(z_sb, gg_sb), writes=(v_sb,))
                k.do("vector", lambda e, sc=sc, c_sb=c_sb, v_sb=v_sb,
                     h_sb=h_sb: e.tensor_tensor_scan(
                         h_sb.ap, c_sb.ap, v_sb.ap, state.ap[:, sc:sc + 1],
                         ALU.mult, ALU.add),
                     reads=(c_sb, v_sb, state), writes=(h_sb,))
                k.do("vector", lambda e, sc=sc, h_sb=h_sb: e.tensor_copy(
                    state.ap[:, sc:sc + 1], h_sb.ap[:, 511:512]),
                    reads=(h_sb,), writes=(state,))
                k.do("vector", lambda e, sc=sc, c_sb=c_sb, cp_sb=cp_sb:
                     e.tensor_tensor_scan(
                         cp_sb.ap, c_sb.ap, zt.ap,
                         cstate.ap[:, sc:sc + 1], ALU.mult, ALU.add),
                     reads=(c_sb, zt, cstate), writes=(cp_sb,))
                k.do("vector", lambda e, sc=sc, cp_sb=cp_sb: e.tensor_copy(
                    cstate.ap[:, sc:sc + 1], cp_sb.ap[:, 511:512]),
                    reads=(cp_sb,), writes=(cstate,))
                k.do("gpsimd", lambda e, sc=sc, th=th, t0=t0, h_sb=h_sb:
                     e.dma_start(
                         out=scr["sVloc"].ap[sc * 128:(sc + 1) * 128,
                                             t0 + th * 512:
                                             t0 + th * 512 + 512],
                         in_=h_sb.ap), reads=(h_sb,),
                     writes=(scr["sVloc"],), dma=True)
                k.do("gpsimd", lambda e, sc=sc, th=th, t0=t0, cp_sb=cp_sb:
                     e.dma_start(
                         out=scr["sCprod"].ap[sc * 128:(sc + 1) * 128,
                                              t0 + th * 512:
                                              t0 + th * 512 + 512],
                         in_=cp_sb.ap), reads=(cp_sb,),
                     writes=(scr["sCprod"],), dma=True)
    k.do("vector", lambda e: e.tensor_copy(carry_st.ap[:, 0:32], state.ap),
         reads=(state,), writes=(carry_st,))
    k.do("vector", lambda e: e.tensor_copy(
        carry_st.ap[:, 32:40], xn1b.ap[:, 1023:8192:1024]),
        reads=(xn1b,), writes=(carry_st,))
    if "carry_out" in scr:
        k.do("gpsimd", lambda e: e.dma_start(
            out=scr["carry_out"].ap, in_=carry_st.ap),
            reads=(carry_st,), writes=(scr["carry_out"],), dma=True)
    A.reset()
    return carry_st



def emit_p2b(k, A, inp, c, scr, carry_sb):
    A.mark()
    psA, psB, psO = c.psA, c.psB, c.psO
    sum_sb = A.alloc("gb_sum", 16384, BF16)
    vl = [A.alloc(f"gb_vl{i}", 512, BF16) for i in range(2)]
    cp = [A.alloc(f"gb_cp{i}", 512, BF16) for i in range(2)]
    xs = [A.alloc(f"gb_xs{i}", 512, BF16) for i in range(2)]
    hf = A.alloc("gb_hf", 1024)
    NW = 2
    wo_roll = [A.alloc(f"gb_wo{i}", 2048, BF16) for i in range(NW)]
    mix_st = A.alloc("gb_mix", 8192)
    mix2_st = A.alloc("gb_mix2", 4096)
    boc = A.alloc("gb_boc", 8)
    k.do("sync", lambda e: e.dma_start(out=boc.ap, in_=inp["boc"].ap),
         reads=(inp["boc"],), writes=(boc,), dma=True)

    def wo_dma(dc, slot):
        src = inp["wo_blk"].ap[dc]
        q = "sync" if os.environ.get("DEVBF16_WO") else "gpsimd"
        k.do(q, lambda e: e.dma_start(out=wo_roll[slot].ap, in_=src),
             reads=(inp["wo_blk"],), writes=(wo_roll[slot],), dma=True)

    for tile in range(4):
        t0 = tile * 1024
        for sc in range(32):
            sl = sc % 2
            k.do("sync", lambda e, sc=sc, t0=t0, b=vl[sl]: e.dma_start(
                out=b.ap, in_=scr["sVloc"].ap[sc * 128:(sc + 1) * 128,
                                              t0:t0 + 1024]),
                reads=(scr["sVloc"],), writes=(vl[sl],), dma=True)
            k.do("sync", lambda e, sc=sc, t0=t0, b=cp[sl]: e.dma_start(
                out=b.ap, in_=scr["sCprod"].ap[sc * 128:(sc + 1) * 128,
                                               t0:t0 + 1024]),
                reads=(scr["sCprod"],), writes=(cp[sl],), dma=True)
            k.do("sync", lambda e, sc=sc, t0=t0, b=xs[sl]: e.dma_start(
                out=b.ap, in_=scr["sXSK"].ap[sc * 128:(sc + 1) * 128,
                                             t0:t0 + 1024]),
                reads=(scr["sXSK"],), writes=(xs[sl],), dma=True)
            k.do("vector", lambda e, sc=sc, sl=sl: e.scalar_tensor_tensor(
                hf.ap, cp[sl].ap, carry_sb.ap[:, sc:sc + 1], vl[sl].ap,
                ALU.mult, ALU.add),
                reads=(cp[sl], vl[sl], carry_sb), writes=(hf,))
            k.do("vector", lambda e, sc=sc, sl=sl: e.tensor_add(
                sum_sb.ap[:, sc * 1024:(sc + 1) * 1024], hf.ap, xs[sl].ap),
                reads=(hf, xs[sl]), writes=(sum_sb,))
        k.flush("vector", sum_sb, hf.ap[:, 0:1])
        if "dbg_sum" in scr:
            k.do("gpsimd", lambda e, t0=t0: e.dma_start(
                out=scr["dbg_sum"].ap[:, t0:t0 + 1024].rearrange(
                    "(ec d) t -> d ec t", d=128),
                in_=sum_sb.ap.rearrange("p (ec t) -> p ec t", ec=32)),
                reads=(sum_sb,), writes=(scr["dbg_sum"],), dma=True)
        if tile == 0:
            wo_dma(0, 0)
        for dc in range(8):
            w = wo_roll[dc % NW]
            if dc < 7:
                wo_dma(dc + 1, (dc + 1) % NW)
            elif tile < 3:
                wo_dma(0, 0)
            for th in range(2):
                pso_off = (th % 2) * 512
                for ec in range(32):
                    k.do("tensor", lambda e, ec=ec, th=th, w=w,
                         pso_off=pso_off: e.matmul(
                             psO.ap[:, pso_off:pso_off + 512],
                             w.ap[:, ec * 128:(ec + 1) * 128],
                             sum_sb.ap[:, ec * 1024 + th * 512:
                                       ec * 1024 + th * 512 + 512],
                             start=(ec == 0), stop=(ec == 31),
                             skip_group_check=True),
                         reads=(w, sum_sb), writes=(psO,))
                k.do("vector", lambda e, dc=dc, th=th, pso_off=pso_off:
                     e.tensor_single_scalar(
                         mix_st.ap[:, dc * 1024 + th * 512:
                                   dc * 1024 + th * 512 + 512],
                         psO.ap[:, pso_off:pso_off + 512],
                         boc.ap[:, dc:dc + 1], ALU.add),
                     reads=(psO, boc), writes=(mix_st,))
                if os.environ.get("DEVSER_P2B"):
                    k.barrier()
                if "dbg_mix2" in scr and th == 0:
                    k.do("vector", lambda e, dc=dc, pso_off=pso_off:
                         e.tensor_single_scalar(
                             mix2_st.ap[:, dc * 512:(dc + 1) * 512],
                             psO.ap[:, pso_off:pso_off + 512],
                             boc.ap[:, dc:dc + 1], ALU.add),
                         reads=(psO, boc), writes=(mix2_st,))
        k.do("gpsimd", lambda e, t0=t0: e.dma_start(
            out=scr["sMix"].ap.rearrange("(dc d) t -> d dc t", d=128)[
                :, :, t0:t0 + 1024],
            in_=mix_st.ap.rearrange("p (dc t) -> p dc t", dc=8)),
            reads=(mix_st,), writes=(scr["sMix"],), dma=True)
        if "dbg_mix2" in scr:
            k.do("gpsimd", lambda e, t0=t0: e.dma_start(
                out=scr["dbg_mix2"].ap.rearrange("(dc d) t -> d dc t",
                                                 d=128)[:, :,
                                                        t0 // 2:t0 // 2 + 512],
                in_=mix2_st.ap.rearrange("p (dc t) -> p dc t", dc=8)),
                reads=(mix2_st,), writes=(scr["dbg_mix2"],), dma=True)
    A.reset()



def emit_p3(k, A, inp, c, scr, carry_sb):
    A.mark()
    psA, psB, psO = c.psA, c.psB, c.psO
    wr_bf = A.alloc("ff_wr", 4096, BF16)
    wk_bf = A.alloc("ff_wk", 4096, BF16)
    wv_bf = A.alloc("ff_wv", 4096, BF16)
    for nm, b in (("wrT", wr_bf), ("wkkT", wk_bf), ("wvvT", wv_bf)):
        src = inp[nm].ap.rearrange("(kc p) h -> p kc h", p=128)
        dst = b.ap.rearrange("p (kc h) -> p kc h", kc=8)
        k.do("gpsimd", lambda e, src=src, dst=dst: e.dma_start(
            out=dst, in_=src), reads=(inp[nm],), writes=(b,), dma=True)
    mur = A.alloc("ff_mur", 8)
    muk = A.alloc("ff_muk", 8)
    omr = A.alloc("ff_omr", 8)
    omk = A.alloc("ff_omk", 8)
    for nm, b in (("murc", mur), ("mukc", muk)):
        k.do("sync", lambda e, nm=nm, b=b: e.dma_start(
            out=b.ap, in_=inp[nm].ap), reads=(inp[nm],), writes=(b,),
            dma=True)
    k.do("vector", lambda e: e.tensor_scalar(
        omr.ap, mur.ap, -1.0, 1.0, ALU.mult, ALU.add),
        reads=(mur,), writes=(omr,))
    k.do("vector", lambda e: e.tensor_scalar(
        omk.ap, muk.ap, -1.0, 1.0, ALU.mult, ALU.add),
        reads=(muk,), writes=(omk,))
    xn1s = A.alloc("ff_xn1", 2056, BF16)
    CW = 514
    x1t = A.alloc("ff_x1", 4096)
    mixt = A.alloc("ff_mix", 4096)
    xr_sb = A.alloc("ff_xr", 2048, BF16)
    xk_sb = A.alloc("ff_xk", 2048, BF16)
    r_sb = A.alloc("ff_r", 4096)
    kact = A.alloc("ff_kact", 2048, BF16)
    ktmp = A.alloc("ff_ktmp", 512)
    t1 = A.alloc("ff_t1", 512)
    ffnc = A.alloc("ff_ffnc", 512)
    ut = A.alloc("ff_ut", 512)
    outt = A.alloc("ff_out", 4096)
    out_deps = []

    for tile in range(8):
        t0 = tile * 512
        if tile == 0:
            k.do("sync", lambda e: e.dma_start(
                out=xn1s.ap.rearrange("p (kc t) -> p kc t", kc=8)[
                    :, :, 1:513],
                in_=scr["sXN1"].ap.rearrange("(kc d) t -> d kc t", d=128)[
                    :, :, 0:512]),
                reads=(scr["sXN1"],), writes=(xn1s,), dma=True)
            k.do("vector", lambda e: e.tensor_copy(
                xn1s.ap[:, 0:8 * CW:CW], carry_sb.ap[:, 32:40]),
                reads=(carry_sb,), writes=(xn1s,))
        else:
            k.do("sync", lambda e, t0=t0: e.dma_start(
                out=xn1s.ap.rearrange("p (kc t) -> p kc t", kc=8)[
                    :, :, 0:513],
                in_=scr["sXN1"].ap.rearrange("(kc d) t -> d kc t", d=128)[
                    :, :, t0 - 1:t0 + 512]),
                reads=(scr["sXN1"],), writes=(xn1s,), dma=True)
        k.do("sync", lambda e, t0=t0: e.dma_start(
            out=x1t.ap.rearrange("p (m t) -> p m t", m=8),
            in_=scr["sX1"].ap.rearrange("(m d) t -> d m t", d=128)[
                :, :, t0:t0 + 512]),
            reads=(scr["sX1"],), writes=(x1t,), dma=True)
        k.do("sync", lambda e, t0=t0: e.dma_start(
            out=mixt.ap.rearrange("p (m t) -> p m t", m=8),
            in_=scr["sMix"].ap.rearrange("(m d) t -> d m t", d=128)[
                :, :, t0:t0 + 512]),
            reads=(scr["sMix"],), writes=(mixt,), dma=True)
        for kc in range(8):
            prevv = xn1s.ap[:, kc * CW:kc * CW + 512]
            curr = xn1s.ap[:, kc * CW + 1:kc * CW + 513]
            for mu, om, dst in ((mur, omr, xr_sb), (muk, omk, xk_sb)):
                k.do("vector", lambda e, kc=kc, om=om, prevv=prevv:
                     e.tensor_single_scalar(
                         t1.ap, prevv, om.ap[:, kc:kc + 1], ALU.mult),
                     reads=(xn1s, om), writes=(t1,))
                k.do("vector", lambda e, kc=kc, mu=mu, curr=curr, dst=dst:
                     e.scalar_tensor_tensor(
                         dst.ap[:, kc * 512:(kc + 1) * 512], curr,
                         mu.ap[:, kc:kc + 1], t1.ap, ALU.mult, ALU.add),
                     reads=(xn1s, mu, t1), writes=(dst,))
        for dc in range(8):
            off = 0 if dc % 2 == 0 else 512
            for kc in range(8):
                k.do("tensor", lambda e, kc=kc, dc=dc, off=off: e.matmul(
                    psA.ap[:, off:off + 512],
                    wr_bf.ap[:, kc * 1024 + dc * 128:
                             kc * 1024 + (dc + 1) * 128],
                    xr_sb.ap[:, kc * 512:(kc + 1) * 512],
                    start=(kc == 0), stop=(kc == 7), skip_group_check=True),
                    reads=(wr_bf, xr_sb), writes=(psA,))
            k.do("scalar", lambda e, dc=dc, off=off: e.activation(
                r_sb.ap[:, dc * 512:(dc + 1) * 512], psA.ap[:, off:off + 512],
                AF.Sigmoid), reads=(psA,), writes=(r_sb,))
            for kc in range(8):
                k.do("tensor", lambda e, kc=kc, dc=dc: e.matmul(
                    psB.ap[:, 0:512],
                    wk_bf.ap[:, kc * 1024 + dc * 128:
                             kc * 1024 + (dc + 1) * 128],
                    xk_sb.ap[:, kc * 512:(kc + 1) * 512],
                    start=(kc == 0), stop=(kc == 7), skip_group_check=True),
                    reads=(wk_bf, xk_sb), writes=(psB,))
            k.do("scalar", lambda e: e.activation(
                ktmp.ap, psB.ap[:, 0:512], AF.Relu),
                reads=(psB,), writes=(ktmp,))
            k.do("scalar", lambda e, dc=dc: e.activation(
                kact.ap[:, dc * 512:(dc + 1) * 512], ktmp.ap, AF.Square),
                reads=(ktmp,), writes=(kact,))
        for dc in range(8):
            slot = dc % 4
            for kc in range(8):
                k.do("tensor", lambda e, kc=kc, dc=dc, slot=slot: e.matmul(
                    psO.ap[:, slot * 512:(slot + 1) * 512],
                    wv_bf.ap[:, kc * 1024 + dc * 128:
                             kc * 1024 + (dc + 1) * 128],
                    kact.ap[:, kc * 512:(kc + 1) * 512],
                    start=(kc == 0), stop=(kc == 7), skip_group_check=True),
                    reads=(wv_bf, kact), writes=(psO,))
            k.do("vector", lambda e, dc=dc, slot=slot: e.tensor_mul(
                ffnc.ap, psO.ap[:, slot * 512:(slot + 1) * 512],
                r_sb.ap[:, dc * 512:(dc + 1) * 512]),
                reads=(psO, r_sb), writes=(ffnc,))
            k.do("vector", lambda e, dc=dc: e.tensor_add(
                ut.ap, x1t.ap[:, dc * 512:(dc + 1) * 512],
                mixt.ap[:, dc * 512:(dc + 1) * 512]),
                reads=(x1t, mixt), writes=(ut,))
            k.do("vector", lambda e, dc=dc: e.tensor_add(
                outt.ap[:, dc * 512:(dc + 1) * 512], ut.ap, ffnc.ap),
                reads=(ut, ffnc), writes=(outt,))
        dep = k.do("gpsimd", lambda e, t0=t0: e.dma_start(
            out=scr["outT"].ap.rearrange("(m d) t -> d m t", d=128)[
                :, :, t0:t0 + 512],
            in_=outt.ap.rearrange("p (m t) -> p m t", m=8)),
            reads=(outt,), writes=(scr["outT"],), dma=True)
        out_deps.append(dep)
    A.reset()
    return out_deps



def emit_exchange(k, A, c, scr, carry_st, carry_sb):
    zt40 = A.alloc("xc_z40", 40, F32)
    k.do("vector", lambda e: e.memset(zt40.ap, 0.0), writes=(zt40,))
    for blk_i in range(3):
        k.do("gpsimd", lambda e, blk_i=blk_i: e.dma_start(
            out=scr["cc_in"].ap[blk_i * 128:(blk_i + 1) * 128, :],
            in_=zt40.ap), reads=(zt40,), writes=(scr["cc_in"],), dma=True)

    def own_rows(e):
        par = e.compute_val(e.partition_id() % 2)
        return scr["cc_in"].ap[bass.ds(par * 128 + 128, 128), :]
    k.do("gpsimd", lambda e: e.dma_start(out=own_rows(e), in_=carry_st.ap),
         reads=(carry_st,), writes=(scr["cc_in"],), dma=True)
    k.do("gpsimd", lambda e: e.collective_compute(
        "AllReduce", ALU.add,
        replica_groups=[[0, 1], [2, 3], [4, 5], [6, 7]],
        ins=[scr["cc_in"].ap], outs=[scr["cc_out"].ap]),
        reads=(scr["cc_in"],), writes=(scr["cc_out"],), dma=True)

    def my_rows(e):
        par = e.compute_val(e.partition_id() % 2)
        return scr["cc_out"].ap[bass.ds(par * 128, 128), :]
    k.do("sync", lambda e: e.dma_start(out=carry_sb.ap, in_=my_rows(e)),
         reads=(scr["cc_out"],), writes=(carry_sb,), dma=True)



ARENA_W = 52900

SCRATCH_SPECS = {
    "sQT": ([D, HALF], BF16), "sKT": ([D, S], BF16), "sV": ([S, D], BF16),
    "sXT": ([D, HALF], F32), "sXN": ([D, HALF], BF16),
    "sCT": ([8, HALF], BF16), "sMoE": ([D, HALF], F32),
    "sOA": ([D, HALF], F32), "sOB": ([D, HALF], F32),
    "sRA": ([2, HALF], F32), "sRB": ([2, HALF], F32),
    "sX1": ([D, HALF], F32), "sXN1": ([D, HALF], BF16),
    "sXSK": ([E1, HALF], BF16), "sVloc": ([E1, HALF], BF16),
    "sCprod": ([E1, HALF], BF16), "carry_out": ([128, 40], F32),
    "carry_in": ([128, 40], F32), "sMix": ([D, HALF], F32),
    "cc_in": ([384, 40], F32), "cc_out": ([384, 40], F32),
    "dbg_sum": ([E1, HALF], BF16), "dbg_mix2": ([D, HALF // 2], F32),
    "outT": ([D, HALF], F32),
}

LAUNCH_DEFS = {
    4: dict(inputs=L1_INPUTS | L2_INPUTS,
            scr_in=set(),
            scr_out={"sXN1", "sXSK", "sVloc", "sCprod", "carry_out", "sX1"},
            scr_int={"sQT", "sKT", "sV", "sXT", "sXN", "sCT", "sMoE",
                     "sOA", "sOB", "sRA", "sRB"}),
    0: dict(inputs=L1_INPUTS | L2_INPUTS | L3_INPUTS,
            scr_in=set(),
            scr_out={"outT"},
            scr_int={"sQT", "sKT", "sV", "sXT", "sXN", "sCT", "sMoE",
                     "sOA", "sOB", "sRA", "sRB", "sX1", "sXN1", "sXSK",
                     "sVloc", "sCprod", "sMix", "cc_in", "cc_out"}),
    1: dict(inputs=L1_INPUTS,
            scr_in=set(),
            scr_out={"sX1"},
            scr_int={"sQT", "sKT", "sV", "sXT", "sXN", "sCT", "sMoE",
                     "sOA", "sOB", "sRA", "sRB"}),
    2: dict(inputs=L2_INPUTS,
            scr_in={"sX1"},
            scr_out={"sXN1", "sXSK", "sVloc", "sCprod", "carry_out"},
            scr_int=set()),
    3: dict(inputs=L3_INPUTS,
            scr_in={"sX1", "sXN1", "sXSK", "sVloc", "sCprod", "carry_in"},
            scr_out={"outT"},
            scr_int={"sMix"}),
}


def build_launch(which, dbg=(), phases=None):
    import contextlib
    nc = bass.Bass("TRN2", target_bir_lowering=False, debug=False)
    stack = contextlib.ExitStack()
    k = K(nc, stack)
    A = Arena(k, ARENA_W)
    ldef = LAUNCH_DEFS[which]
    inp = {}
    for name, shape, dt in INPUT_SPECS:
        if name in ldef["inputs"]:
            inp[name] = k.dram(name, shape, dt, "ExternalInput")
    scr = {}
    for name in (ldef["scr_in"] | ldef["scr_out"] | ldef["scr_int"]
                 | set(dbg)):
        shape, dt = SCRATCH_SPECS[name]
        if name in ldef["scr_in"]:
            kind = "ExternalInput"
        elif name in ldef["scr_out"] or name in dbg:
            kind = "ExternalOutput"
        else:
            kind = "Internal"
        scr[name] = k.dram(name, shape, dt, kind)
    c = emit_common(k, A, inp, want_sel8=(which in (0, 1, 4)))
    if which == 0:
        build_l0_consts(k, A, inp, c)
        emit_p0a(k, A, inp, c, scr)
        A.reset()
        emit_p0b(k, A, inp, c, scr)
        A.reset()
        emit_p1(k, A, inp, c, scr)
        emit_p1c(k, A, inp, c, scr)
        carry_st = A.alloc("carry_st", 40, F32)
        carry_sb = A.alloc("carry_sb", 40, F32)
        emit_p2a(k, A, inp, c, scr, carry_st)
        emit_exchange(k, A, c, scr, carry_st, carry_sb)
        emit_p2b(k, A, inp, c, scr, carry_sb)
        emit_p3(k, A, inp, c, scr, carry_sb)
    elif which == 4:
        build_l0_consts(k, A, inp, c)
        emit_p0a(k, A, inp, c, scr)
        A.reset()
        emit_p0b(k, A, inp, c, scr)
        A.reset()
        emit_p1(k, A, inp, c, scr)
        emit_p1c(k, A, inp, c, scr)
        carry_st = A.alloc("carry_st", 40, F32)
        emit_p2a(k, A, inp, c, scr, carry_st)
    elif which == 1:
        ph = phases or ("p0a", "p0b", "p1", "p1c")
        build_l0_consts(k, A, inp, c)
        if "p0a" in ph:
            emit_p0a(k, A, inp, c, scr)
        A.reset()
        if "p0b" in ph:
            emit_p0b(k, A, inp, c, scr)
        A.reset()
        if "p1" in ph:
            emit_p1(k, A, inp, c, scr)
        if "p1c" in ph:
            emit_p1c(k, A, inp, c, scr)
    elif which == 2:
        carry_st = A.alloc("carry_st", 40, F32)
        emit_p2a(k, A, inp, c, scr, carry_st)
    else:
        carry_sb = A.alloc("carry_sb", 40, F32)
        k.do("sync", lambda e: e.dma_start(
            out=carry_sb.ap, in_=scr["carry_in"].ap),
            reads=(scr["carry_in"],), writes=(carry_sb,), dma=True)
        ph = phases or ("p2b", "p3")
        if "p2b" in ph:
            emit_p2b(k, A, inp, c, scr, carry_sb)
        if "p3" in ph:
            emit_p3(k, A, inp, c, scr, carry_sb)
    k.barrier()
    k.finish([(s, v) for s, v in k.cnt.items() if v > 0])
    k.emit_block()
    stack.close()
    return nc


MODE = os.environ.get("KERNEL_MODE", "2L")


def kernel(**inputs):
    maps = prep_inputs(inputs)
    cores = list(range(NCORE))
    if MODE == "2L":
        nc1 = build_launch(4)
        names = sorted(LAUNCH_DEFS[4]["inputs"])
        m1 = [{n: maps[c][n] for n in names} for c in cores]
        r1 = run_bass_kernel_spmd(nc1, m1, core_ids=cores)
        nc2 = build_launch(3)
        zc = np.zeros((128, 40), np.float32)
        m3 = []
        for c in cores:
            carry = (np.asarray(r1.results[c - 1]["carry_out"])
                     if c % 2 == 1 else zc)
            m3.append(dict({n: maps[c][n] for n in sorted(L3_INPUTS)},
                           sX1=r1.results[c]["sX1"],
                           sXN1=r1.results[c]["sXN1"],
                           sXSK=r1.results[c]["sXSK"],
                           sVloc=r1.results[c]["sVloc"],
                           sCprod=r1.results[c]["sCprod"],
                           carry_in=carry))
        r3 = run_bass_kernel_spmd(nc2, m3, core_ids=cores)
        out = np.empty((B, S, D), np.float32)
        for c in cores:
            b, r = c // 2, c % 2
            out[b, r * HALF:(r + 1) * HALF] = \
                np.asarray(r3.results[c]["outT"]).T
        return out
    if MODE == "fused":
        nc = build_launch(0)
        names = sorted(LAUNCH_DEFS[0]["inputs"])
        m = [{n: maps[c][n] for n in names} for c in cores]
        res = run_bass_kernel_spmd(nc, m, core_ids=cores)
        out = np.empty((B, S, D), np.float32)
        for c in cores:
            b, r = c // 2, c % 2
            out[b, r * HALF:(r + 1) * HALF] = \
                np.asarray(res.results[c]["outT"]).T
        return out
    nc1 = build_launch(1)
    m1 = [{n: maps[c][n] for n in sorted(L1_INPUTS)} for c in cores]
    r1 = run_bass_kernel_spmd(nc1, m1, core_ids=cores)
    nc2 = build_launch(2)
    m2 = [dict({n: maps[c][n] for n in sorted(L2_INPUTS)},
               sX1=r1.results[c]["sX1"]) for c in cores]
    r2 = run_bass_kernel_spmd(nc2, m2, core_ids=cores)
    nc3 = build_launch(3)
    zc = np.zeros((128, 40), np.float32)
    m3 = []
    for c in cores:
        carry = r2.results[c - 1]["carry_out"] if c % 2 == 1 else zc
        m3.append(dict({n: maps[c][n] for n in sorted(L3_INPUTS)},
                       sX1=r1.results[c]["sX1"],
                       sXN1=r2.results[c]["sXN1"],
                       sXSK=r2.results[c]["sXSK"],
                       sVloc=r2.results[c]["sVloc"],
                       sCprod=r2.results[c]["sCprod"],
                       carry_in=carry))
    r3 = run_bass_kernel_spmd(nc3, m3, core_ids=cores)
    out = np.empty((B, S, D), np.float32)
    for c in cores:
        b, r = c // 2, c % 2
        out[b, r * HALF:(r + 1) * HALF] = \
            np.asarray(r3.results[c]["outT"]).T
    return out


# version 04
# speedup vs baseline: 2.3870x; 2.3325x over previous
import math
import os
import numpy as np
import ml_dtypes
import concourse.bass as bass
import concourse.mybir as mybir
from concourse.bass_utils import run_bass_kernel_spmd

F32 = mybir.dt.float32
BF16 = mybir.dt.bfloat16
AF = mybir.ActivationFunctionType
ALU = mybir.AluOpType
AX = mybir.AxisListType

B, S, D = 4, 8192, 1024
HALF = S // 2
NCORE = 8
EXP, IMOE, GUF = 8, 2048, 4096
E1, F1 = 4096, 8192
SCALE = 1.0 / math.sqrt(512.0)

ENGINES = ("sync", "scalar", "vector", "tensor", "gpsimd")


class Buf:
    __slots__ = ("name", "ap", "last_write", "readers")

    def __init__(self, name, ap):
        self.name = name
        self.ap = ap
        self.last_write = None
        self.readers = []


class K:

    NLANE = 8

    def __init__(self, nc, stack):
        self.nc = nc
        self.stack = stack
        self.plan = {e: [] for e in ENGINES}
        self.sem_handles = {
            e: stack.enter_context(nc.semaphore(f"sem_{e}")) for e in ENGINES}
        for q in ("sync", "scalar", "gpsimd", "vector"):
            for l in range(self.NLANE):
                nm = f"{q}_d{l}"
                self.sem_handles[nm] = stack.enter_context(nc.semaphore(nm))
        self.cnt = {s: 0 for s in self.sem_handles}
        self.dma_rr = {q: 0 for q in ("sync", "scalar", "gpsimd", "vector")}
        self.waited = {e: {} for e in ENGINES}

    def sbuf(self, name, shape, dtype):
        h = self.stack.enter_context(self.nc.sbuf_tensor(name, shape, dtype))
        return Buf(name, h[:, :])

    def psum(self, name, shape, dtype):
        h = self.stack.enter_context(self.nc.psum_tensor(name, shape, dtype))
        return Buf(name, h[:, :])

    def dram(self, name, shape, dtype, kind):
        return Buf(name, self.nc.dram_tensor(name, shape, dtype,
                                             kind=kind).ap())

    def do(self, eng, fn, reads=(), writes=(), dma=False):
        inc = 16 if dma else 1
        if dma:
            lane = self.dma_rr[eng]
            self.dma_rr[eng] = (lane + 1) % self.NLANE
            sig = f"{eng}_d{lane}"
        else:
            sig = eng
        deps = []
        for r in reads:
            if r.last_write is not None:
                deps.append(r.last_write)
        for w in writes:
            if w.last_write is not None:
                deps.append(w.last_write)
            deps.extend(w.readers)
        waited = self.waited[eng]
        need = {}
        for dsem, dval in deps:
            if dval > need.get(dsem, 0):
                need[dsem] = dval
        todo = []
        for dsem, dval in sorted(need.items()):
            if dsem == eng == "tensor":
                continue
            if waited.get(dsem, 0) < dval:
                todo.append((dsem, dval))
                waited[dsem] = dval
        self.cnt[sig] += inc
        val = self.cnt[sig]
        handles = self.sem_handles

        def emit(e, todo=tuple(todo), fn=fn, inc=inc, sig=sig):
            for dsem, dval in todo:
                e.wait_ge(handles[dsem], dval)
            fn(e).then_inc(handles[sig], inc)
        self.plan[eng].append(emit)
        dep = (sig, val)
        for w in writes:
            w.last_write = dep
            w.readers = []
        for r in reads:
            r.readers.append(dep)
        return dep

    def flush(self, eng, buf, scratch_ap):
        dep = self.do(eng, lambda e: e.tensor_copy(scratch_ap, scratch_ap),
                      reads=(buf,), writes=())
        buf.last_write = dep
        return dep

    def barrier(self):
        handles = self.sem_handles
        snap = dict(self.cnt)
        for eng in ENGINES:
            todo = []
            for s, v in snap.items():
                if s == eng or v == 0:
                    continue
                if self.waited[eng].get(s, 0) < v:
                    todo.append((s, v))
                    self.waited[eng][s] = v
            if todo:
                def emit(e, todo=tuple(todo)):
                    for ds, dv in todo:
                        e.wait_ge(handles[ds], dv)
                self.plan[eng].append(emit)

    def finish(self, deps):
        handles = self.sem_handles

        def emit(e, deps=tuple(deps)):
            for dsem, dval in deps:
                e.wait_ge(handles[dsem], dval)
        self.plan["gpsimd"].append(emit)

    def emit_block(self):
        with self.nc.Block() as block:
            @block.sync
            def _(eng):
                for f in self.plan["sync"]:
                    f(eng)

            @block.scalar
            def _(eng):
                for f in self.plan["scalar"]:
                    f(eng)

            @block.vector
            def _(eng):
                for f in self.plan["vector"]:
                    f(eng)

            @block.tensor
            def _(eng):
                for f in self.plan["tensor"]:
                    f(eng)

            @block.gpsimd
            def _(eng):
                for f in self.plan["gpsimd"]:
                    f(eng)


class Arena:
    def __init__(self, k, words):
        self.k = k
        self.t = k.stack.enter_context(
            k.nc.sbuf_tensor("arena", [128, words], F32))
        self.words = words
        self.off = 0
        self.watermark = 0
        self.marks = []

    def alloc(self, name, words, dtype=F32, parts=128):
        w = int(words)
        assert self.off + w <= self.words, (
            f"arena overflow: {name} needs {w} at {self.off}/{self.words}")
        ap = self.t[0:parts, self.off:self.off + w]
        if dtype == BF16:
            ap = ap.bitcast(BF16)
        self.off += w
        self.watermark = max(self.watermark, self.off)
        return Buf(name, ap)

    def mark(self):
        self.marks.append(self.off)

    def reset(self):
        self.k.barrier()
        self.off = self.marks.pop()



def _rope_table():
    pos = np.arange(S, dtype=np.float32)
    inv = np.float32(10000.0) ** (
        np.arange(0, 512, 2, dtype=np.float32) / np.float32(512.0))
    radians = (pos[:, None] / inv[None, :]).astype(np.float32)
    sincos = np.stack(
        [np.sin(radians), np.cos(radians)], axis=-1).reshape(S, 512)
    return np.repeat(sincos, 2, axis=-1).astype(np.float32)


def _colchunks(v, nchunk):
    return np.ascontiguousarray(np.asarray(v).reshape(nchunk, 128).T)


def prep_inputs(inp):
    g = {k: np.asarray(v) for k, v in inp.items()}
    x = g["L0_attn_hidden_states"]
    tT = np.ascontiguousarray(_rope_table().T)
    eye_bf = np.eye(128, dtype=np.float32).astype(ml_dtypes.bfloat16)
    sel8 = np.zeros((8, 1024), dtype=np.float32)
    for e in range(8):
        sel8[e, 128 * e:128 * e + 128] = 1.0
    shared = {
        "wqT": np.ascontiguousarray(g["L0_attn_Wq"].T),
        "wkT": np.ascontiguousarray(g["L0_attn_Wk"].T),
        "wvT": np.ascontiguousarray(g["L0_attn_Wv"].T),
        "bq_row": np.ascontiguousarray(g["L0_attn_bq"].reshape(1, D)),
        "bk_row": np.ascontiguousarray(g["L0_attn_bk"].reshape(1, D)),
        "bv_row": np.ascontiguousarray(g["L0_attn_bv"].reshape(1, D)),
        "routerT": np.ascontiguousarray(g["L0_ffn_router_weight"].T),
        "rbias_row": np.ascontiguousarray(
            g["L0_ffn_router_bias"].reshape(1, EXP)),
        "wgu_blk": np.ascontiguousarray(
            g["L0_ffn_gate_up_proj"].transpose(0, 2, 1)
            .reshape(EXP, 8, 128, 32, 128).transpose(0, 3, 2, 1, 4)
            .reshape(EXP, 32, 128, 1024)),
        "gub_col": np.ascontiguousarray(
            g["L0_ffn_gate_up_proj_bias"].reshape(EXP, 32, 128)
            .transpose(0, 2, 1)),
        "dwn_blk": np.ascontiguousarray(
            g["L0_ffn_down_proj"].transpose(0, 2, 1)
            .reshape(EXP, 16, 128, 1024).transpose(0, 2, 1, 3)
            .reshape(EXP, 128, 16384)),
        "db8": np.ascontiguousarray(g["L0_ffn_down_proj_bias"]),
        "w1_blk": np.ascontiguousarray(
            g["L1_attn_W1"].T.reshape(8, 128, 32, 128).transpose(2, 1, 0, 3)
            .reshape(32, 128, 1024)),
        "w2_blk": np.ascontiguousarray(
            g["L1_attn_W2"].T.reshape(8, 128, 32, 128).transpose(2, 1, 0, 3)
            .reshape(32, 128, 1024)),
        "b1c": _colchunks(g["L1_attn_b1"], 32),
        "b2c": _colchunks(g["L1_attn_b2"], 32),
        "wg_blk": np.ascontiguousarray(
            g["L1_attn_Wg"].T.reshape(32, 128, 64, 128).transpose(2, 1, 0, 3)
            .reshape(64, 128, 4096)),
        "bgc": _colchunks(g["L1_attn_bg"], 64),
        "wo_blk": np.ascontiguousarray(
            g["L1_attn_Wo"].T.reshape(32, 128, 8, 128).transpose(2, 1, 0, 3)
            .reshape(8, 128, 4096)).astype(
                ml_dtypes.bfloat16 if os.environ.get("DEVBF16_WO")
                else np.float32),
        "boc": _colchunks(g["L1_attn_bo"], 8),
        "wrT": np.ascontiguousarray(g["L1_ffn_W_r"].T),
        "wkkT": np.ascontiguousarray(g["L1_ffn_W_k"].T),
        "wvvT": np.ascontiguousarray(g["L1_ffn_W_v"].T),
        "murc": _colchunks(g["L1_ffn_mu_r"], 8),
        "mukc": _colchunks(g["L1_ffn_mu_k"], 8),
        "eye_bf": eye_bf,
        "sel8": sel8,
    }
    maps = []
    for c in range(NCORE):
        b, r = c // 2, c % 2
        own = slice(r * HALF, (r + 1) * HALF)
        oth = slice((1 - r) * HALF, (2 - r) * HALF)
        m = dict(shared)
        m["x_own"] = np.ascontiguousarray(x[b, own])
        m["x_oth"] = np.ascontiguousarray(x[b, oth])
        m["tT_own"] = np.ascontiguousarray(tT[:, own])
        m["tT_oth"] = np.ascontiguousarray(tT[:, oth])
        maps.append(m)
    return maps


INPUT_SPECS = [
    ("x_own", [HALF, D], F32), ("x_oth", [HALF, D], F32),
    ("tT_own", [D, HALF], F32), ("tT_oth", [D, HALF], F32),
    ("wqT", [D, D], F32), ("wkT", [D, D], F32), ("wvT", [D, D], F32),
    ("bq_row", [1, D], F32), ("bk_row", [1, D], F32), ("bv_row", [1, D], F32),
    ("routerT", [D, EXP], F32), ("rbias_row", [1, EXP], F32),
    ("wgu_blk", [EXP, 32, 128, 1024], F32), ("gub_col", [EXP, 128, 32], F32),
    ("dwn_blk", [EXP, 128, 16384], F32), ("db8", [EXP, D], F32),
    ("w1_blk", [32, 128, 1024], F32), ("w2_blk", [32, 128, 1024], F32),
    ("b1c", [128, 32], F32), ("b2c", [128, 32], F32),
    ("wg_blk", [64, 128, 4096], F32), ("bgc", [128, 64], F32),
    ("wo_blk", [8, 128, 4096],
     BF16 if os.environ.get("DEVBF16_WO") else F32),
    ("boc", [128, 8], F32),
    ("wrT", [D, D], F32), ("wkkT", [D, D], F32), ("wvvT", [D, D], F32),
    ("murc", [128, 8], F32), ("mukc", [128, 8], F32),
    ("eye_bf", [128, 128], BF16), ("sel8", [8, 1024], F32),
]

L1_INPUTS = {"x_own", "x_oth", "tT_own", "tT_oth", "wqT", "wkT", "wvT",
             "bq_row", "bk_row", "bv_row", "routerT", "rbias_row", "wgu_blk",
             "gub_col", "dwn_blk", "db8", "eye_bf", "sel8"}
L2_INPUTS = {"w1_blk", "w2_blk", "b1c", "b2c", "wg_blk", "bgc", "eye_bf",
             "sel8"}
L3_INPUTS = {"wo_blk", "boc", "wrT", "wkkT", "wvvT", "murc", "mukc", "eye_bf",
             "sel8"}



class Ctx:
    pass


def emit_common(k, A, inp, want_sel8=True):
    c = Ctx()
    c.psA = k.psum("psA", [128, 1024], F32)
    c.psB = k.psum("psB", [128, 512], F32)
    c.psT = k.psum("psT", [128, 512], BF16)
    c.psO = k.psum("psO", [128, 2048], F32)

    c.eye_bf = A.alloc("eye_bf", 64, BF16)
    c.ones_bf = A.alloc("ones_bf", 64, BF16)
    c.ones_f32 = A.alloc("ones_f32", 128, F32)
    c.eps_col = A.alloc("eps_col", 1, F32)

    k.do("sync", lambda e: e.dma_start(out=c.eye_bf.ap, in_=inp["eye_bf"].ap),
         reads=(inp["eye_bf"],), writes=(c.eye_bf,), dma=True)
    k.do("vector", lambda e: e.memset(c.ones_bf.ap, 1.0), writes=(c.ones_bf,))
    k.do("vector", lambda e: e.memset(c.ones_f32.ap, 1.0),
         writes=(c.ones_f32,))
    k.do("vector", lambda e: e.memset(c.eps_col.ap, 1e-5),
         writes=(c.eps_col,))
    if want_sel8:
        c.sel8_bf = A.alloc("sel8_bf", 512, BF16)
        sel_st = A.alloc("sel_st", 1024, F32)
        k.do("sync", lambda e: e.dma_start(out=sel_st.ap[0:8, :],
                                           in_=inp["sel8"].ap),
             reads=(inp["sel8"],), writes=(sel_st,), dma=True)
        k.do("vector", lambda e: e.tensor_copy(c.sel8_bf.ap[0:8, :],
                                               sel_st.ap[0:8, :]),
             reads=(sel_st,), writes=(c.sel8_bf,))
        c.sel8_f32 = sel_st
    return c


def load_bf16_weight(k, A, inp, c, name, rows, cols):
    kc = rows // 128
    wbf = A.alloc(name + "_bf", kc * cols // 2, BF16)
    src = inp[name].ap.rearrange("(kc p) h -> p kc h", p=128)
    dst = wbf.ap.rearrange("p (kc h) -> p kc h", kc=kc)
    k.do("gpsimd", lambda e: e.dma_start(out=dst, in_=src),
         reads=(inp[name],), writes=(wbf,), dma=True)
    return wbf


def build_l0_consts(k, A, inp, c):
    A.mark()
    c.gub = A.alloc("gub", 256, F32)
    c.db8_bf = A.alloc("db8_bf", 512, BF16)
    A.mark()
    c.eye_f32 = A.alloc("eye_f32", 128, F32)
    c.wqT_bf = A.alloc("wqT_bf", 4096, BF16)
    c.wkT_bf = A.alloc("wkT_bf", 4096, BF16)
    c.wvT_bf = A.alloc("wvT_bf", 4096, BF16)
    c.bq_col = A.alloc("bq_col", 8, F32)
    c.bk_col = A.alloc("bk_col", 8, F32)
    c.bv_row_bf = A.alloc("bv_row_bf", 512, BF16)
    c.routerT_bf = A.alloc("routerT_bf", 32, BF16)
    c.rbias_row_bf = A.alloc("rbias_row_bf", 8, BF16)
    A.mark()
    stage = A.alloc("wstage", 8192, F32)
    rstage = A.alloc("rstage", 64, F32)
    rbst = A.alloc("rbst", 8, F32)
    bvraw = A.alloc("bv_raw", 1024, F32)
    dbst = A.alloc("dbst", 1024, F32)
    raws = {n: A.alloc(n + "_raw", 1024, F32) for n in ("bq_row", "bk_row")}
    flds = {n: A.alloc(n + "_fld", 1024, F32) for n in ("bq_row", "bk_row")}

    k.do("vector", lambda e: e.tensor_copy(c.eye_f32.ap, c.eye_bf.ap),
         reads=(c.eye_bf,), writes=(c.eye_f32,))
    k.do("sync", lambda e: e.dma_start(
        out=c.gub.ap.rearrange("p (ex n) -> p ex n", ex=8),
        in_=inp["gub_col"].ap.rearrange("ex p n -> p ex n")), reads=(inp["gub_col"],), writes=(c.gub,),
        dma=True)
    k.do("sync", lambda e: e.dma_start(out=dbst.ap[0:8, :],
                                       in_=inp["db8"].ap),
         reads=(inp["db8"],), writes=(dbst,), dma=True)
    k.do("vector", lambda e: e.tensor_copy(c.db8_bf.ap[0:8, :],
                                           dbst.ap[0:8, :]),
         reads=(dbst,), writes=(c.db8_bf,))
    for wname, fold in (("wqT", True), ("wkT", True), ("wvT", False)):
        wbf = getattr(c, wname + "_bf")
        src_ = inp[wname].ap.rearrange("(kc p) h -> p kc h", p=128)
        dst = stage.ap.rearrange("p (kc h) -> p kc h", kc=8)
        k.do("sync", lambda e, s=src_, d=dst: e.dma_start(out=d, in_=s),
             reads=(inp[wname],), writes=(stage,), dma=True)
        if fold:
            k.do("vector", lambda e, sv=stage.ap, bv=wbf.ap: e.tensor_sub(
                bv[:, 0:8192:2], sv[:, 0:8192:2], sv[:, 1:8192:2]),
                reads=(stage,), writes=(wbf,))
            k.do("vector", lambda e, sv=stage.ap, bv=wbf.ap: e.tensor_add(
                bv[:, 1:8192:2], sv[:, 0:8192:2], sv[:, 1:8192:2]),
                reads=(stage,), writes=(wbf,))
        else:
            k.do("vector", lambda e, sv=stage.ap, bv=wbf.ap: e.tensor_copy(
                bv, sv), reads=(stage,), writes=(wbf,))
    for bname, tgt in (("bq_row", c.bq_col), ("bk_row", c.bk_col)):
        raw, fld = raws[bname], flds[bname]
        k.do("sync", lambda e, raw=raw, s=inp[bname]: e.dma_start(
            out=raw.ap[0:1, :], in_=s.ap), reads=(inp[bname],),
            writes=(raw,), dma=True)
        k.do("vector", lambda e, raw=raw, fld=fld: e.tensor_sub(
            fld.ap[0:1, 0:1024:2], raw.ap[0:1, 0:1024:2],
            raw.ap[0:1, 1:1024:2]), reads=(raw,), writes=(fld,))
        k.do("vector", lambda e, raw=raw, fld=fld: e.tensor_add(
            fld.ap[0:1, 1:1024:2], raw.ap[0:1, 0:1024:2],
            raw.ap[0:1, 1:1024:2]), reads=(raw,), writes=(fld,))
        for m in range(8):
            k.do("tensor", lambda e, fld=fld, m=m: e.matmul(
                c.psB.ap[:, m:m + 1], fld.ap[0:1, m * 128:(m + 1) * 128],
                c.ones_f32.ap[0:1, 0:1], start=True, stop=True,
                skip_group_check=True),
                reads=(fld, c.ones_f32), writes=(c.psB,))
        k.do("vector", lambda e, tgt=tgt: e.tensor_copy(
            tgt.ap, c.psB.ap[:, 0:8]), reads=(c.psB,), writes=(tgt,))
    k.do("sync", lambda e: e.dma_start(out=bvraw.ap[0:1, :],
                                       in_=inp["bv_row"].ap),
         reads=(inp["bv_row"],), writes=(bvraw,), dma=True)
    k.do("vector", lambda e: e.tensor_copy(c.bv_row_bf.ap[0:1, :],
                                           bvraw.ap[0:1, :]),
         reads=(bvraw,), writes=(c.bv_row_bf,))
    k.do("sync", lambda e: e.dma_start(
        out=rstage.ap.rearrange("p (kc h) -> p kc h", kc=8),
        in_=inp["routerT"].ap.rearrange("(kc p) h -> p kc h", p=128)),
        reads=(inp["routerT"],), writes=(rstage,), dma=True)
    k.do("vector", lambda e: e.tensor_copy(c.routerT_bf.ap, rstage.ap),
         reads=(rstage,), writes=(c.routerT_bf,))
    k.do("sync", lambda e: e.dma_start(out=rbst.ap[0:1, 0:8],
                                       in_=inp["rbias_row"].ap),
         reads=(inp["rbias_row"],), writes=(rbst,), dma=True)
    k.do("vector", lambda e: e.tensor_copy(c.rbias_row_bf.ap[0:1, 0:8],
                                           rbst.ap[0:1, 0:8]),
         reads=(rbst,), writes=(c.rbias_row_bf,))
    A.reset()



def emit_p0a(k, A, inp, c, scr):
    A.mark()
    psA, psB, psT, psO = c.psA, c.psB, c.psT, c.psO
    xtok = [A.alloc(f"xtok{i}", 4096) for i in (0, 1)]
    ttb = [A.alloc(f"ttb{i}", 4096) for i in (0, 1)]
    xT = A.alloc("xT", 4096)
    sq = A.alloc("p0sq", 4096)
    sums = A.alloc("p0sums", 4)
    sqs = A.alloc("p0sqs", 4)
    rows_sum = A.alloc("rows_sum", 128)
    rows_sq = A.alloc("rows_sq", 128)
    rows_m = A.alloc("rows_m", 128)
    rows_msq = A.alloc("rows_msq", 128)
    rows_m2 = A.alloc("rows_m2", 128)
    rows_var = A.alloc("rows_var", 128)
    rows_rstd = A.alloc("rows_rstd", 128)
    rows_B = A.alloc("rows_B", 128)
    xn = A.alloc("p0xn", 2048, BF16)
    xnf = A.alloc("p0xnf", 512)
    qtmp = A.alloc("p0qtmp", 512)
    qk_stage = A.alloc("qk_stage", 2048, BF16)
    v_stage = A.alloc("v_stage", 2048, BF16)
    lg = A.alloc("p0lg", 8)
    srt = [A.alloc(f"p0srt{i}", 8) for i in (0, 1)]
    negm = A.alloc("p0negm", 1)
    eexp = A.alloc("p0eexp", 8)
    maskb = A.alloc("p0mask", 8)
    em = A.alloc("p0em", 8)
    ssum = A.alloc("p0ssum", 1)
    rec = A.alloc("p0rec", 1)
    comb = A.alloc("p0comb", 8)
    comb_bf = A.alloc("p0comb_bf", 4, BF16)
    combT = A.alloc("p0combT", 256, BF16)

    for tt in range(16):
        own = tt < 8
        lt = tt % 8
        t0 = lt * 512
        gk = t0 if own else HALF + t0
        xin = inp["x_own" if own else "x_oth"]
        tin = inp["tT_own" if own else "tT_oth"]
        buf = xtok[tt % 2]
        tb = ttb[tt % 2]
        k.do("sync", lambda e, xin=xin, t0=t0, buf=buf: e.dma_start(
            out=buf.ap.rearrange("p (a d) -> p a d", a=4),
            in_=xin.ap[t0:t0 + 512, :].rearrange("(a p) d -> p a d", p=128)),
            reads=(xin,), writes=(buf,), dma=True)
        k.do("sync", lambda e, tin=tin, t0=t0, tb=tb: e.dma_start(
            out=tb.ap.rearrange("p (kc t) -> p kc t", kc=8),
            in_=tin.ap[:, t0:t0 + 512].rearrange("(kc p) t -> p kc t", p=128)),
            reads=(tin,), writes=(tb,), dma=True)
        for a in range(4):
            for kc in range(8):
                k.do("tensor", lambda e, a=a, kc=kc, buf=buf: e.transpose(
                    psA.ap[:, kc * 128:(kc + 1) * 128],
                    buf.ap[:, a * 1024 + kc * 128:a * 1024 + (kc + 1) * 128],
                    c.eye_f32.ap),
                    reads=(buf, c.eye_f32), writes=(psA,))
            k.do("vector", lambda e, a=a: e.tensor_copy(
                xT.ap.rearrange("p (kc t) -> p kc t", kc=8)[
                    :, :, a * 128:(a + 1) * 128],
                psA.ap.rearrange("p (kc t) -> p kc t", kc=8)),
                reads=(psA,), writes=(xT,))
        if own:
            k.do("gpsimd", lambda e, t0=t0: e.dma_start(
                out=scr["sXT"].ap.rearrange("(kc d) t -> d kc t", d=128)[
                    :, :, t0:t0 + 512],
                in_=xT.ap.rearrange("p (kc t) -> p kc t", kc=8)),
                reads=(xT,), writes=(scr["sXT"],), dma=True)
        k.do("scalar", lambda e, buf=buf: e.activation(
            sq.ap, buf.ap, AF.Square), reads=(buf,), writes=(sq,))
        k.do("vector", lambda e, buf=buf: e.reduce_sum(
            sums.ap, buf.ap.rearrange("p (a d) -> p a d", a=4), axis=AX.X),
            reads=(buf,), writes=(sums,))
        k.do("vector", lambda e: e.reduce_sum(
            sqs.ap, sq.ap.rearrange("p (a d) -> p a d", a=4), axis=AX.X),
            reads=(sq,), writes=(sqs,))
        k.do("tensor", lambda e: e.transpose(
            psB.ap[0:4, 0:128], sums.ap, c.eye_f32.ap),
            reads=(sums, c.eye_f32), writes=(psB,))
        k.do("vector", lambda e: e.tensor_copy(
            rows_sum.ap[0:4, :], psB.ap[0:4, 0:128]),
            reads=(psB,), writes=(rows_sum,))
        k.do("tensor", lambda e: e.transpose(
            psB.ap[0:4, 0:128], sqs.ap, c.eye_f32.ap),
            reads=(sqs, c.eye_f32), writes=(psB,))
        k.do("vector", lambda e: e.tensor_copy(
            rows_sq.ap[0:4, :], psB.ap[0:4, 0:128]),
            reads=(psB,), writes=(rows_sq,))
        k.do("scalar", lambda e: e.activation(
            rows_m.ap[0:4, :], rows_sum.ap[0:4, :], AF.Copy,
            scale=1.0 / 1024.0), reads=(rows_sum,), writes=(rows_m,))
        k.do("scalar", lambda e: e.activation(
            rows_msq.ap[0:4, :], rows_sq.ap[0:4, :], AF.Copy,
            scale=1.0 / 1024.0), reads=(rows_sq,), writes=(rows_msq,))
        k.do("scalar", lambda e: e.activation(
            rows_m2.ap[0:4, :], rows_m.ap[0:4, :], AF.Square),
            reads=(rows_m,), writes=(rows_m2,))
        k.do("vector", lambda e: e.tensor_sub(
            rows_var.ap[0:4, :], rows_msq.ap[0:4, :], rows_m2.ap[0:4, :]),
            reads=(rows_msq, rows_m2), writes=(rows_var,))
        k.do("scalar", lambda e: e.activation(
            rows_var.ap[0:4, :], rows_var.ap[0:4, :], AF.Ln,
            bias=c.eps_col.ap[0:4, 0:1]),
            reads=(rows_var, c.eps_col), writes=(rows_var,))
        k.do("scalar", lambda e: e.activation(
            rows_rstd.ap[0:4, :], rows_var.ap[0:4, :], AF.Exp, scale=-0.5),
            reads=(rows_var,), writes=(rows_rstd,))
        k.do("vector", lambda e: e.scalar_tensor_tensor(
            rows_B.ap[0:4, :], rows_m.ap[0:4, :], -1.0, rows_rstd.ap[0:4, :],
            ALU.mult, ALU.mult), reads=(rows_m, rows_rstd), writes=(rows_B,))
        for a in range(4):
            k.do("tensor", lambda e, a=a: e.matmul(
                psA.ap[:, a * 128:(a + 1) * 128],
                c.sel8_f32.ap[0:4, a * 128:(a + 1) * 128],
                rows_rstd.ap[0:4, :], start=True, stop=True),
                reads=(c.sel8_f32, rows_rstd), writes=(psA,))
            k.do("tensor", lambda e, a=a: e.matmul(
                psA.ap[:, 512 + a * 128:512 + (a + 1) * 128],
                c.sel8_f32.ap[0:4, a * 128:(a + 1) * 128],
                rows_B.ap[0:4, :], start=True, stop=True),
                reads=(c.sel8_f32, rows_B), writes=(psA,))
        for kc in range(8):
            k.do("vector", lambda e, kc=kc: e.tensor_mul(
                xnf.ap, xT.ap[:, kc * 512:(kc + 1) * 512], psA.ap[:, 0:512]),
                reads=(xT, psA), writes=(xnf,))
            k.do("vector", lambda e, kc=kc: e.tensor_add(
                xn.ap[:, kc * 512:(kc + 1) * 512], xnf.ap,
                psA.ap[:, 512:1024]), reads=(xnf, psA), writes=(xn,))
        if own:
            k.do("gpsimd", lambda e, t0=t0: e.dma_start(
                out=scr["sXN"].ap.rearrange("(kc d) t -> d kc t", d=128)[
                    :, :, t0:t0 + 512],
                in_=xn.ap.rearrange("p (kc t) -> p kc t", kc=8)),
                reads=(xn,), writes=(scr["sXN"],), dma=True)
        proj = [("sKT", c.wkT_bf, c.bk_col, gk)]
        if own:
            proj.append(("sQT", c.wqT_bf, c.bq_col, t0))
        for sname, wbf, bcol, dcol in proj:
            for m in range(8):
                slot = psA.ap[:, (m % 2) * 512:(m % 2) * 512 + 512]
                pslot = Buf("slot", None)
                for kc in range(8):
                    k.do("tensor", lambda e, m=m, kc=kc, wbf=wbf, slot=slot:
                         e.matmul(
                             slot,
                             wbf.ap[:, kc * 1024 + m * 128:
                                    kc * 1024 + (m + 1) * 128],
                             xn.ap[:, kc * 512:(kc + 1) * 512],
                             start=(kc == 0), stop=(kc == 7)),
                         reads=(wbf, xn), writes=(psA,))
                k.do("vector", lambda e, m=m, bcol=bcol, slot=slot, tb=tb:
                     e.scalar_tensor_tensor(
                         qk_stage.ap[:, m * 512:(m + 1) * 512], slot,
                         bcol.ap[:, m:m + 1],
                         tb.ap[:, m * 512:(m + 1) * 512],
                         ALU.add, ALU.mult),
                     reads=(psA, bcol, tb), writes=(qk_stage,))
            k.do("gpsimd", lambda e, sname=sname, dcol=dcol: e.dma_start(
                out=scr[sname].ap.rearrange("(kc d) t -> d kc t", d=128)[
                    :, :, dcol:dcol + 512],
                in_=qk_stage.ap.rearrange("p (kc t) -> p kc t", kc=8)),
                reads=(qk_stage,), writes=(scr[sname],), dma=True)
        for hv in range(2):
            for a in range(4):
                for kc in range(8):
                    k.do("tensor", lambda e, hv=hv, a=a, kc=kc: e.matmul(
                        psO.ap[:, a * 512:(a + 1) * 512],
                        xn.ap[:, kc * 512 + a * 128:kc * 512 + (a + 1) * 128],
                        c.wvT_bf.ap[:, kc * 1024 + hv * 512:
                                    kc * 1024 + hv * 512 + 512],
                        start=(kc == 0), stop=False),
                        reads=(xn, c.wvT_bf), writes=(psO,))
                k.do("tensor", lambda e, hv=hv, a=a: e.matmul(
                    psO.ap[:, a * 512:(a + 1) * 512],
                    c.ones_bf.ap[0:1, 0:128],
                    c.bv_row_bf.ap[0:1, hv * 512:hv * 512 + 512],
                    start=False, stop=True),
                    reads=(c.ones_bf, c.bv_row_bf), writes=(psO,))
                k.do("vector", lambda e, hv=hv, a=a: e.tensor_copy(
                    v_stage.ap[:, a * 1024 + hv * 512:a * 1024 + hv * 512
                               + 512],
                    psO.ap[:, a * 512:(a + 1) * 512]),
                    reads=(psO,), writes=(v_stage,))
        k.do("gpsimd", lambda e, gk=gk: e.dma_start(
            out=scr["sV"].ap[gk:gk + 512, :].rearrange(
                "(a p) h -> p a h", p=128),
            in_=v_stage.ap.rearrange("p (a h) -> p a h", a=4)),
            reads=(v_stage,), writes=(scr["sV"],), dma=True)
        if not own:
            continue
        for a in range(4):
            for kc in range(8):
                k.do("tensor", lambda e, a=a, kc=kc: e.matmul(
                    psB.ap[:, 0:8],
                    xn.ap[:, kc * 512 + a * 128:kc * 512 + (a + 1) * 128],
                    c.routerT_bf.ap[:, kc * 8:(kc + 1) * 8],
                    start=(kc == 0), stop=False),
                    reads=(xn, c.routerT_bf), writes=(psB,))
            k.do("tensor", lambda e: e.matmul(
                psB.ap[:, 0:8], c.ones_bf.ap[0:1, 0:128],
                c.rbias_row_bf.ap[0:1, 0:8], start=False, stop=True),
                reads=(c.ones_bf, c.rbias_row_bf), writes=(psB,))
            k.do("vector", lambda e: e.tensor_copy(lg.ap, psB.ap[:, 0:8]),
                 reads=(psB,), writes=(lg,))
            cur = lg
            for r in range(8):
                dst = srt[r % 2]
                if r % 2 == 0:
                    k.do("vector", lambda e, cur=cur, dst=dst: e.tensor_max(
                        dst.ap[:, 0:8:2], cur.ap[:, 0:8:2], cur.ap[:, 1:8:2]),
                        reads=(cur,), writes=(dst,))
                    k.do("vector", lambda e, cur=cur, dst=dst: e.tensor_tensor(
                        dst.ap[:, 1:8:2], cur.ap[:, 0:8:2], cur.ap[:, 1:8:2],
                        ALU.min), reads=(cur,), writes=(dst,))
                else:
                    k.do("vector", lambda e, cur=cur, dst=dst: e.tensor_copy(
                        dst.ap[:, 0:1], cur.ap[:, 0:1]),
                        reads=(cur,), writes=(dst,))
                    k.do("vector", lambda e, cur=cur, dst=dst: e.tensor_copy(
                        dst.ap[:, 7:8], cur.ap[:, 7:8]),
                        reads=(cur,), writes=(dst,))
                    k.do("vector", lambda e, cur=cur, dst=dst: e.tensor_max(
                        dst.ap[:, 1:7:2], cur.ap[:, 1:7:2], cur.ap[:, 2:8:2]),
                        reads=(cur,), writes=(dst,))
                    k.do("vector", lambda e, cur=cur, dst=dst: e.tensor_tensor(
                        dst.ap[:, 2:8:2], cur.ap[:, 1:7:2], cur.ap[:, 2:8:2],
                        ALU.min), reads=(cur,), writes=(dst,))
                cur = dst
            k.do("vector", lambda e, cur=cur: e.tensor_scalar_mul(
                negm.ap, cur.ap[:, 0:1], -1.0), reads=(cur,), writes=(negm,))
            k.do("scalar", lambda e: e.activation(
                eexp.ap, lg.ap, AF.Exp, bias=negm.ap),
                reads=(lg, negm), writes=(eexp,))
            k.do("vector", lambda e, cur=cur: e.tensor_single_scalar(
                maskb.ap, lg.ap, cur.ap[:, 3:4], ALU.is_ge),
                reads=(lg, cur), writes=(maskb,))
            k.do("vector", lambda e: e.tensor_mul(em.ap, maskb.ap, eexp.ap),
                 reads=(maskb, eexp), writes=(em,))
            k.do("vector", lambda e: e.reduce_sum(ssum.ap, em.ap, axis=AX.X),
                 reads=(em,), writes=(ssum,))
            k.do("vector", lambda e: e.reciprocal(rec.ap, ssum.ap),
                 reads=(ssum,), writes=(rec,))
            k.do("vector", lambda e: e.tensor_single_scalar(
                comb.ap, em.ap, rec.ap, ALU.mult),
                reads=(em, rec), writes=(comb,))
            k.do("vector", lambda e: e.tensor_copy(comb_bf.ap, comb.ap),
                 reads=(comb,), writes=(comb_bf,))
            k.do("tensor", lambda e, a=a: e.transpose(
                psT.ap[0:8, a * 128:(a + 1) * 128], comb_bf.ap, c.eye_bf.ap),
                reads=(comb_bf, c.eye_bf), writes=(psT,))
            k.do("vector", lambda e, a=a: e.tensor_copy(
                combT.ap[0:8, a * 128:(a + 1) * 128],
                psT.ap[0:8, a * 128:(a + 1) * 128]),
                reads=(psT,), writes=(combT,))
        k.do("gpsimd", lambda e, t0=t0: e.dma_start(
            out=scr["sCT"].ap[:, t0:t0 + 512], in_=combT.ap[0:8, :]),
            reads=(combT,), writes=(scr["sCT"],), dma=True)
    A.reset()



def emit_p0b(k, A, inp, c, scr):
    A.mark()
    psA, psB, psT, psO = c.psA, c.psB, c.psT, c.psO
    xnb = A.alloc("mx_xnb", 4096, BF16)
    ct = A.alloc("mx_ct", 512, BF16)
    moe_acc = A.alloc("mx_acc", 8192)
    act_sb = [A.alloc(f"mx_act{s}", 4096, BF16) for s in range(2)]
    NGU = 8
    gu_roll = [A.alloc(f"mx_gu{i}", 512, BF16) for i in range(NGU)]
    dwn_roll = [A.alloc(f"mx_dwn{i}", 8192, BF16) for i in range(2)]
    comb_sb = [A.alloc(f"mx_comb{s}", 512) for s in range(2)]
    gmin2 = [A.alloc(f"mx_gmin{i}", 512) for i in range(2)]
    upc_2 = [A.alloc(f"mx_upc{i}", 512) for i in range(2)]
    upc22 = [A.alloc(f"mx_upc2{i}", 512) for i in range(2)]
    gs2 = [A.alloc(f"mx_gs{i}", 512) for i in range(2)]
    tmp2 = [A.alloc(f"mx_tmp{i}", 512) for i in range(2)]

    def gu_dma(e_i, j, slot):
        src = inp["wgu_blk"].ap[e_i, j]
        k.do("gpsimd", lambda e: e.dma_start(out=gu_roll[slot].ap, in_=src),
             reads=(inp["wgu_blk"],), writes=(gu_roll[slot],), dma=True)

    for tile in range(4):
        t0 = tile * 1024
        k.do("sync", lambda e, t0=t0: e.dma_start(
            out=xnb.ap.rearrange("p (kc t) -> p kc t", kc=8),
            in_=scr["sXN"].ap.rearrange("(kc d) t -> d kc t", d=128)[
                :, :, t0:t0 + 1024]),
            reads=(scr["sXN"],), writes=(xnb,), dma=True)
        k.do("sync", lambda e, t0=t0: e.dma_start(
            out=ct.ap[0:8, :], in_=scr["sCT"].ap[:, t0:t0 + 1024]),
            reads=(scr["sCT"],), writes=(ct,), dma=True)
        for dc in range(8):
            for sub in range(2):
                k.do("tensor", lambda e, dc=dc, sub=sub: e.matmul(
                    psB.ap, c.db8_bf.ap[0:8, dc * 128:(dc + 1) * 128],
                    ct.ap[0:8, sub * 512:(sub + 1) * 512],
                    start=True, stop=True),
                    reads=(c.db8_bf, ct), writes=(psB,))
                k.do("vector", lambda e, dc=dc, sub=sub: e.tensor_copy(
                    moe_acc.ap[:, dc * 1024 + sub * 512:
                               dc * 1024 + sub * 512 + 512], psB.ap),
                    reads=(psB,), writes=(moe_acc,))
        for ei in range(EXP):
            dwn = dwn_roll[ei % 2]
            k.do("gpsimd", lambda e, ei=ei, dwn=dwn: e.dma_start(
                out=dwn.ap, in_=inp["dwn_blk"].ap[ei]),
                reads=(inp["dwn_blk"],), writes=(dwn,), dma=True)
            for sub in range(2):
                k.do("tensor", lambda e, ei=ei, sub=sub: e.matmul(
                    psB.ap, c.sel8_bf.ap[0:8, ei * 128:(ei + 1) * 128],
                    ct.ap[0:8, sub * 512:(sub + 1) * 512],
                    start=True, stop=True),
                    reads=(c.sel8_bf, ct), writes=(psB,))
                k.do("vector", lambda e, sub=sub: e.tensor_scalar_mul(
                    comb_sb[sub].ap, psB.ap, 1.0 / 1.702),
                    reads=(psB,), writes=(comb_sb[sub],))
            if ei == 0 and tile == 0:
                for pre in range(NGU // 2):
                    gu_dma(0, pre, 2 * pre)
                    gu_dma(0, 16 + pre, 2 * pre + 1)
            for ic in range(16):
                gslot, uslot = (ic % (NGU // 2)) * 2, (ic % (NGU // 2)) * 2 + 1
                gbuf, ubuf = gu_roll[gslot], gu_roll[uslot]
                gu_ps = [(psA, 0), (psA, 512)] if ic % 2 == 0 else \
                        [(psB, 0), (psO, 0)]
                for sub in range(2):
                    (gps, goff), (ups, uoff) = gu_ps[0], gu_ps[1]
                    if sub == 1:
                        (gps, goff), (ups, uoff) = (psB, 0), (psO, 0)
                    else:
                        (gps, goff), (ups, uoff) = (psA, 0), (psA, 512)
                    for kc in range(8):
                        k.do("tensor", lambda e, kc=kc, sub=sub, gbuf=gbuf,
                             gps=gps, goff=goff: e.matmul(
                                 gps.ap[:, goff:goff + 512],
                                 gbuf.ap[:, kc * 128:(kc + 1) * 128],
                                 xnb.ap[:, kc * 1024 + sub * 512:
                                        kc * 1024 + sub * 512 + 512],
                                 start=(kc == 0), stop=(kc == 7)),
                             reads=(gbuf, xnb), writes=(gps,))
                    for kc in range(8):
                        k.do("tensor", lambda e, kc=kc, sub=sub, ubuf=ubuf,
                             ups=ups, uoff=uoff: e.matmul(
                                 ups.ap[:, uoff:uoff + 512],
                                 ubuf.ap[:, kc * 128:(kc + 1) * 128],
                                 xnb.ap[:, kc * 1024 + sub * 512:
                                        kc * 1024 + sub * 512 + 512],
                                 start=(kc == 0), stop=(kc == 7)),
                             reads=(ubuf, xnb), writes=(ups,))
                    gb = c.gub.ap[:, ei * 32 + ic:ei * 32 + ic + 1]
                    ub = c.gub.ap[:, ei * 32 + 16 + ic:ei * 32 + 16 + ic + 1]
                    gmin, upc = gmin2[ic % 2], upc_2[ic % 2]
                    upc2, gs = upc22[ic % 2], gs2[ic % 2]
                    k.do("vector", lambda e, gps=gps, goff=goff, gb=gb,
                         gmin=gmin: e.tensor_scalar(
                             gmin.ap, gps.ap[:, goff:goff + 512],
                             gb, 7.0, ALU.add, ALU.min),
                         reads=(gps, c.gub), writes=(gmin,))
                    k.do("vector", lambda e, ups=ups, uoff=uoff, ub=ub,
                         upc=upc: e.tensor_scalar(
                             upc.ap, ups.ap[:, uoff:uoff + 512],
                             ub, 7.0, ALU.add, ALU.min),
                         reads=(ups, c.gub), writes=(upc,))
                    k.do("vector", lambda e, upc=upc, upc2=upc2:
                         e.tensor_scalar(
                             upc2.ap, upc.ap, -7.0, 1.0, ALU.max, ALU.add),
                         reads=(upc,), writes=(upc2,))
                    k.do("scalar", lambda e, gmin=gmin, gs=gs: e.activation(
                        gs.ap, gmin.ap, AF.Silu, scale=1.702),
                        reads=(gmin,), writes=(gs,))
                    k.do("vector", lambda e, sub=sub, ic=ic, upc2=upc2,
                         gs=gs: e.tensor_mul(
                             act_sb[sub].ap[:, ic * 512:(ic + 1) * 512],
                             upc2.ap, gs.ap),
                         reads=(upc2, gs), writes=(act_sb[sub],))
                nic = ic + NGU // 2
                ne, nj = (ei, nic) if nic < 16 else (ei + 1, nic - 16)
                if ne < EXP:
                    gu_dma(ne, nj, gslot)
                    gu_dma(ne, 16 + nj, uslot)
                elif tile < 3:
                    gu_dma(0, nj, gslot)
                    gu_dma(0, 16 + nj, uslot)
            for sub in range(2):
                for dg in range(2):
                    for dc4 in range(4):
                        dc = dg * 4 + dc4
                        for ic in range(16):
                            k.do("tensor", lambda e, ic=ic, dc=dc, sub=sub,
                                 dc4=dc4, dwn=dwn: e.matmul(
                                     psO.ap[:, dc4 * 512:(dc4 + 1) * 512],
                                     dwn.ap[:, ic * 1024 + dc * 128:
                                            ic * 1024 + (dc + 1) * 128],
                                     act_sb[sub].ap[:, ic * 512:
                                                    (ic + 1) * 512],
                                     start=(ic == 0), stop=(ic == 15)),
                                 reads=(dwn, act_sb[sub]), writes=(psO,))
                        tmp = tmp2[dc4 % 2]
                        k.do("vector", lambda e, dc=dc, sub=sub, dc4=dc4,
                             tmp=tmp: e.tensor_mul(
                                 tmp.ap,
                                 psO.ap[:, dc4 * 512:(dc4 + 1) * 512],
                                 comb_sb[sub].ap),
                             reads=(psO, comb_sb[sub]), writes=(tmp,))
                        k.do("vector", lambda e, dc=dc, sub=sub, tmp=tmp:
                             e.tensor_add(
                                 moe_acc.ap[:, dc * 1024 + sub * 512:
                                            dc * 1024 + sub * 512 + 512],
                                 moe_acc.ap[:, dc * 1024 + sub * 512:
                                            dc * 1024 + sub * 512 + 512],
                                 tmp.ap),
                             reads=(tmp, moe_acc), writes=(moe_acc,))
        k.do("gpsimd", lambda e, t0=t0: e.dma_start(
            out=scr["sMoE"].ap.rearrange("(dc d) t -> d dc t", d=128)[
                :, :, t0:t0 + 1024],
            in_=moe_acc.ap.rearrange("p (dc t) -> p dc t", dc=8)),
            reads=(moe_acc,), writes=(scr["sMoE"],), dma=True)
    A.reset()



def emit_p1(k, A, inp, c, scr):
    A.mark()
    psA, psB, psT, psO = c.psA, c.psB, c.psT, c.psO
    Ksb = A.alloc("at_K", 16384, BF16)
    Vsb = A.alloc("at_V", 16384, BF16)
    Ssb_r = A.alloc("at_S", 8192)
    Ssb = [Buf(f"at_S{i}", Ssb_r.ap[:, i * 256:(i + 1) * 256])
           for i in range(32)]
    Pbf_r = A.alloc("at_P", 4096, BF16)
    Pbf = [Buf(f"at_P{i}", Pbf_r.ap[:, i * 256:(i + 1) * 256])
           for i in range(32)]
    Qsb = [A.alloc(f"at_Q{i}", 1024, BF16) for i in range(2)]
    Mrun = A.alloc("at_M", 256)
    Mbf = A.alloc("at_Mbf", 128, BF16)
    mcolb = A.alloc("at_mcol", 1, BF16)
    mtsb = A.alloc("at_mt", 128, BF16)
    mrow = A.alloc("at_mrow", 128, BF16)
    Mb = A.alloc("at_Mb", 256)
    ost = A.alloc("at_ost", 2048)
    rowst = A.alloc("at_rows", 512)
    stmp2 = [A.alloc(f"at_stmp{i}", 256) for i in range(2)]

    for h in range(2):
        r0 = h * 4096
        oname, rname = ("sOA", "sRA") if h == 0 else ("sOB", "sRB")
        k.do("sync", lambda e, r0=r0: e.dma_start(
            out=Ksb.ap.rearrange("p (m t) -> p m t", m=8),
            in_=scr["sKT"].ap.rearrange("(m d) t -> d m t", d=128)[
                :, :, r0:r0 + 4096]),
            reads=(scr["sKT"],), writes=(Ksb,), dma=True)
        k.do("sync", lambda e, r0=r0: e.dma_start(
            out=Vsb.ap.rearrange("p (tc hh) -> p tc hh", tc=32),
            in_=scr["sV"].ap[r0:r0 + 4096, :].rearrange(
                "(tc p) hh -> p tc hh", p=128)),
            reads=(scr["sV"],), writes=(Vsb,), dma=True)
        for qb in range(16):
            q0 = qb * 256
            Q = Qsb[qb % 2]
            k.do("sync", lambda e, q0=q0, Q=Q: e.dma_start(
                out=Q.ap.rearrange("p (m t) -> p m t", m=8),
                in_=scr["sQT"].ap.rearrange("(m d) t -> d m t", d=128)[
                    :, :, q0:q0 + 256]),
                reads=(scr["sQT"],), writes=(Q,), dma=True)
            for kt in range(32):
                off = 0 if kt % 2 == 0 else 512
                for m in range(8):
                    k.do("tensor", lambda e, kt=kt, m=m, off=off, Q=Q:
                         e.matmul(
                             psA.ap[:, off:off + 256],
                             Ksb.ap[:, m * 4096 + kt * 128:
                                    m * 4096 + (kt + 1) * 128],
                             Q.ap[:, m * 256:(m + 1) * 256],
                             start=(m == 0), stop=(m == 7),
                             skip_group_check=True),
                         reads=(Ksb, Q), writes=(psA,))
                k.do("vector", lambda e, kt=kt, off=off: e.tensor_copy(
                    Ssb[kt].ap, psA.ap[:, off:off + 256]),
                    reads=(psA,), writes=(Ssb[kt],))
                if kt == 0:
                    k.do("vector", lambda e, off=off: e.tensor_copy(
                        Mrun.ap, psA.ap[:, off:off + 256]),
                        reads=(psA,), writes=(Mrun,))
                else:
                    k.do("vector", lambda e, off=off: e.tensor_max(
                        Mrun.ap, Mrun.ap, psA.ap[:, off:off + 256]),
                        reads=(psA, Mrun), writes=(Mrun,))
            k.do("vector", lambda e: e.tensor_copy(Mbf.ap, Mrun.ap),
                 reads=(Mrun,), writes=(Mbf,))
            for j in range(2):
                k.do("tensor", lambda e, j=j: e.transpose(
                    psT.ap[:, j * 128:(j + 1) * 128],
                    Mbf.ap[:, j * 128:(j + 1) * 128], c.eye_bf.ap),
                    reads=(Mbf, c.eye_bf), writes=(psT,))
                k.do("vector", lambda e, j=j: e.tensor_copy(
                    mtsb.ap[:, j * 128:(j + 1) * 128],
                    psT.ap[:, j * 128:(j + 1) * 128]),
                    reads=(psT,), writes=(mtsb,))
                k.do("vector", lambda e, j=j: e.reduce_max(
                    mcolb.ap[:, j:j + 1], mtsb.ap[:, j * 128:(j + 1) * 128],
                    axis=AX.X), reads=(mtsb,), writes=(mcolb,))
            for j in range(2):
                k.do("tensor", lambda e, j=j: e.transpose(
                    psT.ap[0:1, 256 + j * 128:256 + (j + 1) * 128],
                    mcolb.ap[:, j:j + 1], c.eye_bf.ap),
                    reads=(mcolb, c.eye_bf), writes=(psT,))
                k.do("vector", lambda e, j=j: e.tensor_copy(
                    mrow.ap[0:1, j * 128:(j + 1) * 128],
                    psT.ap[0:1, 256 + j * 128:256 + (j + 1) * 128]),
                    reads=(psT,), writes=(mrow,))
            k.do("tensor", lambda e: e.matmul(
                psB.ap[:, 0:256], c.ones_bf.ap[0:1, :],
                mrow.ap[0:1, 0:256], start=True, stop=True,
                skip_group_check=True),
                reads=(c.ones_bf, mrow), writes=(psB,))
            k.do("vector", lambda e: e.tensor_copy(Mb.ap, psB.ap[:, 0:256]),
                 reads=(psB,), writes=(Mb,))
            for kt in range(32):
                st_ = stmp2[kt % 2]
                k.do("vector", lambda e, kt=kt, st_=st_: e.tensor_sub(
                    st_.ap, Ssb[kt].ap, Mb.ap),
                    reads=(Ssb[kt], Mb), writes=(st_,))
                k.do("scalar", lambda e, kt=kt, st_=st_: e.activation(
                    Pbf[kt].ap, st_.ap, AF.Exp,
                    scale=SCALE), reads=(st_,), writes=(Pbf[kt],))
            for dg in range(2):
                for kt in range(32):
                    for mi in range(4):
                        m = dg * 4 + mi
                        k.do("tensor", lambda e, kt=kt, m=m, mi=mi: e.matmul(
                            psO.ap[:, mi * 512:mi * 512 + 256],
                            Vsb.ap[:, kt * 1024 + m * 128:
                                   kt * 1024 + (m + 1) * 128],
                            Pbf[kt].ap,
                            start=(kt == 0), stop=(kt == 31),
                            skip_group_check=True),
                            reads=(Vsb, Pbf[kt]), writes=(psO,))
                    if dg == 0:
                        k.do("tensor", lambda e, kt=kt: e.matmul(
                            psB.ap[0:1, 256:512], c.ones_bf.ap[:, 0:1],
                            Pbf[kt].ap,
                            start=(kt == 0), stop=(kt == 31),
                            skip_group_check=True),
                            reads=(c.ones_bf, Pbf[kt]), writes=(psB,))
                k.do("vector", lambda e, dg=dg: e.tensor_copy(
                    ost.ap.rearrange("p (m t) -> p m t", m=8)[
                        :, dg * 4:(dg + 1) * 4, :],
                    psO.ap.rearrange("p (b h) -> p b h", b=4)[:, :, 0:256]),
                    reads=(psO,), writes=(ost,))
            k.do("vector", lambda e: e.tensor_copy(
                rowst.ap[0:1, 0:256], mrow.ap[0:1, 0:256]),
                reads=(mrow,), writes=(rowst,))
            k.do("vector", lambda e: e.tensor_copy(
                rowst.ap[0:1, 256:512], psB.ap[0:1, 256:512]),
                reads=(psB,), writes=(rowst,))
            k.do("gpsimd", lambda e, q0=q0, oname=oname: e.dma_start(
                out=scr[oname].ap.rearrange("(m d) t -> d m t", d=128)[
                    :, :, q0:q0 + 256],
                in_=ost.ap.rearrange("p (m t) -> p m t", m=8)),
                reads=(ost,), writes=(scr[oname],), dma=True)
            k.do("gpsimd", lambda e, q0=q0, rname=rname: e.dma_start(
                out=scr[rname].ap[0:1, q0:q0 + 256],
                in_=rowst.ap[0:1, 0:256]),
                reads=(rowst,), writes=(scr[rname],), dma=True)
            k.do("gpsimd", lambda e, q0=q0, rname=rname: e.dma_start(
                out=scr[rname].ap[1:2, q0:q0 + 256],
                in_=rowst.ap[0:1, 256:512]),
                reads=(rowst,), writes=(scr[rname],), dma=True)
    A.reset()



def emit_p1c(k, A, inp, c, scr):
    A.mark()
    psA = c.psA
    xTt = A.alloc("mg_x", 4096)
    moet = A.alloc("mg_moe", 4096)
    oat = A.alloc("mg_oa", 4096)
    obt = A.alloc("mg_ob", 4096)
    x1t = A.alloc("mg_x1", 4096)
    mA = A.alloc("mg_mA", 512)
    sA = A.alloc("mg_sA", 512)
    mB = A.alloc("mg_mB", 512)
    sB = A.alloc("mg_sB", 512)
    mm = A.alloc("mg_m", 512)
    fA = A.alloc("mg_fA", 512)
    fB = A.alloc("mg_fB", 512)
    u1 = A.alloc("mg_u1", 512)
    u2 = A.alloc("mg_u2", 512)
    srow = A.alloc("mg_s", 512)
    rec = A.alloc("mg_rec", 512)
    gA = A.alloc("mg_gA", 512)
    gB = A.alloc("mg_gB", 512)
    ut = A.alloc("mg_ut", 512)
    vt = A.alloc("mg_vt", 512)

    for tile in range(8):
        t0 = tile * 512
        for nm, b in (("sXT", xTt), ("sMoE", moet), ("sOA", oat),
                      ("sOB", obt)):
            k.do("sync", lambda e, nm=nm, b=b, t0=t0: e.dma_start(
                out=b.ap.rearrange("p (m t) -> p m t", m=8),
                in_=scr[nm].ap.rearrange("(m d) t -> d m t", d=128)[
                    :, :, t0:t0 + 512]),
                reads=(scr[nm],), writes=(b,), dma=True)
        for nm, rb, sb_ in (("sRA", mA, sA), ("sRB", mB, sB)):
            k.do("sync", lambda e, nm=nm, rb=rb, t0=t0: e.dma_start(
                out=rb.ap[0:1, :], in_=scr[nm].ap[0:1, t0:t0 + 512]),
                reads=(scr[nm],), writes=(rb,), dma=True)
            k.do("sync", lambda e, nm=nm, sb_=sb_, t0=t0: e.dma_start(
                out=sb_.ap[0:1, :], in_=scr[nm].ap[1:2, t0:t0 + 512]),
                reads=(scr[nm],), writes=(sb_,), dma=True)
        R1 = (slice(0, 1), slice(None))
        k.do("vector", lambda e: e.tensor_max(
            mm.ap[R1], mA.ap[R1], mB.ap[R1]), reads=(mA, mB), writes=(mm,))
        k.do("vector", lambda e: e.tensor_sub(
            fA.ap[R1], mA.ap[R1], mm.ap[R1]), reads=(mA, mm), writes=(fA,))
        k.do("vector", lambda e: e.tensor_sub(
            fB.ap[R1], mB.ap[R1], mm.ap[R1]), reads=(mB, mm), writes=(fB,))
        k.do("scalar", lambda e: e.activation(
            fA.ap[R1], fA.ap[R1], AF.Exp, scale=SCALE),
            reads=(fA,), writes=(fA,))
        k.do("scalar", lambda e: e.activation(
            fB.ap[R1], fB.ap[R1], AF.Exp, scale=SCALE),
            reads=(fB,), writes=(fB,))
        k.do("vector", lambda e: e.tensor_mul(
            u1.ap[R1], sA.ap[R1], fA.ap[R1]), reads=(sA, fA), writes=(u1,))
        k.do("vector", lambda e: e.tensor_mul(
            u2.ap[R1], sB.ap[R1], fB.ap[R1]), reads=(sB, fB), writes=(u2,))
        k.do("vector", lambda e: e.tensor_add(
            srow.ap[R1], u1.ap[R1], u2.ap[R1]), reads=(u1, u2),
            writes=(srow,))
        k.do("vector", lambda e: e.reciprocal(rec.ap[R1], srow.ap[R1]),
             reads=(srow,), writes=(rec,))
        k.do("vector", lambda e: e.tensor_mul(
            gA.ap[R1], fA.ap[R1], rec.ap[R1]), reads=(fA, rec), writes=(gA,))
        k.do("vector", lambda e: e.tensor_mul(
            gB.ap[R1], fB.ap[R1], rec.ap[R1]), reads=(fB, rec), writes=(gB,))
        k.do("tensor", lambda e: e.matmul(
            psA.ap[:, 0:512], c.ones_f32.ap[0:1, :], gA.ap[0:1, :],
            start=True, stop=True, skip_group_check=True),
            reads=(c.ones_f32, gA), writes=(psA,))
        k.do("tensor", lambda e: e.matmul(
            psA.ap[:, 512:1024], c.ones_f32.ap[0:1, :], gB.ap[0:1, :],
            start=True, stop=True, skip_group_check=True),
            reads=(c.ones_f32, gB), writes=(psA,))
        for dc in range(8):
            sl = slice(dc * 512, (dc + 1) * 512)
            k.do("vector", lambda e, sl=sl: e.tensor_mul(
                ut.ap, oat.ap[:, sl], psA.ap[:, 0:512]),
                reads=(oat, psA), writes=(ut,))
            k.do("vector", lambda e, sl=sl: e.tensor_mul(
                vt.ap, obt.ap[:, sl], psA.ap[:, 512:1024]),
                reads=(obt, psA), writes=(vt,))
            k.do("vector", lambda e: e.tensor_add(ut.ap, ut.ap, vt.ap),
                 reads=(ut, vt), writes=(ut,))
            k.do("vector", lambda e, sl=sl: e.tensor_add(
                ut.ap, ut.ap, xTt.ap[:, sl]), reads=(ut, xTt), writes=(ut,))
            k.do("vector", lambda e, sl=sl: e.tensor_add(
                x1t.ap[:, sl], ut.ap, moet.ap[:, sl]),
                reads=(ut, moet), writes=(x1t,))
        k.do("gpsimd", lambda e, t0=t0: e.dma_start(
            out=scr["sX1"].ap.rearrange("(m d) t -> d m t", d=128)[
                :, :, t0:t0 + 512],
            in_=x1t.ap.rearrange("p (m t) -> p m t", m=8)),
            reads=(x1t,), writes=(scr["sX1"],), dma=True)
    A.reset()



def emit_p2a(k, A, inp, c, scr, carry_st):
    A.mark()
    psA, psB, psT, psO = c.psA, c.psB, c.psT, c.psO
    x1b = A.alloc("ga_x1", 8192)
    xn1b = A.alloc("ga_xn1", 4096, BF16)
    xinb = A.alloc("ga_xin", 16384, BF16)
    xstat = [A.alloc(f"ga_xs{i}", 256, BF16) for i in range(2)]
    sqstat = [A.alloc(f"ga_sq{i}", 256, BF16) for i in range(2)]
    srow = A.alloc("ga_srow", 512)
    sqrow = A.alloc("ga_sqrow", 512)
    rstdrow = A.alloc("ga_rstdrow", 512)
    brow = A.alloc("ga_brow", 512)
    lntmp = A.alloc("ga_lntmp", 512)
    NR = 4
    w_roll = [A.alloc(f"ga_w{i}", 512, BF16) for i in range(3)]
    w_roll.append(w_roll[2])
    NWG = 4
    wg_roll = [A.alloc(f"ga_wg{i}", 2048, BF16) for i in range(NWG)]
    xsk_tmp = [A.alloc(f"ga_xsk{i}", 256, BF16) for i in range(2)]
    c_sb2 = [A.alloc(f"ga_c{i}", 512) for i in range(2)]
    z_sb2 = [A.alloc(f"ga_z{i}", 512) for i in range(2)]
    g_sb2 = [A.alloc(f"ga_g{i}", 512) for i in range(2)]
    t1_only = A.alloc("ga_t1", 512)
    t2_only = A.alloc("ga_t2", 512)
    gg_only = A.alloc("ga_gg", 512)
    t1_sb2 = [t1_only, t1_only]
    t2_sb2 = [t2_only, t2_only]
    gg_sb2 = [gg_only, gg_only]
    v_sb2 = [A.alloc(f"ga_v{i}", 512) for i in range(2)]
    h_sb2 = [A.alloc(f"ga_h{i}", 512) for i in range(2)]
    cp_sb2 = [A.alloc(f"ga_cp{i}", 512) for i in range(2)]
    state = A.alloc("ga_state", 32)
    cstate = A.alloc("ga_cstate", 32)
    zt = A.alloc("ga_zt", 512)
    negbg = A.alloc("ga_negbg", 64)
    bgc = A.alloc("ga_bgc", 64)
    b1c = A.alloc("ga_b1c", 32)
    b2c = A.alloc("ga_b2c", 32)

    for nm, b in (("bgc", bgc), ("b1c", b1c), ("b2c", b2c)):
        k.do("sync", lambda e, nm=nm, b=b: e.dma_start(
            out=b.ap, in_=inp[nm].ap), reads=(inp[nm],), writes=(b,),
            dma=True)
    k.do("vector", lambda e: e.tensor_scalar_mul(negbg.ap, bgc.ap, -1.0),
         reads=(bgc,), writes=(negbg,))
    k.do("vector", lambda e: e.memset(state.ap, 0.0), writes=(state,))
    k.do("vector", lambda e: e.memset(cstate.ap, 1.0), writes=(cstate,))
    k.do("vector", lambda e: e.memset(zt.ap, 0.0), writes=(zt,))

    def w_dma(wname, col, slot, nkc, roll):
        src = inp[wname].ap[col]
        k.do("gpsimd", lambda e: e.dma_start(out=roll[slot].ap, in_=src),
             reads=(inp[wname],), writes=(roll[slot],), dma=True)

    for tile in range(4):
        t0 = tile * 1024
        k.do("sync", lambda e, t0=t0: e.dma_start(
            out=x1b.ap.rearrange("p (kc t) -> p kc t", kc=8),
            in_=scr["sX1"].ap.rearrange("(kc d) t -> d kc t", d=128)[
                :, :, t0:t0 + 1024]),
            reads=(scr["sX1"],), writes=(x1b,), dma=True)
        for th in range(2):
            hoff = th * 512
            for kc in range(8):
                xs = xstat[kc % 2]
                sq = sqstat[kc % 2]
                k.do("vector", lambda e, kc=kc, hoff=hoff, xs=xs:
                     e.tensor_copy(
                         xs.ap, x1b.ap[:, kc * 1024 + hoff:
                                       kc * 1024 + hoff + 512]),
                     reads=(x1b,), writes=(xs,))
                k.do("scalar", lambda e, kc=kc, hoff=hoff, sq=sq:
                     e.activation(
                         sq.ap, x1b.ap[:, kc * 1024 + hoff:
                                       kc * 1024 + hoff + 512], AF.Square),
                     reads=(x1b,), writes=(sq,))
                k.do("tensor", lambda e, kc=kc, xs=xs: e.matmul(
                    psB.ap[0:1, 0:512], c.ones_bf.ap[:, 0:1], xs.ap,
                    start=(kc == 0), stop=(kc == 7), skip_group_check=True),
                    reads=(xs, c.ones_bf), writes=(psB,))
                k.do("tensor", lambda e, kc=kc, sq=sq: e.matmul(
                    psA.ap[0:1, 0:512], c.ones_bf.ap[:, 0:1], sq.ap,
                    start=(kc == 0), stop=(kc == 7), skip_group_check=True),
                    reads=(sq, c.ones_bf), writes=(psA,))
            R1 = (slice(0, 1), slice(None))
            k.do("vector", lambda e: e.tensor_copy(
                srow.ap[R1], psB.ap[0:1, 0:512]), reads=(psB,),
                writes=(srow,))
            k.do("vector", lambda e: e.tensor_copy(
                sqrow.ap[R1], psA.ap[0:1, 0:512]), reads=(psA,),
                writes=(sqrow,))
            k.do("scalar", lambda e: e.activation(
                srow.ap[R1], srow.ap[R1], AF.Copy, scale=1.0 / 1024.0),
                reads=(srow,), writes=(srow,))
            k.do("scalar", lambda e: e.activation(
                sqrow.ap[R1], sqrow.ap[R1], AF.Copy, scale=1.0 / 1024.0),
                reads=(sqrow,), writes=(sqrow,))
            k.do("scalar", lambda e: e.activation(
                lntmp.ap[R1], srow.ap[R1], AF.Square),
                reads=(srow,), writes=(lntmp,))
            k.do("vector", lambda e: e.tensor_sub(
                sqrow.ap[R1], sqrow.ap[R1], lntmp.ap[R1]),
                reads=(sqrow, lntmp), writes=(sqrow,))
            k.do("scalar", lambda e: e.activation(
                sqrow.ap[R1], sqrow.ap[R1], AF.Ln,
                bias=c.eps_col.ap[0:1, 0:1]),
                reads=(sqrow, c.eps_col), writes=(sqrow,))
            k.do("scalar", lambda e: e.activation(
                rstdrow.ap[R1], sqrow.ap[R1], AF.Exp, scale=-0.5),
                reads=(sqrow,), writes=(rstdrow,))
            k.do("vector", lambda e: e.scalar_tensor_tensor(
                brow.ap[R1], srow.ap[R1], -1.0, rstdrow.ap[R1],
                ALU.mult, ALU.mult), reads=(srow, rstdrow), writes=(brow,))
            k.do("tensor", lambda e: e.matmul(
                psA.ap[:, 0:512], c.ones_f32.ap[0:1, :], rstdrow.ap[R1],
                start=True, stop=True, skip_group_check=True),
                reads=(c.ones_f32, rstdrow), writes=(psA,))
            k.do("tensor", lambda e: e.matmul(
                psB.ap[:, 0:512], c.ones_f32.ap[0:1, :], brow.ap[R1],
                start=True, stop=True, skip_group_check=True),
                reads=(c.ones_f32, brow), writes=(psB,))
            for kc in range(8):
                k.do("vector", lambda e, kc=kc, hoff=hoff: e.tensor_mul(
                    lntmp.ap, x1b.ap[:, kc * 1024 + hoff:
                                     kc * 1024 + hoff + 512],
                    psA.ap[:, 0:512]), reads=(x1b, psA), writes=(lntmp,))
                k.do("vector", lambda e, kc=kc, hoff=hoff: e.tensor_add(
                    xn1b.ap[:, kc * 1024 + hoff:kc * 1024 + hoff + 512],
                    lntmp.ap, psB.ap[:, 0:512]),
                    reads=(lntmp, psB), writes=(xn1b,))
        k.do("gpsimd", lambda e, t0=t0: e.dma_start(
            out=scr["sXN1"].ap.rearrange("(kc d) t -> d kc t", d=128)[
                :, :, t0:t0 + 1024],
            in_=xn1b.ap.rearrange("p (kc t) -> p kc t", kc=8)),
            reads=(xn1b,), writes=(scr["sXN1"],), dma=True)
        if tile == 0:
            for pre in range(2):
                w_dma("w1_blk", pre, pre, 8, w_roll)
        for ec in range(32):
            wslot = ec % 2
            w2slot = 2
            w_dma("w2_blk", ec, w2slot, 8, w_roll)
            for th in range(2):
                for kc in range(8):
                    k.do("tensor", lambda e, kc=kc, th=th, ec=ec,
                         w=w_roll[wslot]: e.matmul(
                             psO.ap[:, th * 512:th * 512 + 512],
                             w.ap[:, kc * 128:(kc + 1) * 128],
                             xn1b.ap[:, kc * 1024 + th * 512:
                                     kc * 1024 + th * 512 + 512],
                             start=(kc == 0), stop=(kc == 7),
                             skip_group_check=True),
                         reads=(w_roll[wslot], xn1b), writes=(psO,))
                k.do("scalar", lambda e, ec=ec, th=th: e.activation(
                    xinb.ap[:, ec * 1024 + th * 512:ec * 1024 + th * 512
                            + 512],
                    psO.ap[:, th * 512:th * 512 + 512], AF.Silu,
                    bias=b1c.ap[:, ec:ec + 1]),
                    reads=(psO, b1c), writes=(xinb,))
            if ec < 31:
                w_dma("w1_blk", ec + 1, (ec + 1) % 2, 8, w_roll)
            elif tile < 3:
                w_dma("w1_blk", 0, 0, 8, w_roll)
            for th in range(2):
                xsk = xsk_tmp[th]
                for kc in range(8):
                    k.do("tensor", lambda e, kc=kc, th=th, ec=ec,
                         w=w_roll[w2slot]: e.matmul(
                             psO.ap[:, 1024 + th * 512:1024 + th * 512
                                    + 512],
                             w.ap[:, kc * 128:(kc + 1) * 128],
                             xn1b.ap[:, kc * 1024 + th * 512:
                                     kc * 1024 + th * 512 + 512],
                             start=(kc == 0), stop=(kc == 7),
                             skip_group_check=True),
                         reads=(w_roll[w2slot], xn1b), writes=(psO,))
                k.do("scalar", lambda e, ec=ec, th=th, xsk=xsk: e.activation(
                    xsk.ap, psO.ap[:, 1024 + th * 512:1024 + th * 512 + 512],
                    AF.Silu, bias=b2c.ap[:, ec:ec + 1]),
                    reads=(psO, b2c), writes=(xsk,))
                k.do("gpsimd", lambda e, ec=ec, th=th, t0=t0, xsk=xsk:
                     e.dma_start(
                         out=scr["sXSK"].ap[ec * 128:(ec + 1) * 128,
                                            t0 + th * 512:t0 + th * 512
                                            + 512],
                         in_=xsk.ap), reads=(xsk,), writes=(scr["sXSK"],),
                     dma=True)
        if tile == 0:
            w_dma("wg_blk", 0, 0, 32, wg_roll)
            w_dma("wg_blk", 32, 1, 32, wg_roll)
        for sc in range(32):
            gslot, hslot = (sc % (NWG // 2)) * 2, (sc % (NWG // 2)) * 2 + 1
            if sc < 31:
                nxt = sc + 1
                w_dma("wg_blk", nxt, (nxt % (NWG // 2)) * 2, 32, wg_roll)
                w_dma("wg_blk", 32 + nxt, (nxt % (NWG // 2)) * 2 + 1, 32,
                      wg_roll)
            elif tile < 3:
                w_dma("wg_blk", 0, 0, 32, wg_roll)
                w_dma("wg_blk", 32, 1, 32, wg_roll)
            for th in range(2):
                par = (sc * 2 + th) % 2
                gps = psA.ap[:, 0:512] if par == 0 else psO.ap[:, 0:512]
                gbuf = psA if par == 0 else psO
                hps = (psA.ap[:, 512:1024] if par == 0
                       else psO.ap[:, 512:1024])
                c_sb, z_sb, g_sb = c_sb2[par], z_sb2[par], g_sb2[par]
                t1_sb, t2_sb, gg_sb = t1_sb2[par], t2_sb2[par], gg_sb2[par]
                v_sb, h_sb, cp_sb = v_sb2[par], h_sb2[par], cp_sb2[par]
                for ec in range(32):
                    k.do("tensor", lambda e, ec=ec, th=th, gps=gps,
                         w=wg_roll[gslot]: e.matmul(
                             gps,
                             w.ap[:, ec * 128:(ec + 1) * 128],
                             xinb.ap[:, ec * 1024 + th * 512:
                                     ec * 1024 + th * 512 + 512],
                             start=(ec == 0), stop=(ec == 31),
                             skip_group_check=True),
                         reads=(wg_roll[gslot], xinb), writes=(gbuf,))
                for ec in range(32):
                    k.do("tensor", lambda e, ec=ec, th=th, hps=hps,
                         w=wg_roll[hslot]: e.matmul(
                             hps,
                             w.ap[:, ec * 128:(ec + 1) * 128],
                             xinb.ap[:, ec * 1024 + th * 512:
                                     ec * 1024 + th * 512 + 512],
                             start=(ec == 0), stop=(ec == 31),
                             skip_group_check=True),
                         reads=(wg_roll[hslot], xinb), writes=(gbuf,))
                k.do("scalar", lambda e, sc=sc, gps=gps, c_sb=c_sb:
                     e.activation(
                         c_sb.ap, gps, AF.Sigmoid, scale=-1.0,
                         bias=negbg.ap[:, sc:sc + 1]),
                     reads=(gbuf, negbg), writes=(c_sb,))
                k.do("scalar", lambda e, sc=sc, gps=gps, z_sb=z_sb:
                     e.activation(
                         z_sb.ap, gps, AF.Sigmoid,
                         bias=bgc.ap[:, sc:sc + 1]),
                     reads=(gbuf, bgc), writes=(z_sb,))
                k.do("scalar", lambda e, sc=sc, hps=hps, g_sb=g_sb:
                     e.activation(
                         g_sb.ap, hps, AF.Relu,
                         bias=bgc.ap[:, 32 + sc:33 + sc]),
                     reads=(gbuf, bgc), writes=(g_sb,))
                k.do("scalar", lambda e, sc=sc, hps=hps, t1_sb=t1_sb:
                     e.activation(
                         t1_sb.ap, hps, AF.Relu, scale=-1.0,
                         bias=negbg.ap[:, 32 + sc:33 + sc]),
                     reads=(gbuf, negbg), writes=(t1_sb,))
                k.do("scalar", lambda e, t1_sb=t1_sb, t2_sb=t2_sb:
                     e.activation(
                         t2_sb.ap, t1_sb.ap, AF.Sigmoid, scale=-1.0),
                     reads=(t1_sb,), writes=(t2_sb,))
                k.do("vector", lambda e, g_sb=g_sb, t2_sb=t2_sb,
                     gg_sb=gg_sb: e.tensor_add(
                         gg_sb.ap, g_sb.ap, t2_sb.ap),
                     reads=(g_sb, t2_sb), writes=(gg_sb,))
                k.do("vector", lambda e, z_sb=z_sb, gg_sb=gg_sb, v_sb=v_sb:
                     e.tensor_mul(
                         v_sb.ap, z_sb.ap, gg_sb.ap),
                     reads=(z_sb, gg_sb), writes=(v_sb,))
                k.do("vector", lambda e, sc=sc, c_sb=c_sb, v_sb=v_sb,
                     h_sb=h_sb: e.tensor_tensor_scan(
                         h_sb.ap, c_sb.ap, v_sb.ap, state.ap[:, sc:sc + 1],
                         ALU.mult, ALU.add),
                     reads=(c_sb, v_sb, state), writes=(h_sb,))
                k.do("vector", lambda e, sc=sc, h_sb=h_sb: e.tensor_copy(
                    state.ap[:, sc:sc + 1], h_sb.ap[:, 511:512]),
                    reads=(h_sb,), writes=(state,))
                k.do("vector", lambda e, sc=sc, c_sb=c_sb, cp_sb=cp_sb:
                     e.tensor_tensor_scan(
                         cp_sb.ap, c_sb.ap, zt.ap,
                         cstate.ap[:, sc:sc + 1], ALU.mult, ALU.add),
                     reads=(c_sb, zt, cstate), writes=(cp_sb,))
                k.do("vector", lambda e, sc=sc, cp_sb=cp_sb: e.tensor_copy(
                    cstate.ap[:, sc:sc + 1], cp_sb.ap[:, 511:512]),
                    reads=(cp_sb,), writes=(cstate,))
                k.do("gpsimd", lambda e, sc=sc, th=th, t0=t0, h_sb=h_sb:
                     e.dma_start(
                         out=scr["sVloc"].ap[sc * 128:(sc + 1) * 128,
                                             t0 + th * 512:
                                             t0 + th * 512 + 512],
                         in_=h_sb.ap), reads=(h_sb,),
                     writes=(scr["sVloc"],), dma=True)
                k.do("gpsimd", lambda e, sc=sc, th=th, t0=t0, cp_sb=cp_sb:
                     e.dma_start(
                         out=scr["sCprod"].ap[sc * 128:(sc + 1) * 128,
                                              t0 + th * 512:
                                              t0 + th * 512 + 512],
                         in_=cp_sb.ap), reads=(cp_sb,),
                     writes=(scr["sCprod"],), dma=True)
    k.do("vector", lambda e: e.tensor_copy(carry_st.ap[:, 0:32], state.ap),
         reads=(state,), writes=(carry_st,))
    k.do("vector", lambda e: e.tensor_copy(
        carry_st.ap[:, 32:40], xn1b.ap[:, 1023:8192:1024]),
        reads=(xn1b,), writes=(carry_st,))
    if "carry_out" in scr:
        k.do("gpsimd", lambda e: e.dma_start(
            out=scr["carry_out"].ap, in_=carry_st.ap),
            reads=(carry_st,), writes=(scr["carry_out"],), dma=True)
    A.reset()
    return carry_st



def emit_p2b(k, A, inp, c, scr, carry_sb):
    A.mark()
    psA, psB, psO = c.psA, c.psB, c.psO
    sum_sb = A.alloc("gb_sum", 16384, BF16)
    vl = [A.alloc(f"gb_vl{i}", 512, BF16) for i in range(2)]
    cp = [A.alloc(f"gb_cp{i}", 512, BF16) for i in range(2)]
    xs = [A.alloc(f"gb_xs{i}", 512, BF16) for i in range(2)]
    hf = A.alloc("gb_hf", 1024)
    NW = 2
    wo_roll = [A.alloc(f"gb_wo{i}", 2048, BF16) for i in range(NW)]
    mix_st = A.alloc("gb_mix", 8192)
    mix2_st = A.alloc("gb_mix2", 4096)
    boc = A.alloc("gb_boc", 8)
    k.do("sync", lambda e: e.dma_start(out=boc.ap, in_=inp["boc"].ap),
         reads=(inp["boc"],), writes=(boc,), dma=True)

    def wo_dma(dc, slot):
        src = inp["wo_blk"].ap[dc]
        q = "sync" if os.environ.get("DEVBF16_WO") else "gpsimd"
        k.do(q, lambda e: e.dma_start(out=wo_roll[slot].ap, in_=src),
             reads=(inp["wo_blk"],), writes=(wo_roll[slot],), dma=True)

    for tile in range(4):
        t0 = tile * 1024
        for sc in range(32):
            sl = sc % 2
            k.do("sync", lambda e, sc=sc, t0=t0, b=vl[sl]: e.dma_start(
                out=b.ap, in_=scr["sVloc"].ap[sc * 128:(sc + 1) * 128,
                                              t0:t0 + 1024]),
                reads=(scr["sVloc"],), writes=(vl[sl],), dma=True)
            k.do("sync", lambda e, sc=sc, t0=t0, b=cp[sl]: e.dma_start(
                out=b.ap, in_=scr["sCprod"].ap[sc * 128:(sc + 1) * 128,
                                               t0:t0 + 1024]),
                reads=(scr["sCprod"],), writes=(cp[sl],), dma=True)
            k.do("sync", lambda e, sc=sc, t0=t0, b=xs[sl]: e.dma_start(
                out=b.ap, in_=scr["sXSK"].ap[sc * 128:(sc + 1) * 128,
                                             t0:t0 + 1024]),
                reads=(scr["sXSK"],), writes=(xs[sl],), dma=True)
            k.do("vector", lambda e, sc=sc, sl=sl: e.scalar_tensor_tensor(
                hf.ap, cp[sl].ap, carry_sb.ap[:, sc:sc + 1], vl[sl].ap,
                ALU.mult, ALU.add),
                reads=(cp[sl], vl[sl], carry_sb), writes=(hf,))
            k.do("vector", lambda e, sc=sc, sl=sl: e.tensor_add(
                sum_sb.ap[:, sc * 1024:(sc + 1) * 1024], hf.ap, xs[sl].ap),
                reads=(hf, xs[sl]), writes=(sum_sb,))
        k.flush("vector", sum_sb, hf.ap[:, 0:1])
        if "dbg_sum" in scr:
            k.do("gpsimd", lambda e, t0=t0: e.dma_start(
                out=scr["dbg_sum"].ap[:, t0:t0 + 1024].rearrange(
                    "(ec d) t -> d ec t", d=128),
                in_=sum_sb.ap.rearrange("p (ec t) -> p ec t", ec=32)),
                reads=(sum_sb,), writes=(scr["dbg_sum"],), dma=True)
        if tile == 0:
            wo_dma(0, 0)
        for dc in range(8):
            w = wo_roll[dc % NW]
            if dc < 7:
                wo_dma(dc + 1, (dc + 1) % NW)
            elif tile < 3:
                wo_dma(0, 0)
            for th in range(2):
                pso_off = (th % 2) * 512
                for ec in range(32):
                    k.do("tensor", lambda e, ec=ec, th=th, w=w,
                         pso_off=pso_off: e.matmul(
                             psO.ap[:, pso_off:pso_off + 512],
                             w.ap[:, ec * 128:(ec + 1) * 128],
                             sum_sb.ap[:, ec * 1024 + th * 512:
                                       ec * 1024 + th * 512 + 512],
                             start=(ec == 0), stop=(ec == 31),
                             skip_group_check=True),
                         reads=(w, sum_sb), writes=(psO,))
                k.do("vector", lambda e, dc=dc, th=th, pso_off=pso_off:
                     e.tensor_single_scalar(
                         mix_st.ap[:, dc * 1024 + th * 512:
                                   dc * 1024 + th * 512 + 512],
                         psO.ap[:, pso_off:pso_off + 512],
                         boc.ap[:, dc:dc + 1], ALU.add),
                     reads=(psO, boc), writes=(mix_st,))
                if os.environ.get("DEVSER_P2B"):
                    k.barrier()
                if "dbg_mix2" in scr and th == 0:
                    k.do("vector", lambda e, dc=dc, pso_off=pso_off:
                         e.tensor_single_scalar(
                             mix2_st.ap[:, dc * 512:(dc + 1) * 512],
                             psO.ap[:, pso_off:pso_off + 512],
                             boc.ap[:, dc:dc + 1], ALU.add),
                         reads=(psO, boc), writes=(mix2_st,))
        k.do("gpsimd", lambda e, t0=t0: e.dma_start(
            out=scr["sMix"].ap.rearrange("(dc d) t -> d dc t", d=128)[
                :, :, t0:t0 + 1024],
            in_=mix_st.ap.rearrange("p (dc t) -> p dc t", dc=8)),
            reads=(mix_st,), writes=(scr["sMix"],), dma=True)
        if "dbg_mix2" in scr:
            k.do("gpsimd", lambda e, t0=t0: e.dma_start(
                out=scr["dbg_mix2"].ap.rearrange("(dc d) t -> d dc t",
                                                 d=128)[:, :,
                                                        t0 // 2:t0 // 2 + 512],
                in_=mix2_st.ap.rearrange("p (dc t) -> p dc t", dc=8)),
                reads=(mix2_st,), writes=(scr["dbg_mix2"],), dma=True)
    A.reset()



def emit_p3(k, A, inp, c, scr, carry_sb):
    A.mark()
    psA, psB, psO = c.psA, c.psB, c.psO
    wr_bf = A.alloc("ff_wr", 4096, BF16)
    wk_bf = A.alloc("ff_wk", 4096, BF16)
    wv_bf = A.alloc("ff_wv", 4096, BF16)
    for nm, b in (("wrT", wr_bf), ("wkkT", wk_bf), ("wvvT", wv_bf)):
        src = inp[nm].ap.rearrange("(kc p) h -> p kc h", p=128)
        dst = b.ap.rearrange("p (kc h) -> p kc h", kc=8)
        k.do("gpsimd", lambda e, src=src, dst=dst: e.dma_start(
            out=dst, in_=src), reads=(inp[nm],), writes=(b,), dma=True)
    mur = A.alloc("ff_mur", 8)
    muk = A.alloc("ff_muk", 8)
    omr = A.alloc("ff_omr", 8)
    omk = A.alloc("ff_omk", 8)
    for nm, b in (("murc", mur), ("mukc", muk)):
        k.do("sync", lambda e, nm=nm, b=b: e.dma_start(
            out=b.ap, in_=inp[nm].ap), reads=(inp[nm],), writes=(b,),
            dma=True)
    k.do("vector", lambda e: e.tensor_scalar(
        omr.ap, mur.ap, -1.0, 1.0, ALU.mult, ALU.add),
        reads=(mur,), writes=(omr,))
    k.do("vector", lambda e: e.tensor_scalar(
        omk.ap, muk.ap, -1.0, 1.0, ALU.mult, ALU.add),
        reads=(muk,), writes=(omk,))
    xn1s = A.alloc("ff_xn1", 2056, BF16)
    CW = 514
    x1t = A.alloc("ff_x1", 4096)
    mixt = A.alloc("ff_mix", 4096)
    xr_sb = A.alloc("ff_xr", 2048, BF16)
    xk_sb = A.alloc("ff_xk", 2048, BF16)
    r_sb = A.alloc("ff_r", 4096)
    kact = A.alloc("ff_kact", 2048, BF16)
    ktmp = A.alloc("ff_ktmp", 512)
    t1 = A.alloc("ff_t1", 512)
    ffnc = A.alloc("ff_ffnc", 512)
    ut = A.alloc("ff_ut", 512)
    outt = A.alloc("ff_out", 4096)
    out_deps = []

    for tile in range(8):
        t0 = tile * 512
        if tile == 0:
            k.do("sync", lambda e: e.dma_start(
                out=xn1s.ap.rearrange("p (kc t) -> p kc t", kc=8)[
                    :, :, 1:513],
                in_=scr["sXN1"].ap.rearrange("(kc d) t -> d kc t", d=128)[
                    :, :, 0:512]),
                reads=(scr["sXN1"],), writes=(xn1s,), dma=True)
            k.do("vector", lambda e: e.tensor_copy(
                xn1s.ap[:, 0:8 * CW:CW], carry_sb.ap[:, 32:40]),
                reads=(carry_sb,), writes=(xn1s,))
        else:
            k.do("sync", lambda e, t0=t0: e.dma_start(
                out=xn1s.ap.rearrange("p (kc t) -> p kc t", kc=8)[
                    :, :, 0:513],
                in_=scr["sXN1"].ap.rearrange("(kc d) t -> d kc t", d=128)[
                    :, :, t0 - 1:t0 + 512]),
                reads=(scr["sXN1"],), writes=(xn1s,), dma=True)
        k.do("sync", lambda e, t0=t0: e.dma_start(
            out=x1t.ap.rearrange("p (m t) -> p m t", m=8),
            in_=scr["sX1"].ap.rearrange("(m d) t -> d m t", d=128)[
                :, :, t0:t0 + 512]),
            reads=(scr["sX1"],), writes=(x1t,), dma=True)
        k.do("sync", lambda e, t0=t0: e.dma_start(
            out=mixt.ap.rearrange("p (m t) -> p m t", m=8),
            in_=scr["sMix"].ap.rearrange("(m d) t -> d m t", d=128)[
                :, :, t0:t0 + 512]),
            reads=(scr["sMix"],), writes=(mixt,), dma=True)
        for kc in range(8):
            prevv = xn1s.ap[:, kc * CW:kc * CW + 512]
            curr = xn1s.ap[:, kc * CW + 1:kc * CW + 513]
            for mu, om, dst in ((mur, omr, xr_sb), (muk, omk, xk_sb)):
                k.do("vector", lambda e, kc=kc, om=om, prevv=prevv:
                     e.tensor_single_scalar(
                         t1.ap, prevv, om.ap[:, kc:kc + 1], ALU.mult),
                     reads=(xn1s, om), writes=(t1,))
                k.do("vector", lambda e, kc=kc, mu=mu, curr=curr, dst=dst:
                     e.scalar_tensor_tensor(
                         dst.ap[:, kc * 512:(kc + 1) * 512], curr,
                         mu.ap[:, kc:kc + 1], t1.ap, ALU.mult, ALU.add),
                     reads=(xn1s, mu, t1), writes=(dst,))
        for dc in range(8):
            off = 0 if dc % 2 == 0 else 512
            for kc in range(8):
                k.do("tensor", lambda e, kc=kc, dc=dc, off=off: e.matmul(
                    psA.ap[:, off:off + 512],
                    wr_bf.ap[:, kc * 1024 + dc * 128:
                             kc * 1024 + (dc + 1) * 128],
                    xr_sb.ap[:, kc * 512:(kc + 1) * 512],
                    start=(kc == 0), stop=(kc == 7), skip_group_check=True),
                    reads=(wr_bf, xr_sb), writes=(psA,))
            k.do("scalar", lambda e, dc=dc, off=off: e.activation(
                r_sb.ap[:, dc * 512:(dc + 1) * 512], psA.ap[:, off:off + 512],
                AF.Sigmoid), reads=(psA,), writes=(r_sb,))
            for kc in range(8):
                k.do("tensor", lambda e, kc=kc, dc=dc: e.matmul(
                    psB.ap[:, 0:512],
                    wk_bf.ap[:, kc * 1024 + dc * 128:
                             kc * 1024 + (dc + 1) * 128],
                    xk_sb.ap[:, kc * 512:(kc + 1) * 512],
                    start=(kc == 0), stop=(kc == 7), skip_group_check=True),
                    reads=(wk_bf, xk_sb), writes=(psB,))
            k.do("scalar", lambda e: e.activation(
                ktmp.ap, psB.ap[:, 0:512], AF.Relu),
                reads=(psB,), writes=(ktmp,))
            k.do("scalar", lambda e, dc=dc: e.activation(
                kact.ap[:, dc * 512:(dc + 1) * 512], ktmp.ap, AF.Square),
                reads=(ktmp,), writes=(kact,))
        for dc in range(8):
            slot = dc % 4
            for kc in range(8):
                k.do("tensor", lambda e, kc=kc, dc=dc, slot=slot: e.matmul(
                    psO.ap[:, slot * 512:(slot + 1) * 512],
                    wv_bf.ap[:, kc * 1024 + dc * 128:
                             kc * 1024 + (dc + 1) * 128],
                    kact.ap[:, kc * 512:(kc + 1) * 512],
                    start=(kc == 0), stop=(kc == 7), skip_group_check=True),
                    reads=(wv_bf, kact), writes=(psO,))
            k.do("vector", lambda e, dc=dc, slot=slot: e.tensor_mul(
                ffnc.ap, psO.ap[:, slot * 512:(slot + 1) * 512],
                r_sb.ap[:, dc * 512:(dc + 1) * 512]),
                reads=(psO, r_sb), writes=(ffnc,))
            k.do("vector", lambda e, dc=dc: e.tensor_add(
                ut.ap, x1t.ap[:, dc * 512:(dc + 1) * 512],
                mixt.ap[:, dc * 512:(dc + 1) * 512]),
                reads=(x1t, mixt), writes=(ut,))
            k.do("vector", lambda e, dc=dc: e.tensor_add(
                outt.ap[:, dc * 512:(dc + 1) * 512], ut.ap, ffnc.ap),
                reads=(ut, ffnc), writes=(outt,))
        dep = k.do("gpsimd", lambda e, t0=t0: e.dma_start(
            out=scr["outT"].ap.rearrange("(m d) t -> d m t", d=128)[
                :, :, t0:t0 + 512],
            in_=outt.ap.rearrange("p (m t) -> p m t", m=8)),
            reads=(outt,), writes=(scr["outT"],), dma=True)
        out_deps.append(dep)
    A.reset()
    return out_deps



def emit_exchange(k, A, c, scr, carry_st, carry_sb):
    zt40 = A.alloc("xc_z40", 40, F32)
    k.do("vector", lambda e: e.memset(zt40.ap, 0.0), writes=(zt40,))
    for blk_i in range(3):
        k.do("gpsimd", lambda e, blk_i=blk_i: e.dma_start(
            out=scr["cc_in"].ap[blk_i * 128:(blk_i + 1) * 128, :],
            in_=zt40.ap), reads=(zt40,), writes=(scr["cc_in"],), dma=True)

    def own_rows(e):
        par = e.compute_val(e.partition_id() % 2)
        return scr["cc_in"].ap[bass.ds(par * 128 + 128, 128), :]
    k.do("gpsimd", lambda e: e.dma_start(out=own_rows(e), in_=carry_st.ap),
         reads=(carry_st,), writes=(scr["cc_in"],), dma=True)
    k.do("gpsimd", lambda e: e.collective_compute(
        "AllReduce", ALU.add,
        replica_groups=[[0, 1], [2, 3], [4, 5], [6, 7]],
        ins=[scr["cc_in"].ap], outs=[scr["cc_out"].ap]),
        reads=(scr["cc_in"],), writes=(scr["cc_out"],), dma=True)

    def my_rows(e):
        par = e.compute_val(e.partition_id() % 2)
        return scr["cc_out"].ap[bass.ds(par * 128, 128), :]
    k.do("sync", lambda e: e.dma_start(out=carry_sb.ap, in_=my_rows(e)),
         reads=(scr["cc_out"],), writes=(carry_sb,), dma=True)



ARENA_W = 52900

SCRATCH_SPECS = {
    "sQT": ([D, HALF], BF16), "sKT": ([D, S], BF16), "sV": ([S, D], BF16),
    "sXT": ([D, HALF], F32), "sXN": ([D, HALF], BF16),
    "sCT": ([8, HALF], BF16), "sMoE": ([D, HALF], F32),
    "sOA": ([D, HALF], F32), "sOB": ([D, HALF], F32),
    "sRA": ([2, HALF], F32), "sRB": ([2, HALF], F32),
    "sX1": ([D, HALF], F32), "sXN1": ([D, HALF], BF16),
    "sXSK": ([E1, HALF], BF16), "sVloc": ([E1, HALF], BF16),
    "sCprod": ([E1, HALF], BF16), "carry_out": ([128, 40], F32),
    "carry_in": ([128, 40], F32), "sMix": ([D, HALF], F32),
    "cc_in": ([384, 40], F32), "cc_out": ([384, 40], F32),
    "dbg_sum": ([E1, HALF], BF16), "dbg_mix2": ([D, HALF // 2], F32),
    "outT": ([D, HALF], F32),
}

LAUNCH_DEFS = {
    4: dict(inputs=L1_INPUTS | L2_INPUTS,
            scr_in=set(),
            scr_out={"sXN1", "sXSK", "sVloc", "sCprod", "carry_out", "sX1"},
            scr_int={"sQT", "sKT", "sV", "sXT", "sXN", "sCT", "sMoE",
                     "sOA", "sOB", "sRA", "sRB"}),
    0: dict(inputs=L1_INPUTS | L2_INPUTS | L3_INPUTS,
            scr_in=set(),
            scr_out={"outT"},
            scr_int={"sQT", "sKT", "sV", "sXT", "sXN", "sCT", "sMoE",
                     "sOA", "sOB", "sRA", "sRB", "sX1", "sXN1", "sXSK",
                     "sVloc", "sCprod", "sMix", "cc_in", "cc_out"}),
    1: dict(inputs=L1_INPUTS,
            scr_in=set(),
            scr_out={"sX1"},
            scr_int={"sQT", "sKT", "sV", "sXT", "sXN", "sCT", "sMoE",
                     "sOA", "sOB", "sRA", "sRB"}),
    2: dict(inputs=L2_INPUTS,
            scr_in={"sX1"},
            scr_out={"sXN1", "sXSK", "sVloc", "sCprod", "carry_out"},
            scr_int=set()),
    3: dict(inputs=L3_INPUTS,
            scr_in={"sX1", "sXN1", "sXSK", "sVloc", "sCprod", "carry_in"},
            scr_out={"outT"},
            scr_int={"sMix"}),
}


def build_launch(which, dbg=(), phases=None):
    import contextlib
    nc = bass.Bass("TRN2", target_bir_lowering=False, debug=False)
    stack = contextlib.ExitStack()
    k = K(nc, stack)
    A = Arena(k, ARENA_W)
    ldef = LAUNCH_DEFS[which]
    inp = {}
    for name, shape, dt in INPUT_SPECS:
        if name in ldef["inputs"]:
            inp[name] = k.dram(name, shape, dt, "ExternalInput")
    scr = {}
    for name in (ldef["scr_in"] | ldef["scr_out"] | ldef["scr_int"]
                 | set(dbg)):
        shape, dt = SCRATCH_SPECS[name]
        if name in ldef["scr_in"]:
            kind = "ExternalInput"
        elif name in ldef["scr_out"] or name in dbg:
            kind = "ExternalOutput"
        else:
            kind = "Internal"
        scr[name] = k.dram(name, shape, dt, kind)
    c = emit_common(k, A, inp, want_sel8=(which in (0, 1, 4)))
    if which == 0:
        build_l0_consts(k, A, inp, c)
        emit_p0a(k, A, inp, c, scr)
        A.reset()
        emit_p0b(k, A, inp, c, scr)
        A.reset()
        emit_p1(k, A, inp, c, scr)
        emit_p1c(k, A, inp, c, scr)
        carry_st = A.alloc("carry_st", 40, F32)
        carry_sb = A.alloc("carry_sb", 40, F32)
        emit_p2a(k, A, inp, c, scr, carry_st)
        emit_exchange(k, A, c, scr, carry_st, carry_sb)
        emit_p2b(k, A, inp, c, scr, carry_sb)
        emit_p3(k, A, inp, c, scr, carry_sb)
    elif which == 4:
        build_l0_consts(k, A, inp, c)
        emit_p0a(k, A, inp, c, scr)
        A.reset()
        emit_p0b(k, A, inp, c, scr)
        A.reset()
        emit_p1(k, A, inp, c, scr)
        emit_p1c(k, A, inp, c, scr)
        carry_st = A.alloc("carry_st", 40, F32)
        emit_p2a(k, A, inp, c, scr, carry_st)
    elif which == 1:
        ph = phases or ("p0a", "p0b", "p1", "p1c")
        build_l0_consts(k, A, inp, c)
        if "p0a" in ph:
            emit_p0a(k, A, inp, c, scr)
        A.reset()
        if "p0b" in ph:
            emit_p0b(k, A, inp, c, scr)
        A.reset()
        if "p1" in ph:
            emit_p1(k, A, inp, c, scr)
        if "p1c" in ph:
            emit_p1c(k, A, inp, c, scr)
    elif which == 2:
        carry_st = A.alloc("carry_st", 40, F32)
        emit_p2a(k, A, inp, c, scr, carry_st)
    else:
        carry_sb = A.alloc("carry_sb", 40, F32)
        k.do("sync", lambda e: e.dma_start(
            out=carry_sb.ap, in_=scr["carry_in"].ap),
            reads=(scr["carry_in"],), writes=(carry_sb,), dma=True)
        ph = phases or ("p2b", "p3")
        if "p2b" in ph:
            emit_p2b(k, A, inp, c, scr, carry_sb)
        if "p3" in ph:
            emit_p3(k, A, inp, c, scr, carry_sb)
    k.barrier()
    k.finish([(s, v) for s, v in k.cnt.items() if v > 0])
    k.emit_block()
    stack.close()
    return nc


MODE = os.environ.get("KERNEL_MODE", "2L")


def kernel(**inputs):
    maps = prep_inputs(inputs)
    cores = list(range(NCORE))
    if MODE == "2L":
        nc1 = build_launch(4)
        names = sorted(LAUNCH_DEFS[4]["inputs"])
        m1 = [{n: maps[c][n] for n in names} for c in cores]
        r1 = run_bass_kernel_spmd(nc1, m1, core_ids=cores)
        nc2 = build_launch(3)
        zc = np.zeros((128, 40), np.float32)
        m3 = []
        for c in cores:
            carry = (np.asarray(r1.results[c - 1]["carry_out"])
                     if c % 2 == 1 else zc)
            m3.append(dict({n: maps[c][n] for n in sorted(L3_INPUTS)},
                           sX1=r1.results[c]["sX1"],
                           sXN1=r1.results[c]["sXN1"],
                           sXSK=r1.results[c]["sXSK"],
                           sVloc=r1.results[c]["sVloc"],
                           sCprod=r1.results[c]["sCprod"],
                           carry_in=carry))
        r3 = run_bass_kernel_spmd(nc2, m3, core_ids=cores)
        out = np.empty((B, S, D), np.float32)
        for c in cores:
            b, r = c // 2, c % 2
            out[b, r * HALF:(r + 1) * HALF] = \
                np.asarray(r3.results[c]["outT"]).T
        return out
    if MODE == "fused":
        nc = build_launch(0)
        names = sorted(LAUNCH_DEFS[0]["inputs"])
        m = [{n: maps[c][n] for n in names} for c in cores]
        res = run_bass_kernel_spmd(nc, m, core_ids=cores)
        out = np.empty((B, S, D), np.float32)
        for c in cores:
            b, r = c // 2, c % 2
            out[b, r * HALF:(r + 1) * HALF] = \
                np.asarray(res.results[c]["outT"]).T
        return out
    nc1 = build_launch(1)
    m1 = [{n: maps[c][n] for n in sorted(L1_INPUTS)} for c in cores]
    r1 = run_bass_kernel_spmd(nc1, m1, core_ids=cores)
    nc2 = build_launch(2)
    m2 = [dict({n: maps[c][n] for n in sorted(L2_INPUTS)},
               sX1=r1.results[c]["sX1"]) for c in cores]
    r2 = run_bass_kernel_spmd(nc2, m2, core_ids=cores)
    nc3 = build_launch(3)
    zc = np.zeros((128, 40), np.float32)
    m3 = []
    for c in cores:
        carry = r2.results[c - 1]["carry_out"] if c % 2 == 1 else zc
        m3.append(dict({n: maps[c][n] for n in sorted(L3_INPUTS)},
                       sX1=r1.results[c]["sX1"],
                       sXN1=r2.results[c]["sXN1"],
                       sXSK=r2.results[c]["sXSK"],
                       sVloc=r2.results[c]["sVloc"],
                       sCprod=r2.results[c]["sCprod"],
                       carry_in=carry))
    r3 = run_bass_kernel_spmd(nc3, m3, core_ids=cores)
    out = np.empty((B, S, D), np.float32)
    for c in cores:
        b, r = c // 2, c % 2
        out[b, r * HALF:(r + 1) * HALF] = \
            np.asarray(r3.results[c]["outT"]).T
    return out
